# version 02
# baseline (speedup 1.0000x reference)
import numpy as np
import concourse.bass as bass
import concourse.mybir as mybir
from concourse.bass_utils import run_bass_kernel_spmd
from contextlib import ExitStack

F32 = mybir.dt.float32
BF16 = mybir.dt.bfloat16
ALU = mybir.AluOpType
ACTF = mybir.ActivationFunctionType
AX = mybir.AxisListType

D = 2048
DC = 16
NT = 2048
TILE = 1024
H = 32
HD = 64
CW = 1056
NEG = -1.0e30
EPS = 1e-5
SCALE = HD ** -0.5
WIN = 8

ENGINES = ("sync", "tensor", "vector", "scalar", "gpsimd")


class Em:
    def __init__(self, nc, es):
        self.nc = nc
        self.ops = {e: [] for e in ENGINES}
        self.sems = {}
        self.counts = {}
        self.last_wait = {e: {} for e in ENGINES}
        self._vec_dirty = False
        for name in ("t", "v", "a", "g", "wd", "xd", "od", "gd"):
            self.sems[name] = es.enter_context(nc.semaphore("sem_" + name))
            self.counts[name] = 0

    def wait(self, eng, ev):
        if ev is None:
            return
        s, val = ev
        if val <= 0:
            return
        if self.last_wait[eng].get(s, 0) >= val:
            return
        self.last_wait[eng][s] = val
        h = self.sems[s]
        self.ops[eng].append(lambda e, h=h, val=val: e.wait_ge(h, val))

    def waits(self, eng, evs):
        for ev in evs:
            self.wait(eng, ev)

    def op(self, eng, fn, inc=None, dma=False, dep=True):
        if eng == "vector":
            if dep and self._vec_dirty:
                self.ops[eng].append(lambda e: e.drain())
            self._vec_dirty = True
        if inc is None:
            self.ops[eng].append(fn)
            return None
        n = 16 if dma else 1
        self.counts[inc] += n
        val = self.counts[inc]
        h = self.sems[inc]
        if not dma and eng in ("vector", "scalar"):
            self.ops[eng].append(fn)
            self.ops[eng].append(lambda e, h=h, n=n: e.drain().then_inc(h, n))
            if eng == "vector":
                self._vec_dirty = False
        else:
            self.ops[eng].append(lambda e, fn=fn, h=h, n=n: fn(e).then_inc(h, n))
        return (inc, val)


class Banks:
    def __init__(self, em):
        self.em = em
        self.w = {}
        self.r = {}

    def write(self, eng, bank):
        for ev in self.r.get(bank, {}).values():
            self.em.wait(eng, ev)
        self.em.wait(eng, self.w.get(bank))

    def wrote(self, bank, ev):
        self.w[bank] = ev
        self.r[bank] = {}

    def read(self, eng, bank):
        self.em.wait(eng, self.w.get(bank))

    def did_read(self, bank, eng, ev):
        self.r.setdefault(bank, {})[eng] = ev


def build_nc(steps=None, dbg=()):
    nc = bass.Bass("TRN2", target_bir_lowering=False)
    es = ExitStack()

    x_d = nc.dram_tensor("x", [NT, D], F32, kind="ExternalInput").ap()
    wqk_d = nc.dram_tensor("wqk", [32, 128, DC * 128], F32, kind="ExternalInput").ap()
    wv_d = nc.dram_tensor("wv", [4, 128, DC * 512], F32, kind="ExternalInput").ap()
    wp_d = nc.dram_tensor("wp", [4, 128, DC * 512], F32, kind="ExternalInput").ap()
    w1_d = nc.dram_tensor("w1", [16, 128, DC * 128], F32, kind="ExternalInput").ap()
    w3_d = nc.dram_tensor("w3", [16, 128, DC * 128], F32, kind="ExternalInput").ap()
    w2_d = nc.dram_tensor("w2", [4, 128, DC * 512], F32, kind="ExternalInput").ap()
    wr_d = nc.dram_tensor("wr", [128, DC * 8], F32, kind="ExternalInput").ap()
    pb_d = nc.dram_tensor("pb", [1, D], F32, kind="ExternalInput").ap()
    mmm_d = nc.dram_tensor("mm_main", [128, 144], F32, kind="ExternalInput").ap()
    amm_d = nc.dram_tensor("am_main", [128, 144], F32, kind="ExternalInput").ap()
    mms_d = nc.dram_tensor("mm_s0", [8, 16], F32, kind="ExternalInput").ap()
    ams_d = nc.dram_tensor("am_s0", [8, 16], F32, kind="ExternalInput").ap()
    mml_d = nc.dram_tensor("mm_last", [120, 128], F32, kind="ExternalInput").ap()
    aml_d = nc.dram_tensor("am_last", [120, 128], F32, kind="ExternalInput").ap()
    id_d = nc.dram_tensor("ident", [128, 128], F32, kind="ExternalInput").ap()
    cm_d = nc.dram_tensor("coremask", [8, 1], F32, kind="ExternalInput").ap()
    y_d = nc.dram_tensor("y", [NT, D], F32, kind="ExternalOutput").ap()
    cc_i = nc.dram_tensor("cc_i", [8, D], F32)
    cc_o = nc.dram_tensor("cc_o", [8, D], F32)
    dbg_d = {}

    sb = lambda name, shp, dt: es.enter_context(nc.sbuf_tensor(name, shp, dt))
    XNT = sb("XNT", [128, DC * CW], BF16)
    VB = sb("VB", [128, 9 * 2048], BF16)
    ATT = sb("ATT", [128, DC * 1024], BF16)
    SLV = sb("SLV", [128, DC * 8], BF16)
    QKB = sb("QKB", [128, 4 * CW], BF16)
    WA = sb("WA", [128, 2 * 2048], BF16)
    WB = sb("WB", [128, 2 * DC * 512], BF16)
    XCH = sb("XCH", [128, 2 * 2048], F32)
    XBK = sb("XBK", [128, 2 * 128], BF16)
    XBK2 = sb("XBK2", [128, 2 * 128], F32)
    F32R = sb("F32R", [128, 2 * 128], F32)
    XSL = sb("XSL", [128, 2 * 512], F32)
    XPS = sb("XPS", [128, 2 * 512], F32)
    T1R = sb("T1R", [128, 2 * 144], F32)
    PR = sb("PR", [128, 2 * 160], BF16)
    PS0R = sb("PS0R", [32, 32], BF16)
    PTM = sb("PTM", [128, 2 * 128], BF16)
    PTH = sb("PTH", [32, 2 * 128], BF16)
    STT = sb("STT", [128, 32], F32)
    SST = sb("SST", [128, 16], F32)
    RPS = sb("RPS", [128, 40], F32)
    GT = sb("GT", [128, 8 * 8], BF16)
    MMM = sb("MMM", [128, 144], F32)
    AMM = sb("AMM", [128, 144], F32)
    MMS = sb("MMS", [8, 16], F32)
    AMS = sb("AMS", [8, 16], F32)
    MML = sb("MML", [120, 128], F32)
    AML = sb("AML", [120, 128], F32)
    IDB = sb("IDB", [128, 128], BF16)
    IDF = sb("IDF", [128, 128], F32)
    ONE = sb("ONE", [1, 128], BF16)
    PBB = sb("PBB", [1, D], BF16)
    WR = sb("WR", [128, DC * 8], F32)
    CM = sb("CM", [8, 1], F32)
    ZAP = sb("ZAP", [128, 1], F32)
    EPSAP = sb("EPSAP", [128, 1], F32)
    SQSL = sb("SQSL", [128, 512], F32)
    FIN = sb("FIN", [8, 2 * D], F32)

    ps = lambda name, shp, dt: es.enter_context(nc.psum_tensor(name, shp, dt))
    PM = [ps("PM0", [128, 512], F32), ps("PM1", [128, 512], F32)]
    PS = [ps("PS0", [128, 512], F32), ps("PS1", [128, 512], F32)]
    PV = [ps("PV0", [128, 512], F32), ps("PV1", [128, 512], F32)]
    PT = [ps("PT0", [128, 1024], BF16), ps("PT1", [128, 1024], BF16)]
    BK_PM, BK_PS, BK_PV, BK_PT = (0, 1), (2, 3), (4, 5), (6, 7)

    em = Em(nc, es)
    bk = Banks(em)

    ld_evs = []
    for dst, src in ((MMM, mmm_d), (AMM, amm_d), (MMS, mms_d), (AMS, ams_d),
                     (MML, mml_d), (AML, aml_d), (IDF, id_d), (WR, wr_d),
                     (CM, cm_d)):
        ld_evs.append(em.op("sync", lambda e, d=dst, s=src: e.dma_start(
            out=d[:, :], in_=s[:, :]), inc="xd", dma=True))
    ld_evs.append(em.op("gpsimd", lambda e: e.dma_start(
        out=IDB[:, :], in_=id_d[:, :]), inc="wd", dma=True))
    ld_evs.append(em.op("gpsimd", lambda e: e.dma_start(
        out=PBB[:, :], in_=pb_d[:, :]), inc="wd", dma=True))
    em.waits("vector", ld_evs)
    em.op("vector", lambda e: e.memset(ZAP[:, :], 0.0))
    em.op("vector", lambda e: e.memset(EPSAP[:, :], EPS))
    em.op("vector", lambda e: e.memset(PR[:, 144:160], 0.0))
    em.op("vector", lambda e: e.memset(PR[:, 304:320], 0.0))
    em.op("vector", lambda e: e.memset(PS0R[:, :], 0.0))
    em.op("vector", lambda e: e.memset(VB[0:32, 8 * 2048:9 * 2048], 0.0))
    ev_memset = em.op("vector", lambda e: e.memset(ONE[:, :], 1.0), inc="v")
    em.waits("tensor", ld_evs + [ev_memset])
    em.waits("scalar", ld_evs + [ev_memset])

    state = {
        "wa": [None, None], "wb": [None, None],
        "wa_cons": [None, None], "wb_cons": [None, None],
        "xch_cons": [None, None],
        "qk_cons": [None, None, None, None],
        "xsl_cons": [None, None], "xps_cons": [None, None],
        "t1_cons": [None, None], "pr_cons": [None, None],
        "ptm_cons": [None, None], "pth_cons": [None, None],
        "xbk_cons": [None, None], "xbk2_cons": [None, None],
        "f32r_cons": [None, None],
        "y_written": {},
        "ln_ev": {"vector": None, "scalar": None},
        "ln1_chunk_evs": {},
        "ln2_chunk_evs": {},
        "v_done": {"vector": None, "scalar": None},
    }

    def load_wa(slot, src_ap):
        em.wait("gpsimd", state["wa_cons"][slot])
        ev = em.op("gpsimd", lambda e, s=src_ap, sl=slot: e.dma_start(
            out=WA[:, sl * 2048:(sl + 1) * 2048], in_=s,
            max_dma_last_dim=8192), inc="wd", dma=True)
        state["wa"][slot] = ev
        return ev

    def load_wb(slot, src_ap):
        em.wait("gpsimd", state["wb_cons"][slot])
        ev = em.op("gpsimd", lambda e, s=src_ap, sl=slot: e.dma_start(
            out=WB[:, sl * DC * 512:(sl + 1) * DC * 512], in_=s,
            max_dma_last_dim=8192), inc="wd", dma=True)
        state["wb"][slot] = ev
        return ev

    def ln_chunk(tile, i, nrow, rows_src, is_ln2):
        sl = i % 2
        em.wait("sync", state["xch_cons"][sl])
        if is_ln2:
            em.wait("sync", state["y_written"][rows_src // 128])
            src = y_d[rows_src:rows_src + nrow, :]
        else:
            src = x_d[rows_src:rows_src + nrow, :]
        ev_x = em.op("sync", lambda e, s=src, sl=sl, n=nrow: e.dma_start(
            out=XCH[0:n, sl * 2048: sl * 2048 + 2048], in_=s), inc="xd", dma=True)
        em.wait("vector", ev_x)
        em.op("vector", lambda e, sl=sl, n=nrow: e.reduce_sum(
            STT[0:n, 23:24], XCH[0:n, sl * 2048: sl * 2048 + 2048], axis=AX.X))
        em.op("vector", lambda e, n=nrow: e.tensor_scalar(
            out=STT[0:n, 24:25], in0=STT[0:n, 23:24], scalar1=1.0 / D,
            scalar2=None, op0=ALU.mult))
        for q in range(4):
            em.op("vector", lambda e, q=q, sl=sl, n=nrow: e.scalar_tensor_tensor(
                out=SQSL[0:n, 0:512],
                in0=XCH[0:n, sl * 2048 + q * 512: sl * 2048 + (q + 1) * 512],
                scalar=1.0,
                in1=XCH[0:n, sl * 2048 + q * 512: sl * 2048 + (q + 1) * 512],
                op0=ALU.mult, op1=ALU.mult))
            em.op("vector", lambda e, q=q, n=nrow: e.reduce_sum(
                STT[0:n, 16 + q:17 + q], SQSL[0:n, 0:512], axis=AX.X))
        em.op("vector", lambda e, n=nrow: e.tensor_tensor(
            out=STT[0:n, 20:21], in0=STT[0:n, 16:17], in1=STT[0:n, 17:18], op=ALU.add))
        em.op("vector", lambda e, n=nrow: e.tensor_tensor(
            out=STT[0:n, 21:22], in0=STT[0:n, 18:19], in1=STT[0:n, 19:20], op=ALU.add))
        em.op("vector", lambda e, n=nrow: e.tensor_tensor(
            out=STT[0:n, 22:23], in0=STT[0:n, 20:21], in1=STT[0:n, 21:22], op=ALU.add))
        em.op("vector", lambda e, n=nrow: e.tensor_scalar(
            out=STT[0:n, 26:27], in0=STT[0:n, 22:23], scalar1=1.0 / D,
            scalar2=None, op0=ALU.mult))
        em.op("vector", lambda e, n=nrow: e.tensor_tensor(
            out=STT[0:n, 18:19], in0=STT[0:n, 24:25], in1=STT[0:n, 24:25], op=ALU.mult))
        ev_mv = em.op("vector", lambda e, n=nrow: e.tensor_tensor(
            out=STT[0:n, 25:26], in0=STT[0:n, 26:27], in1=STT[0:n, 18:19],
            op=ALU.subtract), inc="v")
        em.wait("scalar", ev_mv)
        ev_sd = em.op("scalar", lambda e, n=nrow: e.activation(
            out=STT[0:n, 27:28], in_=STT[0:n, 25:26], func=ACTF.Sqrt,
            bias=EPSAP[0:n, 0:1], scale=1.0), inc="a")
        em.wait("vector", ev_sd)
        em.op("vector", lambda e, n=nrow: e.reciprocal(
            STT[0:n, 28:29], STT[0:n, 27:28]))
        ev_nb = em.op("vector", lambda e, n=nrow: e.tensor_scalar(
            out=STT[0:n, 29:30], in0=STT[0:n, 24:25], scalar1=STT[0:n, 28:29],
            scalar2=-1.0, op0=ALU.mult, op1=ALU.mult), inc="v")
        em.wait("scalar", ev_nb)

        t0 = i * 128
        ev_nm = None
        chunk_evs = []
        for c in range(DC):
            bs = c % 2
            dst = XBK2 if is_ln2 else XBK
            em.wait("scalar", state["xbk2_cons" if is_ln2 else "xbk_cons"][bs])
            ev_nm = em.op("scalar", lambda e, c=c, sl=sl, bs=bs, n=nrow, d=dst: e.activation(
                out=d[0:n, bs * 128:bs * 128 + 128],
                in_=XCH[0:n, sl * 2048 + c * 128: sl * 2048 + (c + 1) * 128],
                func=ACTF.Identity, bias=STT[0:n, 29:30], scale=STT[0:n, 28:29]),
                inc="a")
            if is_ln2:
                pmb = BK_PM[bs]
                bk.write("tensor", pmb)
                em.wait("tensor", ev_nm)
                ev_tr = em.op("tensor", lambda e, bs=bs, n=nrow: e.transpose(
                    PM[bs][0:128, 0:n], XBK2[0:n, bs * 128:bs * 128 + 128],
                    IDF[0:n, 0:n]), inc="t")
                bk.wrote(pmb, ev_tr)
                state["xbk2_cons"][bs] = ev_tr
                em.wait("vector", ev_tr)
                em.wait("vector", state["f32r_cons"][bs])
                ev_f = em.op("vector", lambda e, bs=bs, n=nrow: e.tensor_copy(
                    F32R[:, bs * 128:bs * 128 + n], PM[bs][0:128, 0:n]), inc="v")
                bk.did_read(pmb, "vector", ev_f)
                em.wait("scalar", ev_f)
                ev_cb = em.op("scalar", lambda e, bs=bs, c=c, t0=t0, n=nrow: e.activation(
                    out=XNT[:, c * CW + t0: c * CW + t0 + n],
                    in_=PM[bs][0:128, 0:n], func=ACTF.Copy, bias=0.0, scale=1.0),
                    inc="a")
                bk.did_read(pmb, "scalar", ev_cb)
                state["ln_ev"]["scalar"] = ev_cb
                bk.write("tensor", BK_PS[0])
                em.wait("tensor", ev_f)
                ev_rmm = em.op("tensor", lambda e, bs=bs, c=c, i=i, n=nrow: e.matmul(
                    PS[0][0:n, 8 * i: 8 * i + 8],
                    lhsT=F32R[:, bs * 128:bs * 128 + n],
                    rhs=WR[:, 8 * c:8 * c + 8],
                    start=(c == 0), stop=(c == DC - 1)), inc="t")
                state["f32r_cons"][bs] = ev_rmm
                if c == DC - 1:
                    bk.wrote(BK_PS[0], ev_rmm)
            else:
                ptb = BK_PT[bs]
                bk.write("tensor", ptb)
                em.wait("tensor", ev_nm)
                ev_tr = em.op("tensor", lambda e, bs=bs, n=nrow: e.transpose(
                    PT[bs][0:128, 0:n], XBK[0:n, bs * 128:bs * 128 + 128],
                    IDB[0:n, 0:n]), inc="t")
                bk.wrote(ptb, ev_tr)
                state["xbk_cons"][bs] = ev_tr
                eng = "vector" if c % 2 == 0 else "scalar"
                em.wait(eng, ev_tr)
                if eng == "vector":
                    ev_e = em.op("vector", lambda e, bs=bs, c=c, t0=t0, n=nrow: e.tensor_copy(
                        XNT[:, c * CW + t0: c * CW + t0 + n], PT[bs][0:128, 0:n]),
                        inc="v")
                else:
                    ev_e = em.op("scalar", lambda e, bs=bs, c=c, t0=t0, n=nrow: e.activation(
                        out=XNT[:, c * CW + t0: c * CW + t0 + n],
                        in_=PT[bs][0:128, 0:n], func=ACTF.Copy, bias=0.0, scale=1.0),
                        inc="a")
                bk.did_read(ptb, eng, ev_e)
                state["ln_ev"][eng] = ev_e
                chunk_evs.append(ev_e)
        if is_ln2:
            state["ln2_chunk_evs"][i] = ev_cb
        else:
            state["ln1_chunk_evs"][i] = chunk_evs[-2:]
        state["xch_cons"][sl] = ev_nm

    def ln_phase(tile, is_ln2):
        base = tile * TILE
        ntok = 1040 if (tile == 0 and not is_ln2) else 1024
        nch = (ntok + 127) // 128
        for i in range(nch):
            nrow = min(128, ntok - i * 128)
            ln_chunk(tile, i, nrow, base + i * 128, is_ln2)
        if is_ln2:
            router_post(tile)

    def router_post(tile):
        RL = T1R
        bk.read("vector", BK_PS[0])
        em.wait("vector", state["t1_cons"][0])
        em.wait("vector", state["t1_cons"][1])
        ev_rl = em.op("vector", lambda e: e.tensor_copy(
            RL[:, 0:128], PS[0][0:128, 0:128]), inc="v")
        bk.did_read(BK_PS[0], "vector", ev_rl)
        ev_g = None
        for i in range(8):
            em.op("vector", lambda e, i=i: e.reduce_max(
                RPS[:, 32:33], RL[:, 8 * i:8 * i + 8], axis=AX.X, negate=True))
            ev1 = em.op("vector", lambda e, i=i: e.tensor_scalar(
                out=RPS[:, 0:8], in0=RL[:, 8 * i:8 * i + 8], scalar1=RPS[:, 32:33],
                scalar2=None, op0=ALU.add), inc="v")
            em.wait("scalar", ev1)
            ev2 = em.op("scalar", lambda e: e.activation(
                out=RPS[:, 0:8], in_=RPS[:, 0:8], func=ACTF.Exp, bias=ZAP[:, 0:1],
                scale=1.0), inc="a")
            em.wait("vector", ev2)
            em.op("vector", lambda e: e.reduce_sum(
                RPS[:, 33:34], RPS[:, 0:8], axis=AX.X))
            em.op("vector", lambda e: e.reciprocal(RPS[:, 34:35], RPS[:, 33:34]))
            em.op("vector", lambda e: e.tensor_scalar(
                out=RPS[:, 0:8], in0=RPS[:, 0:8], scalar1=RPS[:, 34:35],
                scalar2=None, op0=ALU.mult))
            em.op("vector", lambda e: e.max(RPS[:, 8:16], RPS[:, 0:8]))
            em.op("vector", lambda e: e.tensor_scalar(
                out=RPS[:, 16:24], in0=RPS[:, 0:8], scalar1=RPS[:, 9:10],
                scalar2=None, op0=ALU.is_ge))
            em.op("vector", lambda e: e.tensor_tensor(
                out=RPS[:, 35:36], in0=RPS[:, 8:9], in1=RPS[:, 9:10], op=ALU.add))
            em.op("vector", lambda e: e.reciprocal(RPS[:, 36:37], RPS[:, 35:36]))
            ev_g = em.op("vector", lambda e, i=i: e.scalar_tensor_tensor(
                out=GT[:, 8 * i:8 * i + 8], in0=RPS[:, 16:24], scalar=RPS[:, 36:37],
                in1=RPS[:, 0:8], op0=ALU.mult, op1=ALU.mult), inc="v")
        state["gates_ev"] = ev_g
        state["rl_ev"] = ev_rl

    def v_phase(tile):
        ntok = 1040 if tile == 0 else 1024
        nj = 9 if tile == 0 else 8
        ev_e = None
        for n in range(4):
            sl = n % 2
            load_wb(sl, wv_d[n])
            ev_w = state["wb"][sl]
            for j in range(nj):
                nrow = min(128, ntok - j * 128)
                pm = (n * 9 + j) % 2
                pmb = BK_PM[pm]
                bk.write("tensor", pmb)
                em.wait("tensor", ev_w)
                em.waits("tensor", state["ln1_chunk_evs"][j])
                ev_mm = None
                for c in range(DC):
                    ev_mm = em.op("tensor", lambda e, c=c, j=j, sl=sl, pm=pm, nr=nrow: e.matmul(
                        PM[pm][0:nr, 0:512],
                        lhsT=XNT[:, c * CW + j * 128: c * CW + j * 128 + nr],
                        rhs=WB[:, sl * DC * 512 + c * 512: sl * DC * 512 + (c + 1) * 512],
                        start=(c == 0), stop=(c == DC - 1)),
                        inc=("t" if c == DC - 1 else None))
                bk.wrote(pmb, ev_mm)
                state["wb_cons"][sl] = ev_mm
                eng = "vector" if j % 2 == 0 else "scalar"
                em.wait(eng, ev_mm)
                if eng == "vector":
                    ev_e = em.op("vector", lambda e, j=j, n=n, pm=pm, nr=nrow: e.tensor_copy(
                        VB[0:nr, j * 2048 + n * 512: j * 2048 + (n + 1) * 512],
                        PM[pm][0:nr, 0:512]), inc="v")
                else:
                    ev_e = em.op("scalar", lambda e, j=j, n=n, pm=pm, nr=nrow: e.activation(
                        out=VB[0:nr, j * 2048 + n * 512: j * 2048 + (n + 1) * 512],
                        in_=PM[pm][0:nr, 0:512], func=ACTF.Copy, bias=0.0,
                        scale=1.0), inc="a")
                bk.did_read(pmb, eng, ev_e)
                state["v_done"][eng] = ev_e

    def units_for_tile(tile):
        if tile == 0:
            us = [("s0", 0, 8, 0, 16, 0)]
            us += [("main", 8 + 128 * j, 128, 128 * j, 144, j) for j in range(8)]
        else:
            us = [("main", 8 + 128 * j, 128, 128 * j, 144, j) for j in range(7)]
            us.append(("last", 904, 120, 896, 128, 7))
        return us

    def qk_group(tile, pair, kind, off, n, last):
        sl = pair % 2
        ring = kind * 2 + sl
        blk = 2 * pair + kind
        wsl = blk % 2
        if off == 0:
            load_wa(wsl, wqk_d[blk])
        ev_w = state["wa"][wsl]
        state["qk_pm"] = (state.get("qk_pm", 0) + 1) % 2
        pm = state["qk_pm"]
        pmb = BK_PM[pm]
        bk.write("tensor", pmb)
        em.wait("tensor", ev_w)
        ev_mm = None
        for c in range(DC):
            ev_mm = em.op("tensor", lambda e, c=c, off=off, n=n, wsl=wsl, pm=pm: e.matmul(
                PM[pm][0:128, 0:n],
                lhsT=WA[:, wsl * 2048 + c * 128: wsl * 2048 + (c + 1) * 128],
                rhs=XNT[:, c * CW + off: c * CW + off + n],
                start=(c == 0), stop=(c == DC - 1)),
                inc=("t" if c == DC - 1 else None))
        bk.wrote(pmb, ev_mm)
        state["wa_cons"][wsl] = ev_mm
        eng = "vector" if kind == 0 else "scalar"
        em.wait(eng, ev_mm)
        em.wait(eng, state["qk_cons"][ring])
        if eng == "vector":
            ev_e = em.op("vector", lambda e, ring=ring, off=off, n=n, pm=pm: e.tensor_copy(
                QKB[:, ring * CW + off: ring * CW + off + n], PM[pm][0:128, 0:n]),
                inc="v")
        else:
            ev_e = em.op("scalar", lambda e, ring=ring, off=off, n=n, pm=pm: e.activation(
                out=QKB[:, ring * CW + off: ring * CW + off + n],
                in_=PM[pm][0:128, 0:n], func=ACTF.Copy, bias=0.0, scale=1.0),
                inc="a")
        bk.did_read(pmb, eng, ev_e)
        if last:
            state.setdefault("qk_ready", {})[(pair, kind)] = ev_e

    def attn_unit(tile, head, uidx, unit):
        kind, q0, nq, k0, nk, vj = unit
        par = head % 2
        pair = head // 2
        rsl = pair % 2
        p0 = 64 * par
        state["ucnt"] = state.get("ucnt", 0) + 1
        u = state["ucnt"] % 2
        sb0 = 4 * u
        qr = QKB[p0:p0 + 64, rsl * CW + q0: rsl * CW + q0 + nq]
        kr = QKB[p0:p0 + 64, (2 + rsl) * CW + k0: (2 + rsl) * CW + k0 + nk]
        ev_qk_q = state["qk_ready"][(pair, 0)]
        ev_qk_k = state["qk_ready"][(pair, 1)]
        psb = BK_PS[u]
        bk.write("tensor", psb)
        em.waits("tensor", [ev_qk_q, ev_qk_k])
        ev_sc = em.op("tensor", lambda e, qr=qr, kr=kr, u=u, nq=nq, nk=nk: e.matmul(
            PS[u][0:nq, 0:nk], lhsT=qr, rhs=kr, start=True, stop=True), inc="t")
        bk.wrote(psb, ev_sc)
        state["qk_cons"][rsl] = ev_sc
        state["qk_cons"][2 + rsl] = ev_sc
        mm, am = {"main": (MMM, AMM), "s0": (MMS, AMS), "last": (MML, AML)}[kind]
        em.wait("vector", ev_sc)
        em.wait("vector", state["t1_cons"][u])
        em.op("vector", lambda e, u=u, nq=nq, nk=nk, mm=mm: e.scalar_tensor_tensor(
            out=T1R[0:nq, u * 144: u * 144 + nk], in0=PS[u][0:nq, 0:nk],
            scalar=1.0, in1=mm[0:nq, 0:nk], op0=ALU.mult, op1=ALU.mult))
        em.op("vector", lambda e, u=u, nq=nq, nk=nk, am=am: e.tensor_tensor(
            out=T1R[0:nq, u * 144: u * 144 + nk],
            in0=T1R[0:nq, u * 144: u * 144 + nk],
            in1=am[0:nq, 0:nk], op=ALU.add))
        ev_nm = em.op("vector", lambda e, u=u, nq=nq, nk=nk, sb0=sb0: e.reduce_max(
            SST[0:nq, sb0:sb0 + 1], T1R[0:nq, u * 144: u * 144 + nk], axis=AX.X,
            negate=True), inc="v")
        bk.did_read(psb, "vector", ev_nm)
        em.wait("scalar", ev_nm)
        ev_ex = em.op("scalar", lambda e, u=u, nq=nq, nk=nk, sb0=sb0: e.activation(
            out=T1R[0:nq, u * 144: u * 144 + nk],
            in_=T1R[0:nq, u * 144: u * 144 + nk], func=ACTF.Exp,
            bias=SST[0:nq, sb0:sb0 + 1], scale=1.0), inc="a")
        em.wait("vector", ev_ex)
        em.op("vector", lambda e, u=u, nq=nq, nk=nk, sb0=sb0: e.reduce_sum(
            SST[0:nq, sb0 + 1:sb0 + 2], T1R[0:nq, u * 144: u * 144 + nk], axis=AX.X))
        ev_rr = em.op("vector", lambda e, nq=nq, sb0=sb0: e.reciprocal(
            SST[0:nq, sb0 + 2:sb0 + 3], SST[0:nq, sb0 + 1:sb0 + 2]), inc="v")
        em.wait("scalar", ev_rr)
        em.wait("scalar", state["pr_cons"][u])
        if kind == "s0":
            ev_p = em.op("scalar", lambda e, u=u, sb0=sb0: e.activation(
                out=PS0R[0:8, 0:16], in_=T1R[0:8, u * 144: u * 144 + 16],
                func=ACTF.Identity, bias=ZAP[0:8, 0:1], scale=SST[0:8, sb0 + 2:sb0 + 3]),
                inc="a")
        else:
            ev_p = em.op("scalar", lambda e, u=u, nq=nq, nk=nk, sb0=sb0: e.activation(
                out=PR[0:nq, u * 160: u * 160 + nk],
                in_=T1R[0:nq, u * 144: u * 144 + nk],
                func=ACTF.Identity, bias=ZAP[0:nq, 0:1], scale=SST[0:nq, sb0 + 2:sb0 + 3]),
                inc="a")
        state["t1_cons"][u] = ev_p
        ptb = BK_PT[u]
        bk.write("tensor", ptb)
        em.wait("tensor", ev_p)
        if kind == "s0":
            ev_tr = em.op("tensor", lambda e, u=u: e.transpose(
                PT[u][0:32, 0:32], PS0R[0:32, 0:32], IDB[0:32, 0:32]), inc="t")
            state["pr_cons"][u] = ev_tr
            bk.wrote(ptb, ev_tr)
            em.wait("vector", ev_tr)
            em.wait("vector", state["pth_cons"][u])
            ev_pth = em.op("vector", lambda e, u=u: e.tensor_copy(
                PTH[0:32, u * 128: u * 128 + 32], PT[u][0:32, 0:32]), inc="v")
            bk.did_read(ptb, "vector", ev_pth)
            ev_ptm = ev_pth
        else:
            ev_tr = em.op("tensor", lambda e, u=u: e.transpose(
                PT[u][0:128, 0:128], PR[:, u * 160: u * 160 + 128], IDB[:, :]),
                inc="t")
            if kind == "main":
                ev_tr = em.op("tensor", lambda e, u=u: e.transpose(
                    PT[u][0:32, 128:256], PR[:, u * 160 + 128: u * 160 + 160],
                    IDB[:, :]), inc="t")
            state["pr_cons"][u] = ev_tr
            bk.wrote(ptb, ev_tr)
            em.wait("vector", ev_tr)
            em.wait("vector", state["ptm_cons"][u])
            ev_ptm = em.op("vector", lambda e, u=u: e.tensor_copy(
                PTM[:, u * 128: u * 128 + 128], PT[u][0:128, 0:128]), inc="v")
            ev_pth = ev_ptm
            if kind == "main":
                em.wait("vector", state["pth_cons"][u])
                ev_pth = em.op("vector", lambda e, u=u: e.tensor_copy(
                    PTH[0:32, u * 128: u * 128 + 128], PT[u][0:32, 128:256]),
                    inc="v")
            bk.did_read(ptb, "vector", ev_pth)
        pvb = BK_PV[u]
        bk.write("tensor", pvb)
        em.waits("tensor", [ev_ptm, ev_pth,
                            state["v_done"]["vector"], state["v_done"]["scalar"]])
        tp = (0, p0)
        dvc = head * 64
        ch = head // 2
        dv0 = 64 * par
        if kind == "s0":
            ev_pv = em.op("tensor", lambda e, u=u, p0=p0, dvc=dvc, tp=tp: e.matmul(
                PV[u][p0:p0 + 64, 0:8],
                lhsT=VB[0:32, dvc: dvc + 64],
                rhs=PTH[0:32, u * 128: u * 128 + 8],
                start=True, stop=True, tile_position=tp), inc="t")
        elif kind == "last":
            ev_pv = em.op("tensor", lambda e, u=u, p0=p0, dvc=dvc, vj=vj, nq=nq, tp=tp: e.matmul(
                PV[u][p0:p0 + 64, 0:nq],
                lhsT=VB[:, vj * 2048 + dvc: vj * 2048 + dvc + 64],
                rhs=PTM[:, u * 128: u * 128 + nq],
                start=True, stop=True, tile_position=tp), inc="t")
        else:
            em.op("tensor", lambda e, u=u, p0=p0, dvc=dvc, vj=vj, tp=tp: e.matmul(
                PV[u][p0:p0 + 64, 0:128],
                lhsT=VB[:, vj * 2048 + dvc: vj * 2048 + dvc + 64],
                rhs=PTM[:, u * 128: u * 128 + 128],
                start=True, stop=False, tile_position=tp))
            ev_pv = em.op("tensor", lambda e, u=u, p0=p0, dvc=dvc, vj=vj, tp=tp: e.matmul(
                PV[u][p0:p0 + 64, 0:128],
                lhsT=VB[0:32, (vj + 1) * 2048 + dvc: (vj + 1) * 2048 + dvc + 64],
                rhs=PTH[0:32, u * 128: u * 128 + 128],
                start=False, stop=True, tile_position=tp), inc="t")
        state["ptm_cons"][u] = ev_pv
        state["pth_cons"][u] = ev_pv
        bk.wrote(pvb, ev_pv)
        em.wait("vector", ev_pv)
        if tile == 0 and kind == "main" and vj == 7:
            em.op("vector", lambda e, u=u, p0=p0, dv0=dv0, ch=ch, q0=q0: e.tensor_copy(
                ATT[dv0:dv0 + 64, ch * 1024 + q0: ch * 1024 + 1024],
                PV[u][p0:p0 + 64, 0:120]))
            ev_av = em.op("vector", lambda e, u=u, p0=p0, dv0=dv0, ch=ch: e.tensor_copy(
                SLV[dv0:dv0 + 64, ch * 8: ch * 8 + 8],
                PV[u][p0:p0 + 64, 120:128]), inc="v")
        else:
            ev_av = em.op("vector", lambda e, u=u, p0=p0, dv0=dv0, ch=ch, q0=q0, nq=nq: e.tensor_copy(
                ATT[dv0:dv0 + 64, ch * 1024 + q0: ch * 1024 + q0 + nq],
                PV[u][p0:p0 + 64, 0:nq]), inc="v")
        bk.did_read(pvb, "vector", ev_av)
        state["attn_last_ev"] = ev_av

    def qk_attn_phase(tile):
        ncols = 1040 if tile == 0 else 1024
        units = units_for_tile(tile)
        if tile == 1:
            for c in range(DC):
                ev_sv = em.op("vector", lambda e, c=c: e.tensor_copy(
                    ATT[:, c * 1024: c * 1024 + 8], SLV[:, c * 8: c * 8 + 8]),
                    inc="v")
            state["sliver_ev"] = ev_sv
        nsl = [(0, 512), (512, 512), (1024, 16)] if ncols == 1040 else [(0, 512), (512, 512)]
        for kind in (0, 1):
            for gi, (off, n) in enumerate(nsl):
                qk_group(tile, 0, kind, off, n, gi == len(nsl) - 1)
        for pair in range(16):
            groups = []
            if pair + 1 < 16:
                for kind in (0, 1):
                    for gi, (off, n) in enumerate(nsl):
                        groups.append((pair + 1, kind, off, n, gi == len(nsl) - 1))
            ulist = [(2 * pair + hh, uidx, unit)
                     for hh in range(2) for uidx, unit in enumerate(units)]
            per = max(1, len(ulist) // max(1, len(groups)))
            g = 0
            for i, (head, uidx, unit) in enumerate(ulist):
                attn_unit(tile, head, uidx, unit)
                if g < len(groups) and i % per == per - 1:
                    qk_group(tile, *groups[g])
                    g += 1
            while g < len(groups):
                qk_group(tile, *groups[g])
                g += 1

    def proj_phase(tile):
        base = tile * TILE
        em.wait("tensor", state.get("attn_last_ev"))
        if tile == 1:
            em.wait("tensor", state.get("sliver_ev"))
        for n in range(4):
            sl = n % 2
            load_wb(sl, wp_d[n])
            ev_w = state["wb"][sl]
            for t in range(8):
                pm = (n * 8 + t) % 2
                pmb = BK_PM[pm]
                bk.write("tensor", pmb)
                em.wait("tensor", ev_w)
                for c in range(DC):
                    em.op("tensor", lambda e, c=c, t=t, sl=sl, pm=pm: e.matmul(
                        PM[pm][0:128, 0:512],
                        lhsT=ATT[:, c * 1024 + t * 128: c * 1024 + (t + 1) * 128],
                        rhs=WB[:, sl * DC * 512 + c * 512: sl * DC * 512 + (c + 1) * 512],
                        start=(c == 0), stop=False))
                ev_mm = em.op("tensor", lambda e, n=n, pm=pm: e.matmul(
                    PM[pm][0:128, 0:512], lhsT=ONE[0:1, 0:128],
                    rhs=PBB[0:1, n * 512:(n + 1) * 512],
                    start=False, stop=True), inc="t")
                bk.wrote(pmb, ev_mm)
                state["wb_cons"][sl] = ev_mm
                xs = t % 2
                em.wait("sync", state["xsl_cons"][xs])
                ev_x = em.op("sync", lambda e, t=t, n=n, xs=xs, base=base: e.dma_start(
                    out=XSL[:, xs * 512:(xs + 1) * 512],
                    in_=x_d[base + t * 128: base + (t + 1) * 128, n * 512:(n + 1) * 512]),
                    inc="xd", dma=True)
                em.waits("vector", [ev_mm, ev_x])
                em.wait("vector", state["xps_cons"][xs])
                ev_xp = em.op("vector", lambda e, xs=xs, pm=pm: e.tensor_tensor(
                    out=XPS[:, xs * 512:(xs + 1) * 512],
                    in0=PM[pm][0:128, 0:512], in1=XSL[:, xs * 512:(xs + 1) * 512],
                    op=ALU.add), inc="v")
                bk.did_read(pmb, "vector", ev_xp)
                state["xsl_cons"][xs] = ev_xp
                em.wait("scalar", ev_xp)
                ev_o = em.op("scalar", lambda e, t=t, n=n, xs=xs, base=base: e.dma_start(
                    out=y_d[base + t * 128: base + (t + 1) * 128, n * 512:(n + 1) * 512],
                    in_=XPS[:, xs * 512:(xs + 1) * 512]), inc="od", dma=True)
                state["xps_cons"][xs] = ev_o
                if n == 3:
                    state["y_written"][tile * 8 + t] = ev_o

    def ffn_mlp(tile, which):
        src, scw = (XNT, CW) if which == 0 else (ATT, 1024)
        dst = ATT if which == 0 else VB
        w_dd = w1_d if which == 0 else w3_d

        ev_e = None
        for m in range(16):
            wsl = m % 2
            load_wa(wsl, w_dd[m])
            ev_w = state["wa"][wsl]
            for n in range(2):
                pm = (m * 2 + n) % 2
                pmb = BK_PM[pm]
                bk.write("tensor", pmb)
                em.wait("tensor", ev_w)
                if which == 0:
                    em.wait("tensor", state["ln2_chunk_evs"][4 * n + 3])
                ev_mm = None
                for c in range(DC):
                    ev_mm = em.op("tensor", lambda e, c=c, n=n, wsl=wsl, pm=pm, src=src, scw=scw: e.matmul(
                        PM[pm][0:128, 0:512],
                        lhsT=WA[:, wsl * 2048 + c * 128: wsl * 2048 + (c + 1) * 128],
                        rhs=src[:, c * scw + n * 512: c * scw + n * 512 + 512],
                        start=(c == 0), stop=(c == DC - 1)),
                        inc=("t" if c == DC - 1 else None))
                bk.wrote(pmb, ev_mm)
                state["wa_cons"][wsl] = ev_mm
                em.wait("scalar", ev_mm)
                ev_e = em.op("scalar", lambda e, m=m, n=n, pm=pm, dst=dst: e.activation(
                    out=dst[:, m * 1024 + n * 512: m * 1024 + (n + 1) * 512],
                    in_=PM[pm][0:128, 0:512], func=ACTF.Silu, bias=ZAP[0:128, 0:1], scale=1.0),
                    inc="a")
                bk.did_read(pmb, "scalar", ev_e)
        state["mlp_ev"] = ev_e

    def ffn_out(tile):
        sc_ps = [PS[0], PS[1], PV[0], PV[1]]
        sc_bk = [BK_PS[0], BK_PS[1], BK_PV[0], BK_PV[1]]
        ev_h3_gate = state.get("h1_mm_last")
        ev_acc = None
        for n in range(4):
            sl = n % 2
            load_wb(sl, w2_d[n])
            ev_w = state["wb"][sl]
            bk.write("tensor", sc_bk[n])
            ev_s = None
            for t in range(8):
                pm = (n * 8 + t) % 2
                pmb = BK_PM[pm]
                bk.write("tensor", pmb)
                em.wait("tensor", ev_w)
                ev_mm = None
                for c in range(DC):
                    ev_mm = em.op("tensor", lambda e, c=c, t=t, sl=sl, pm=pm: e.matmul(
                        PM[pm][0:128, 0:512],
                        lhsT=VB[:, c * 1024 + t * 128: c * 1024 + (t + 1) * 128],
                        rhs=WB[:, sl * DC * 512 + c * 512: sl * DC * 512 + (c + 1) * 512],
                        start=(c == 0), stop=(c == DC - 1)),
                        inc=("t" if c == DC - 1 else None))
                bk.wrote(pmb, ev_mm)
                state["wb_cons"][sl] = ev_mm
                em.wait("scalar", ev_mm)
                em.wait("scalar", ev_h3_gate)
                ev_h3 = em.op("scalar", lambda e, t=t, n=n, pm=pm: e.activation(
                    out=XNT[:, t * 2048 + n * 512: t * 2048 + (n + 1) * 512],
                    in_=PM[pm][0:128, 0:512], func=ACTF.Copy, bias=0.0, scale=1.0),
                    inc="a")
                bk.did_read(pmb, "scalar", ev_h3)
                em.waits("tensor", [ev_h3, state.get("gates_ev")])
                ev_s = em.op("tensor", lambda e, t=t, n=n: e.matmul(
                    sc_ps[n][0:8, 0:512],
                    lhsT=GT[:, t * 8:(t + 1) * 8],
                    rhs=XNT[:, t * 2048 + n * 512: t * 2048 + (n + 1) * 512],
                    start=(t == 0), stop=(t == 7)), inc="t")
            bk.wrote(sc_bk[n], ev_s)
            em.wait("vector", ev_s)
            if tile == 0:
                ev_acc = em.op("vector", lambda e, n=n: e.tensor_copy(
                    FIN[0:8, n * 512:(n + 1) * 512], sc_ps[n][0:8, 0:512]), inc="v")
            else:
                ev_acc = em.op("vector", lambda e, n=n: e.tensor_tensor(
                    out=FIN[0:8, n * 512:(n + 1) * 512],
                    in0=FIN[0:8, n * 512:(n + 1) * 512],
                    in1=sc_ps[n][0:8, 0:512], op=ALU.add), inc="v")
            bk.did_read(sc_bk[n], "vector", ev_acc)
        state["acc_ev"] = ev_acc

    def finale():
        em.wait("gpsimd", state["acc_ev"])
        ev1 = em.op("gpsimd", lambda e: e.dma_start(
            out=cc_i[:, :], in_=FIN[0:8, 0:D]), inc="gd", dma=True)
        em.wait("gpsimd", ev1)
        ev2 = em.op("gpsimd", lambda e: e.collective_compute(
            "AllReduce", ALU.add, replica_groups=[list(range(8))],
            ins=[cc_i[:, :]], outs=[cc_o[:, :]]), inc="g")
        em.wait("gpsimd", ev2)
        ev3 = em.op("gpsimd", lambda e: e.dma_start(
            out=FIN[0:8, D:2 * D], in_=cc_o[:, :]), inc="gd", dma=True)
        em.wait("sync", ev1)
        em.wait("sync", state["y_written"][0])
        ev4 = em.op("sync", lambda e: e.dma_start(
            out=FIN[0:8, 0:D], in_=y_d[0:8, :]), inc="xd", dma=True)
        em.waits("vector", [ev3, ev4])
        em.op("vector", lambda e: e.tensor_scalar(
            out=FIN[0:8, D:2 * D], in0=FIN[0:8, D:2 * D], scalar1=CM[0:8, 0:1],
            scalar2=None, op0=ALU.mult))
        ev5 = em.op("vector", lambda e: e.tensor_tensor(
            out=FIN[0:8, D:2 * D], in0=FIN[0:8, D:2 * D], in1=FIN[0:8, 0:D],
            op=ALU.add), inc="v")
        em.wait("scalar", ev5)
        em.op("scalar", lambda e: e.dma_start(
            out=y_d[0:8, :], in_=FIN[0:8, D:2 * D]), inc="od", dma=True)
        em.op("scalar", lambda e, v=em.counts["od"]: e.wait_ge(em.sems["od"], v))

    def emit_all(nsteps):
        cnt = [0]
        def step(f):
            if nsteps is None or cnt[0] < nsteps:
                f()
            cnt[0] += 1
        for tile in (0, 1):
            step(lambda tile=tile: ln_phase(tile, is_ln2=False))
            step(lambda tile=tile: v_phase(tile))
            step(lambda tile=tile: qk_attn_phase(tile))
            step(lambda tile=tile: proj_phase(tile))
            step(lambda tile=tile: ln_phase(tile, is_ln2=True))
            def _mlp0(tile=tile):
                ffn_mlp(tile, 0)
                state["h1_mm_last"] = state["wa_cons"][1]
            step(_mlp0)
            step(lambda tile=tile: ffn_mlp(tile, 1))
            step(lambda tile=tile: ffn_out(tile))
        step(finale)
    emit_all(steps)

    TENS = {"XNT": XNT, "VB": VB, "ATT": ATT, "QKB": QKB, "GT": GT, "T1R": T1R,
            "SLV": SLV, "FIN": FIN, "STT": STT, "RPS": RPS, "XCH": XCH}
    if dbg:
        for s in ("t", "v", "a", "wd", "xd", "od", "gd", "g"):
            em.wait("scalar", (s, em.counts[s]))
        for name in dbg:
            th = TENS[name]
            shp = [th.shape[0], th.shape[1]]
            dt = th.ap().dtype
            dd = nc.dram_tensor("dbg_" + name, shp, dt, kind="ExternalOutput").ap()
            em.op("scalar", lambda e, dd=dd, th=th: e.dma_start(
                out=dd[:, :], in_=th[:, :]), inc="od", dma=True)
        em.op("scalar", lambda e, v=em.counts["od"]: e.wait_ge(em.sems["od"], v))

    with nc.Block() as block:
        @block.sync
        def _(e):
            for f in em.ops["sync"]:
                f(e)

        @block.tensor
        def _(e):
            for f in em.ops["tensor"]:
                f(e)

        @block.vector
        def _(e):
            for f in em.ops["vector"]:
                f(e)

        @block.scalar
        def _(e):
            for f in em.ops["scalar"]:
                f(e)

        @block.gpsimd
        def _(e):
            for f in em.ops["gpsimd"]:
                f(e)

    es.close()
    return nc



def _host_consts():
    mm_main = np.zeros((128, 144), np.float32)
    am_main = np.full((128, 144), NEG, np.float32)
    for r in range(128):
        for c in range(r, r + 16):
            d = abs(c - r - 8)
            mm_main[r, c] = SCALE * (WIN - d / 2.0) / WIN
            am_main[r, c] = 0.0
    mm_s0 = np.zeros((8, 16), np.float32)
    am_s0 = np.full((8, 16), NEG, np.float32)
    for r in range(8):
        for c in range(0, r + 8):
            d = abs(c - r)
            mm_s0[r, c] = SCALE * (WIN - d / 2.0) / WIN
            am_s0[r, c] = 0.0
    mm_last = np.zeros((120, 128), np.float32)
    am_last = np.full((120, 128), NEG, np.float32)
    for r in range(120):
        for c in range(r, min(r + 16, 128)):
            d = abs(c - r - 8)
            mm_last[r, c] = SCALE * (WIN - d / 2.0) / WIN
            am_last[r, c] = 0.0
    return mm_main, am_main, mm_s0, am_s0, mm_last, am_last


def _prep_weights(qkv_w, proj_w, proj_b, w1, w3, w2, wr):
    f = np.float32
    qkv_w = np.asarray(qkv_w, f)
    qk = np.ascontiguousarray(qkv_w[0:4096])
    T = qk.T
    Bq = T.reshape(16, 128, 32, 128).transpose(2, 1, 0, 3)
    order = []
    for p in range(16):
        order += [p, 16 + p]
    wqk = np.ascontiguousarray(Bq[order]).reshape(32, 128, 16 * 128)

    def nslice(w):
        T = np.asarray(w, f).T
        return np.ascontiguousarray(
            T.reshape(16, 128, 4, 512).transpose(2, 1, 0, 3)).reshape(4, 128, 16 * 512)

    wv = nslice(qkv_w[4096:6144])
    wp = nslice(proj_w)
    w2n = nslice(w2)

    def mblk(w):
        T = np.asarray(w, f).T
        return np.ascontiguousarray(
            T.reshape(16, 128, 16, 128).transpose(2, 1, 0, 3)).reshape(16, 128, 16 * 128)

    w1m = mblk(w1)
    w3m = mblk(w3)
    wrh = np.ascontiguousarray(
        np.asarray(wr, f).T.reshape(16, 128, 8).transpose(1, 0, 2)).reshape(128, 128)
    pb = np.ascontiguousarray(np.asarray(proj_b, f).reshape(1, D))
    return wqk, wv, wp, w1m, w3m, w2n, wrh, pb


def kernel(L0_attn_x, L0_attn_qkv_w, L0_attn_proj_w, L0_attn_proj_b,
           L0_ffn_w1, L0_ffn_w3, L0_ffn_w2, L0_ffn_wr):
    x = np.ascontiguousarray(np.asarray(L0_attn_x, np.float32))
    wqk, wv, wp, w1m, w3m, w2n, wrh, pb = _prep_weights(
        L0_attn_qkv_w, L0_attn_proj_w, L0_attn_proj_b, L0_ffn_w1, L0_ffn_w3,
        L0_ffn_w2, L0_ffn_wr)
    mm_main, am_main, mm_s0, am_s0, mm_last, am_last = _host_consts()
    ident = np.eye(128, dtype=np.float32)
    nc = build_nc()
    in_maps = []
    for c in range(8):
        cm = np.full((8, 1), 1.0 if c == 0 else 0.0, np.float32)
        in_maps.append({
            "x": np.ascontiguousarray(x[c]),
            "wqk": wqk, "wv": wv, "wp": wp, "w1": w1m, "w3": w3m, "w2": w2n,
            "wr": wrh, "pb": pb,
            "mm_main": mm_main, "am_main": am_main, "mm_s0": mm_s0,
            "am_s0": am_s0, "mm_last": mm_last, "am_last": am_last,
            "ident": ident, "coremask": cm,
        })
    res = run_bass_kernel_spmd(nc, in_maps, core_ids=list(range(8)))
    return np.stack([np.asarray(res.results[c]["y"]) for c in range(8)], axis=0)


# version 03
# speedup vs baseline: 1.4636x; 1.4636x over previous
import numpy as np
import concourse.bass as bass
import concourse.mybir as mybir
from concourse.bass_utils import run_bass_kernel_spmd
from contextlib import ExitStack

F32 = mybir.dt.float32
BF16 = mybir.dt.bfloat16
ALU = mybir.AluOpType
ACTF = mybir.ActivationFunctionType
AX = mybir.AxisListType

D = 2048
DC = 16
NT = 2048
TILE = 1024
H = 32
HD = 64
CW = 1040
NEG = -1.0e30
EPS = 1e-5
SCALE = HD ** -0.5
WIN = 8

ENGINES = ("sync", "tensor", "vector", "scalar", "gpsimd")


class Em:
    def __init__(self, nc, es):
        self.nc = nc
        self.ops = {e: [] for e in ENGINES}
        self.sems = {}
        self.counts = {}
        self.last_wait = {e: {} for e in ENGINES}
        self._vec_dirty = False
        for name in ("t", "v", "a", "g", "wd", "xd", "od", "gd"):
            self.sems[name] = es.enter_context(nc.semaphore("sem_" + name))
            self.counts[name] = 0

    def fence(self, eng, inc):
        self.counts[inc] += 1
        val = self.counts[inc]
        h = self.sems[inc]
        self.ops[eng].append(lambda e, h=h: e.drain().then_inc(h, 1))
        if eng == "vector":
            self._vec_dirty = False
        return (inc, val)

    def wait(self, eng, ev):
        if ev is None:
            return
        s, val = ev
        if val <= 0:
            return
        if self.last_wait[eng].get(s, 0) >= val:
            return
        self.last_wait[eng][s] = val
        h = self.sems[s]
        self.ops[eng].append(lambda e, h=h, val=val: e.wait_ge(h, val))

    def waits(self, eng, evs):
        for ev in evs:
            self.wait(eng, ev)

    def op(self, eng, fn, inc=None, dma=False, dep=True):
        if eng == "vector":
            if dep and self._vec_dirty:
                self.ops[eng].append(lambda e: e.drain())
            self._vec_dirty = True
        if inc is None:
            self.ops[eng].append(fn)
            return None
        n = 16 if dma else 1
        self.counts[inc] += n
        val = self.counts[inc]
        h = self.sems[inc]
        if not dma and eng in ("vector", "scalar"):
            self.ops[eng].append(fn)
            self.ops[eng].append(lambda e, h=h, n=n: e.drain().then_inc(h, n))
            if eng == "vector":
                self._vec_dirty = False
        else:
            self.ops[eng].append(lambda e, fn=fn, h=h, n=n: fn(e).then_inc(h, n))
        return (inc, val)


class Banks:
    def __init__(self, em):
        self.em = em
        self.w = {}
        self.r = {}

    def write(self, eng, bank):
        for ev in self.r.get(bank, {}).values():
            self.em.wait(eng, ev)
        self.em.wait(eng, self.w.get(bank))

    def wrote(self, bank, ev):
        self.w[bank] = ev
        self.r[bank] = {}

    def read(self, eng, bank):
        self.em.wait(eng, self.w.get(bank))

    def did_read(self, bank, eng, ev):
        self.r.setdefault(bank, {})[eng] = ev


def build_nc(steps=None, dbg=()):
    nc = bass.Bass("TRN2", target_bir_lowering=False)
    es = ExitStack()

    x_d = nc.dram_tensor("x", [NT, D], F32, kind="ExternalInput").ap()
    wqk_d = nc.dram_tensor("wqk", [32, 128, DC * 128], F32, kind="ExternalInput").ap()
    wv_d = nc.dram_tensor("wv", [4, 128, DC * 512], F32, kind="ExternalInput").ap()
    wp_d = nc.dram_tensor("wp", [4, 128, DC * 512], F32, kind="ExternalInput").ap()
    w1_d = nc.dram_tensor("w1", [16, 128, DC * 128], F32, kind="ExternalInput").ap()
    w3_d = nc.dram_tensor("w3", [16, 128, DC * 128], F32, kind="ExternalInput").ap()
    w2_d = nc.dram_tensor("w2", [4, 128, DC * 512], F32, kind="ExternalInput").ap()
    wr_d = nc.dram_tensor("wr", [128, DC * 8], F32, kind="ExternalInput").ap()
    pb_d = nc.dram_tensor("pb", [1, D], F32, kind="ExternalInput").ap()
    mm2_d = nc.dram_tensor("mm2", [128, 288], F32, kind="ExternalInput").ap()
    b01_d = nc.dram_tensor("b01", [128, 144], F32, kind="ExternalInput").ap()
    mms_d = nc.dram_tensor("mm_s0", [8, 16], F32, kind="ExternalInput").ap()
    ams_d = nc.dram_tensor("am_s0", [8, 16], F32, kind="ExternalInput").ap()
    mml_d = nc.dram_tensor("mm_last", [120, 128], F32, kind="ExternalInput").ap()
    aml_d = nc.dram_tensor("am_last", [120, 128], F32, kind="ExternalInput").ap()
    id_d = nc.dram_tensor("ident", [128, 128], F32, kind="ExternalInput").ap()
    cm_d = nc.dram_tensor("coremask", [8, 1], F32, kind="ExternalInput").ap()
    y_d = nc.dram_tensor("y", [NT, D], F32, kind="ExternalOutput").ap()
    cc_i = nc.dram_tensor("cc_i", [8, D], F32)
    cc_o = nc.dram_tensor("cc_o", [8, D], F32)
    dbg_d = {}

    sb = lambda name, shp, dt: es.enter_context(nc.sbuf_tensor(name, shp, dt))
    XNT = sb("XNT", [128, DC * CW], BF16)
    VB = sb("VB", [128, 9 * 2048], BF16)
    ATT = sb("ATT", [128, DC * 1024], BF16)
    SLV = sb("SLV", [128, DC * 8], BF16)
    QKB = sb("QKB", [128, 4 * CW], BF16)
    WA = sb("WA", [128, 2 * 2048], BF16)
    WB = sb("WB", [128, 2 * DC * 512], BF16)
    XCH = sb("XCH", [128, 2 * 2048], F32)
    XBK = sb("XBK", [128, 2 * 128], BF16)
    XBK2 = sb("XBK2", [128, 2 * 128], F32)
    F32R = sb("F32R", [128, 2 * 128], F32)
    XSL = sb("XSL", [128, 2 * 512], F32)
    XPS = sb("XPS", [128, 2 * 512], F32)
    T1R = sb("T1R", [128, 4 * 288], F32)
    PR = sb("PR", [128, 4 * 320], BF16)
    PS0R = sb("PS0R", [32, 32], BF16)
    PTSB = sb("PTSB", [128, 2 * 512], BF16)
    STT = sb("STT", [128, 32], F32)
    SST = sb("SST", [128, 32], F32)
    RPS = sb("RPS", [128, 40], F32)
    GT = sb("GT", [128, 8 * 8], BF16)
    MM2 = sb("MM2", [128, 288], BF16)
    B01 = sb("B01", [128, 144], BF16)
    MMS = sb("MMS", [8, 16], F32)
    AMS = sb("AMS", [8, 16], F32)
    MML = sb("MML", [120, 128], BF16)
    AML = sb("AML", [120, 128], BF16)
    IDB = sb("IDB", [128, 128], BF16)
    IDF = sb("IDF", [128, 128], F32)
    ONE = sb("ONE", [1, 128], BF16)
    PBB = sb("PBB", [1, D], BF16)
    WR = sb("WR", [128, DC * 8], F32)
    CM = sb("CM", [8, 1], F32)
    ZAP = sb("ZAP", [128, 1], F32)
    EPSAP = sb("EPSAP", [128, 1], F32)
    SQSL = sb("SQSL", [128, 512], F32)
    FIN = sb("FIN", [8, D], F32)

    ps = lambda name, shp, dt: es.enter_context(nc.psum_tensor(name, shp, dt))
    PM = [ps("PM0", [128, 512], F32), ps("PM1", [128, 512], F32)]
    PSW = ps("PSW", [128, 1024], F32)
    PV = [ps("PV0", [128, 512], F32), ps("PV1", [128, 512], F32)]
    PT = [ps("PT0", [128, 1024], BF16), ps("PT1", [128, 1024], BF16)]
    BK_PM, BK_PS, BK_PV, BK_PT = (0, 1), ("PSW", "PSW"), (4, 5), (6, 7)
    PS = [PSW, PSW]

    em = Em(nc, es)
    bk = Banks(em)

    ld_evs = []
    for dst, src in ((MMS, mms_d), (AMS, ams_d),
                     (IDF, id_d), (WR, wr_d),
                     (CM, cm_d)):
        ld_evs.append(em.op("sync", lambda e, d=dst, s=src: e.dma_start(
            out=d[:, :], in_=s[:, :]), inc="xd", dma=True))
    ld_evs.append(em.op("gpsimd", lambda e: e.dma_start(
        out=IDB[:, :], in_=id_d[:, :]), inc="wd", dma=True))
    ld_evs.append(em.op("gpsimd", lambda e: e.dma_start(
        out=PBB[:, :], in_=pb_d[:, :]), inc="wd", dma=True))
    ld_evs.append(em.op("gpsimd", lambda e: e.dma_start(
        out=MM2[:, :], in_=mm2_d[:, :]), inc="wd", dma=True))
    ld_evs.append(em.op("gpsimd", lambda e: e.dma_start(
        out=MML[:, :], in_=mml_d[:, :]), inc="wd", dma=True))
    ld_evs.append(em.op("gpsimd", lambda e: e.dma_start(
        out=AML[:, :], in_=aml_d[:, :]), inc="wd", dma=True))
    ld_evs.append(em.op("gpsimd", lambda e: e.dma_start(
        out=B01[:, :], in_=b01_d[:, :]), inc="wd", dma=True))
    em.waits("vector", ld_evs)
    em.op("vector", lambda e: e.memset(ZAP[:, :], 0.0))
    em.op("vector", lambda e: e.memset(EPSAP[:, :], EPS))
    for s_ in range(4):
        for u_ in range(2):
            em.op("vector", lambda e, o=s_ * 320 + u_ * 160: e.memset(
                PR[:, o + 144: o + 160], 0.0))
    em.op("vector", lambda e: e.memset(PS0R[:, :], 0.0))
    em.op("vector", lambda e: e.memset(VB[0:32, 8 * 2048:9 * 2048], 0.0))
    ev_memset = em.op("vector", lambda e: e.memset(ONE[:, :], 1.0), inc="v")
    em.waits("tensor", ld_evs + [ev_memset])
    em.waits("scalar", ld_evs + [ev_memset])

    state = {
        "wa": [None, None], "wb": [None, None],
        "wa_cons": [None, None], "wb_cons": [None, None],
        "xch_cons": [None, None],
        "qk_cons": [None, None, None, None],
        "xsl_cons": [None, None], "xps_cons": [None, None],
        "t1_cons": [None, None], "pr_cons": [None, None],
        "ptm_cons": [None, None], "pth_cons": [None, None],
        "xbk_cons": [None, None], "xbk2_cons": [None, None],
        "f32r_cons": [None, None],
        "y_written": {},
        "ln_ev": {"vector": None, "scalar": None},
        "ln1_chunk_evs": {},
        "ln2_chunk_evs": {},
        "v_done": {"vector": None, "scalar": None},
    }

    def load_wa(slot, src_ap):
        em.wait("gpsimd", state["wa_cons"][slot])
        ev = em.op("gpsimd", lambda e, s=src_ap, sl=slot: e.dma_start(
            out=WA[:, sl * 2048:(sl + 1) * 2048], in_=s,
            max_dma_last_dim=8192), inc="wd", dma=True)
        state["wa"][slot] = ev
        return ev

    def load_wb(slot, src_ap):
        em.wait("gpsimd", state["wb_cons"][slot])
        ev = em.op("gpsimd", lambda e, s=src_ap, sl=slot: e.dma_start(
            out=WB[:, sl * DC * 512:(sl + 1) * DC * 512], in_=s,
            max_dma_last_dim=8192), inc="wd", dma=True)
        state["wb"][slot] = ev
        return ev

    def ln_chunk(tile, i, nrow, rows_src, is_ln2):
        sl = i % 2
        em.wait("sync", state["xch_cons"][sl])
        if is_ln2:
            em.wait("sync", state["y_written"][rows_src // 128])
            src = y_d[rows_src:rows_src + nrow, :]
        else:
            src = x_d[rows_src:rows_src + nrow, :]
        ev_x = em.op("sync", lambda e, s=src, sl=sl, n=nrow: e.dma_start(
            out=XCH[0:n, sl * 2048: sl * 2048 + 2048], in_=s), inc="xd", dma=True)
        em.wait("vector", ev_x)
        em.op("vector", lambda e, sl=sl, n=nrow: e.reduce_sum(
            STT[0:n, 23:24], XCH[0:n, sl * 2048: sl * 2048 + 2048], axis=AX.X))
        em.op("vector", lambda e, n=nrow: e.tensor_scalar(
            out=STT[0:n, 24:25], in0=STT[0:n, 23:24], scalar1=1.0 / D,
            scalar2=None, op0=ALU.mult))
        for q in range(4):
            em.op("vector", lambda e, q=q, sl=sl, n=nrow: e.scalar_tensor_tensor(
                out=SQSL[0:n, 0:512],
                in0=XCH[0:n, sl * 2048 + q * 512: sl * 2048 + (q + 1) * 512],
                scalar=1.0,
                in1=XCH[0:n, sl * 2048 + q * 512: sl * 2048 + (q + 1) * 512],
                op0=ALU.mult, op1=ALU.mult))
            em.op("vector", lambda e, q=q, n=nrow: e.reduce_sum(
                STT[0:n, 16 + q:17 + q], SQSL[0:n, 0:512], axis=AX.X))
        em.op("vector", lambda e, n=nrow: e.tensor_tensor(
            out=STT[0:n, 20:21], in0=STT[0:n, 16:17], in1=STT[0:n, 17:18], op=ALU.add))
        em.op("vector", lambda e, n=nrow: e.tensor_tensor(
            out=STT[0:n, 21:22], in0=STT[0:n, 18:19], in1=STT[0:n, 19:20], op=ALU.add))
        em.op("vector", lambda e, n=nrow: e.tensor_tensor(
            out=STT[0:n, 22:23], in0=STT[0:n, 20:21], in1=STT[0:n, 21:22], op=ALU.add))
        em.op("vector", lambda e, n=nrow: e.tensor_scalar(
            out=STT[0:n, 26:27], in0=STT[0:n, 22:23], scalar1=1.0 / D,
            scalar2=None, op0=ALU.mult))
        em.op("vector", lambda e, n=nrow: e.tensor_tensor(
            out=STT[0:n, 18:19], in0=STT[0:n, 24:25], in1=STT[0:n, 24:25], op=ALU.mult))
        ev_mv = em.op("vector", lambda e, n=nrow: e.tensor_tensor(
            out=STT[0:n, 25:26], in0=STT[0:n, 26:27], in1=STT[0:n, 18:19],
            op=ALU.subtract), inc="v")
        em.wait("scalar", ev_mv)
        ev_sd = em.op("scalar", lambda e, n=nrow: e.activation(
            out=STT[0:n, 27:28], in_=STT[0:n, 25:26], func=ACTF.Sqrt,
            bias=EPSAP[0:n, 0:1], scale=1.0), inc="a")
        em.wait("vector", ev_sd)
        em.op("vector", lambda e, n=nrow: e.reciprocal(
            STT[0:n, 28:29], STT[0:n, 27:28]))
        ev_nb = em.op("vector", lambda e, n=nrow: e.tensor_scalar(
            out=STT[0:n, 29:30], in0=STT[0:n, 24:25], scalar1=STT[0:n, 28:29],
            scalar2=-1.0, op0=ALU.mult, op1=ALU.mult), inc="v")
        em.wait("scalar", ev_nb)

        t0 = i * 128
        ev_nm = None
        chunk_evs = []
        for c in range(DC):
            bs = c % 2
            dst = XBK2 if is_ln2 else XBK
            em.wait("scalar", state["xbk2_cons" if is_ln2 else "xbk_cons"][bs])
            ev_nm = em.op("scalar", lambda e, c=c, sl=sl, bs=bs, n=nrow, d=dst: e.activation(
                out=d[0:n, bs * 128:bs * 128 + 128],
                in_=XCH[0:n, sl * 2048 + c * 128: sl * 2048 + (c + 1) * 128],
                func=ACTF.Identity, bias=STT[0:n, 29:30], scale=STT[0:n, 28:29]),
                inc="a")
            if is_ln2:
                pmb = BK_PM[bs]
                bk.write("tensor", pmb)
                em.wait("tensor", ev_nm)
                ev_tr = em.op("tensor", lambda e, bs=bs, n=nrow: e.transpose(
                    PM[bs][0:128, 0:n], XBK2[0:n, bs * 128:bs * 128 + 128],
                    IDF[0:n, 0:n]), inc="t")
                bk.wrote(pmb, ev_tr)
                state["xbk2_cons"][bs] = ev_tr
                em.wait("vector", ev_tr)
                em.wait("vector", state["f32r_cons"][bs])
                ev_f = em.op("vector", lambda e, bs=bs, n=nrow: e.tensor_copy(
                    F32R[:, bs * 128:bs * 128 + n], PM[bs][0:128, 0:n]), inc="v")
                bk.did_read(pmb, "vector", ev_f)
                em.wait("scalar", ev_f)
                ev_cb = em.op("scalar", lambda e, bs=bs, c=c, t0=t0, n=nrow: e.activation(
                    out=XNT[:, c * CW + t0: c * CW + t0 + n],
                    in_=PM[bs][0:128, 0:n], func=ACTF.Copy, bias=0.0, scale=1.0),
                    inc="a")
                bk.did_read(pmb, "scalar", ev_cb)
                state["ln_ev"]["scalar"] = ev_cb
                bk.write("tensor", BK_PS[0])
                em.wait("tensor", ev_f)
                ev_rmm = em.op("tensor", lambda e, bs=bs, c=c, i=i, n=nrow: e.matmul(
                    PSW[0:n, 8 * i: 8 * i + 8],
                    lhsT=F32R[:, bs * 128:bs * 128 + n],
                    rhs=WR[:, 8 * c:8 * c + 8],
                    start=(c == 0), stop=(c == DC - 1)), inc="t")
                state["f32r_cons"][bs] = ev_rmm
                if c == DC - 1:
                    bk.wrote(BK_PS[0], ev_rmm)
            else:
                ptb = BK_PT[bs]
                bk.write("tensor", ptb)
                em.wait("tensor", ev_nm)
                ev_tr = em.op("tensor", lambda e, bs=bs, n=nrow: e.transpose(
                    PT[bs][0:128, 0:n], XBK[0:n, bs * 128:bs * 128 + 128],
                    IDB[0:n, 0:n]), inc="t")
                bk.wrote(ptb, ev_tr)
                state["xbk_cons"][bs] = ev_tr
                eng = "vector" if c % 2 == 0 else "scalar"
                em.wait(eng, ev_tr)
                if eng == "vector":
                    ev_e = em.op("vector", lambda e, bs=bs, c=c, t0=t0, n=nrow: e.tensor_copy(
                        XNT[:, c * CW + t0: c * CW + t0 + n], PT[bs][0:128, 0:n]),
                        inc="v")
                else:
                    ev_e = em.op("scalar", lambda e, bs=bs, c=c, t0=t0, n=nrow: e.activation(
                        out=XNT[:, c * CW + t0: c * CW + t0 + n],
                        in_=PT[bs][0:128, 0:n], func=ACTF.Copy, bias=0.0, scale=1.0),
                        inc="a")
                bk.did_read(ptb, eng, ev_e)
                state["ln_ev"][eng] = ev_e
                chunk_evs.append(ev_e)
        if is_ln2:
            state["ln2_chunk_evs"][i] = ev_cb
        else:
            state["ln1_chunk_evs"][i] = chunk_evs[-2:]
        state["xch_cons"][sl] = ev_nm

    def ln_phase(tile, is_ln2):
        base = tile * TILE
        ntok = 1040 if (tile == 0 and not is_ln2) else 1024
        nch = (ntok + 127) // 128
        for i in range(nch):
            nrow = min(128, ntok - i * 128)
            ln_chunk(tile, i, nrow, base + i * 128, is_ln2)
        if is_ln2:
            router_post(tile)

    def router_post(tile):
        RL = T1R
        bk.read("vector", BK_PS[0])
        em.wait("vector", state["t1_cons"][0])
        em.wait("vector", state["t1_cons"][1])
        ev_rl = em.op("vector", lambda e: e.tensor_copy(
            RL[:, 0:128], PSW[0:128, 0:128]), inc="v")
        bk.did_read(BK_PS[0], "vector", ev_rl)
        ev_g = None
        for i in range(8):
            em.op("vector", lambda e, i=i: e.reduce_max(
                RPS[:, 32:33], RL[:, 8 * i:8 * i + 8], axis=AX.X, negate=True))
            ev1 = em.op("vector", lambda e, i=i: e.tensor_scalar(
                out=RPS[:, 0:8], in0=RL[:, 8 * i:8 * i + 8], scalar1=RPS[:, 32:33],
                scalar2=None, op0=ALU.add), inc="v")
            em.wait("scalar", ev1)
            ev2 = em.op("scalar", lambda e: e.activation(
                out=RPS[:, 0:8], in_=RPS[:, 0:8], func=ACTF.Exp, bias=ZAP[:, 0:1],
                scale=1.0), inc="a")
            em.wait("vector", ev2)
            em.op("vector", lambda e: e.reduce_sum(
                RPS[:, 33:34], RPS[:, 0:8], axis=AX.X))
            em.op("vector", lambda e: e.reciprocal(RPS[:, 34:35], RPS[:, 33:34]))
            em.op("vector", lambda e: e.tensor_scalar(
                out=RPS[:, 0:8], in0=RPS[:, 0:8], scalar1=RPS[:, 34:35],
                scalar2=None, op0=ALU.mult))
            em.op("vector", lambda e: e.max(RPS[:, 8:16], RPS[:, 0:8]))
            em.op("vector", lambda e: e.tensor_scalar(
                out=RPS[:, 16:24], in0=RPS[:, 0:8], scalar1=RPS[:, 9:10],
                scalar2=None, op0=ALU.is_ge))
            em.op("vector", lambda e: e.tensor_tensor(
                out=RPS[:, 35:36], in0=RPS[:, 8:9], in1=RPS[:, 9:10], op=ALU.add))
            em.op("vector", lambda e: e.reciprocal(RPS[:, 36:37], RPS[:, 35:36]))
            ev_g = em.op("vector", lambda e, i=i: e.scalar_tensor_tensor(
                out=GT[:, 8 * i:8 * i + 8], in0=RPS[:, 16:24], scalar=RPS[:, 36:37],
                in1=RPS[:, 0:8], op0=ALU.mult, op1=ALU.mult), inc="v")
        state["gates_ev"] = ev_g
        state["rl_ev"] = ev_rl

    def v_phase(tile):
        ntok = 1040 if tile == 0 else 1024
        nj = 9 if tile == 0 else 8
        ev_e = None
        for n in range(4):
            sl = n % 2
            load_wb(sl, wv_d[n])
            ev_w = state["wb"][sl]
            for j in range(nj):
                nrow = min(128, ntok - j * 128)
                pm = (n * 9 + j) % 2
                pmb = BK_PM[pm]
                bk.write("tensor", pmb)
                em.wait("tensor", ev_w)
                em.waits("tensor", state["ln1_chunk_evs"][j])
                ev_mm = None
                for c in range(DC):
                    ev_mm = em.op("tensor", lambda e, c=c, j=j, sl=sl, pm=pm, nr=nrow: e.matmul(
                        PM[pm][0:nr, 0:512],
                        lhsT=XNT[:, c * CW + j * 128: c * CW + j * 128 + nr],
                        rhs=WB[:, sl * DC * 512 + c * 512: sl * DC * 512 + (c + 1) * 512],
                        start=(c == 0), stop=(c == DC - 1)),
                        inc=("t" if c == DC - 1 else None))
                bk.wrote(pmb, ev_mm)
                state["wb_cons"][sl] = ev_mm
                eng = "vector" if j % 2 == 0 else "scalar"
                em.wait(eng, ev_mm)
                if eng == "vector":
                    ev_e = em.op("vector", lambda e, j=j, n=n, pm=pm, nr=nrow: e.tensor_copy(
                        VB[0:nr, j * 2048 + n * 512: j * 2048 + (n + 1) * 512],
                        PM[pm][0:nr, 0:512]), inc="v")
                else:
                    ev_e = em.op("scalar", lambda e, j=j, n=n, pm=pm, nr=nrow: e.activation(
                        out=VB[0:nr, j * 2048 + n * 512: j * 2048 + (n + 1) * 512],
                        in_=PM[pm][0:nr, 0:512], func=ACTF.Copy, bias=0.0,
                        scale=1.0), inc="a")
                bk.did_read(pmb, eng, ev_e)
                state["v_done"][eng] = ev_e

    def units_for_tile(tile):
        if tile == 0:
            us = [("s0", 0, 8, 0, 16, 0)]
            us += [("main", 8 + 128 * j, 128, 128 * j, 144, j) for j in range(8)]
        else:
            us = [("main", 8 + 128 * j, 128, 128 * j, 144, j) for j in range(7)]
            us.append(("last", 904, 120, 896, 128, 7))
        return us

    def qk_group(tile, pair, kind, off, n, last):
        sl = pair % 2
        ring = kind * 2 + sl
        blk = 2 * pair + kind
        wsl = blk % 2
        if off == 0:
            load_wa(wsl, wqk_d[blk])
        ev_w = state["wa"][wsl]
        state["qk_pm"] = (state.get("qk_pm", 0) + 1) % 2
        pm = state["qk_pm"]
        pmb = BK_PM[pm]
        bk.write("tensor", pmb)
        em.wait("tensor", ev_w)
        ev_mm = None
        for c in range(DC):
            ev_mm = em.op("tensor", lambda e, c=c, off=off, n=n, wsl=wsl, pm=pm: e.matmul(
                PM[pm][0:128, 0:n],
                lhsT=WA[:, wsl * 2048 + c * 128: wsl * 2048 + (c + 1) * 128],
                rhs=XNT[:, c * CW + off: c * CW + off + n],
                start=(c == 0), stop=(c == DC - 1)),
                inc=("t" if c == DC - 1 else None))
        bk.wrote(pmb, ev_mm)
        state["wa_cons"][wsl] = ev_mm
        eng = "vector" if kind == 0 else "scalar"
        em.wait(eng, ev_mm)
        em.wait(eng, state["qk_cons"][ring])
        if eng == "vector":
            ev_e = em.op("vector", lambda e, ring=ring, off=off, n=n, pm=pm: e.tensor_copy(
                QKB[:, ring * CW + off: ring * CW + off + n], PM[pm][0:128, 0:n]),
                inc="v")
        else:
            ev_e = em.op("scalar", lambda e, ring=ring, off=off, n=n, pm=pm: e.activation(
                out=QKB[:, ring * CW + off: ring * CW + off + n],
                in_=PM[pm][0:128, 0:n], func=ACTF.Copy, bias=0.0, scale=1.0),
                inc="a")
        bk.did_read(pmb, eng, ev_e)
        if last:
            state.setdefault("qk_ready", {})[(pair, kind)] = ev_e

    def head_bundles(tile, head):
        import os as _os3
        solo_last = int(_os3.environ.get("K_SOLO_LAST", "0"))
        us = units_for_tile(tile)
        bs, i = [], 0
        while i < len(us):
            can_pair = (us[i][0] == "main" and i + 1 < len(us)
                        and us[i + 1][0] == "main")
            if solo_last and tile == 0 and us[i][5] == 6 and us[i][0] == "main":
                can_pair = False
            if can_pair:
                bs.append((head, [us[i], us[i + 1]]))
                i += 2
            else:
                bs.append((head, [us[i]]))
                i += 1
        return bs

    def qk_attn_phase(tile):
        ncols = 1040 if tile == 0 else 1024
        nsl = [(0, 512), (512, 512), (1024, 16)] if ncols == 1040 else [(0, 512), (512, 512)]
        if tile == 1:
            for c in range(DC):
                ev_sv = em.op("vector", lambda e, c=c: e.tensor_copy(
                    ATT[:, c * 1024: c * 1024 + 8], SLV[:, c * 8: c * 8 + 8]),
                    inc="v", dep=(c == 0))
            state["sliver_ev"] = ev_sv
        for kind in (0, 1):
            for gi, (off, n) in enumerate(nsl):
                qk_group(tile, 0, kind, off, n, gi == len(nsl) - 1)

        bundles = []
        for pair in range(16):
            for hh in range(2):
                bundles += head_bundles(tile, 2 * pair + hh)
        import os as _os
        per_pair = len(bundles) // 16
        _lim = int(_os.environ.get("K_DBG_NBUND", "0"))
        if _lim:
            bundles = bundles[:_lim]
        NB = len(bundles)
        qkq = []
        maxpair = (NB - 1) // per_pair if _lim else 15
        import os as _os2
        _noil = int(_os2.environ.get("K_NO_QK_INTERLEAVE", "0"))
        for pair in range(min(15, maxpair)):
            ngr = 2 * len(nsl)
            for g in range(ngr):
                pos = pair * per_pair + (0 if _noil else (g * per_pair) // ngr)
                qkq.append((pos, pair + 1, g))
        ctx = [None] * NB
        incA = {}
        incB = {}
        incC = {}

        def mkctx(bi):
            head, us = bundles[bi]
            c = dict(head=head, units=us, s=bi % 2, s4=bi % 4, par=head % 2,
                     pair=head // 2, dvc=head * 64, nu=len(us))
            c["p0"] = 64 * c["par"]
            return c

        def S0(bi, evA):
            c = ctx[bi]
            em.wait("tensor", evA)
            em.waits("tensor", [state["qk_ready"][(c["pair"], 0)],
                                state["qk_ready"][(c["pair"], 1)]])
            bk.write("tensor", "PSW")
            rsl = c["pair"] % 2
            p0 = c["p0"]
            ev = None
            for ui, (kind, q0, nq, k0, nk, vj) in enumerate(c["units"]):
                qr = QKB[p0:p0 + 64, rsl * CW + q0: rsl * CW + q0 + nq]
                kr = QKB[p0:p0 + 64, (2 + rsl) * CW + k0: (2 + rsl) * CW + k0 + nk]
                ev = em.op("tensor", lambda e, qr=qr, kr=kr, ui=ui, nq=nq, nk=nk: e.matmul(
                    PSW[0:nq, ui * 512: ui * 512 + nk], lhsT=qr, rhs=kr,
                    start=True, stop=True),
                    inc=("t" if ui == c["nu"] - 1 else None))
            c["ev_sc"] = ev
            bk.wrote("PSW", ev)
            state["qk_cons"][rsl] = ev
            state["qk_cons"][2 + rsl] = ev

        def S1(bi):
            c = ctx[bi]
            s = c["s"]
            s4 = c["s4"]
            em.wait("vector", c["ev_sc"])
            em.wait("vector", incC.get(bi - 1))
            if c["nu"] == 2:
                pv = PSW[0:128, 0:1024].rearrange("p (u x) -> p u x", u=2)[:, :, 0:144]
                em.op("vector", lambda e, pv=pv, s=s: e.scalar_tensor_tensor(
                    out=T1R[:, s4 * 288: s4 * 288 + 288], in0=pv, scalar=1.0,
                    in1=MM2[:, 0:288], op0=ALU.mult, op1=ALU.mult), dep=False)
            else:
                kind, q0, nq, k0, nk, vj = c["units"][0]
                mm = {"main": MM2, "s0": MMS, "last": MML}[kind]
                em.op("vector", lambda e, s=s, s4=s4, nq=nq, nk=nk, mm=mm: e.scalar_tensor_tensor(
                    out=T1R[0:nq, s4 * 288: s4 * 288 + nk], in0=PSW[0:nq, 0:nk],
                    scalar=1.0, in1=mm[0:nq, 0:nk], op0=ALU.mult, op1=ALU.mult),
                    dep=False)
            bk.did_read("PSW", "vector", None)

        def S1tt(bi):
            c = ctx[bi]
            s = c["s"]
            s4 = c["s4"]
            kind, q0, nq, k0, nk, vj = c["units"][0]
            am = {"s0": AMS, "last": AML}[kind]
            em.op("vector", lambda e, s=s, s4=s4, nq=nq, nk=nk, am=am: e.tensor_tensor(
                out=T1R[0:nq, s4 * 288: s4 * 288 + nk],
                in0=T1R[0:nq, s4 * 288: s4 * 288 + nk],
                in1=am[0:nq, 0:nk], op=ALU.add), dep=False)

        def S1rmax(bi):
            c = ctx[bi]
            s = c["s"]
            s4 = c["s4"]
            if c["nu"] == 2:
                tv = T1R[:, s4 * 288:(s4 + 1) * 288].rearrange("p (u x) -> p u x", u=2)
                em.op("vector", lambda e, tv=tv, s=s: e.reduce_max(
                    SST[:, s4 * 8: s4 * 8 + 2], tv, axis=AX.X, negate=True), dep=False)
            else:
                kind, q0, nq, k0, nk, vj = c["units"][0]
                em.op("vector", lambda e, s=s, s4=s4, nq=nq, nk=nk: e.reduce_max(
                    SST[0:nq, s4 * 8: s4 * 8 + 1], T1R[0:nq, s4 * 288: s4 * 288 + nk],
                    axis=AX.X, negate=True), dep=False)

        def S2sum(bi):
            c = ctx[bi]
            s = c["s"]
            s4 = c["s4"]
            em.wait("vector", incC.get(bi + 1))
            if c["nu"] == 2:
                tv = T1R[:, s4 * 288:(s4 + 1) * 288].rearrange("p (u x) -> p u x", u=2)
                em.op("vector", lambda e, tv=tv, s=s: e.reduce_sum(
                    SST[:, s4 * 8 + 2: s4 * 8 + 4], tv, axis=AX.X), dep=False)
            else:
                kind, q0, nq, k0, nk, vj = c["units"][0]
                em.op("vector", lambda e, s=s, s4=s4, nq=nq, nk=nk: e.reduce_sum(
                    SST[0:nq, s4 * 8 + 2: s4 * 8 + 3], T1R[0:nq, s4 * 288: s4 * 288 + nk],
                    axis=AX.X), dep=False)

        def S2corr(bi):
            c = ctx[bi]
            s = c["s"]
            s4 = c["s4"]
            for ui, (kind, q0, nq, k0, nk, vj) in enumerate(c["units"]):
                em.op("vector", lambda e, s=s, s4=s4, ui=ui: e.scalar_tensor_tensor(
                    out=SST[:, s4 * 8 + 4 + ui: s4 * 8 + 5 + ui],
                    in0=T1R[:, s4 * 288 + ui * 144 + 143: s4 * 288 + ui * 144 + 144],
                    scalar=-128.0, in1=SST[:, s4 * 8 + 2 + ui: s4 * 8 + 3 + ui],
                    op0=ALU.mult, op1=ALU.add), dep=False)

        def S2recip(bi):
            c = ctx[bi]
            s = c["s"]
            s4 = c["s4"]
            main = c["units"][0][0] == "main"
            base = s4 * 8 + (4 if main else 2)
            nq = 128 if c["nu"] == 2 else c["units"][0][2]
            em.op("vector", lambda e, s=s, s4=s4, base=base, nu=c["nu"], nq=nq: e.reciprocal(
                SST[0:nq, s4 * 8 + 6: s4 * 8 + 6 + nu], SST[0:nq, base: base + nu]),
                dep=False)

        def S2pnorm(bi, need_drain):
            c = ctx[bi]
            s = c["s"]
            s4 = c["s4"]
            em.wait("vector", (ctx[bi - 2].get("ev_tr") if bi >= 2 else None))
            if c["units"][0][0] == "s0":
                em.wait("vector", state.get("s0_tr_ev"))
            for ui, (kind, q0, nq, k0, nk, vj) in enumerate(c["units"]):
                if kind == "main":
                    em.op("vector", lambda e, s4=s4, ui=ui, nq=nq: e.scalar_tensor_tensor(
                        out=PR[0:nq, s4 * 320 + ui * 160: s4 * 320 + ui * 160 + 144],
                        in0=T1R[0:nq, s4 * 288 + ui * 144: s4 * 288 + ui * 144 + 144],
                        scalar=SST[0:nq, s4 * 8 + 6 + ui: s4 * 8 + 7 + ui],
                        in1=B01[0:nq, 0:144], op0=ALU.mult, op1=ALU.mult),
                        dep=(need_drain and ui == 0))
                elif kind == "s0":
                    em.op("vector", lambda e, s=s, s4=s4: e.tensor_scalar(
                        out=PS0R[0:8, 0:16], in0=T1R[0:8, s4 * 288: s4 * 288 + 16],
                        scalar1=SST[0:8, s4 * 8 + 6: s4 * 8 + 7], scalar2=None,
                        op0=ALU.mult), dep=need_drain)
                else:
                    em.op("vector", lambda e, s=s, s4=s4, nq=nq, nk=nk: e.tensor_scalar(
                        out=PR[0:nq, s4 * 320: s4 * 320 + nk],
                        in0=T1R[0:nq, s4 * 288: s4 * 288 + nk],
                        scalar1=SST[0:nq, s4 * 8 + 6: s4 * 8 + 7], scalar2=None,
                        op0=ALU.mult), dep=need_drain)

        def Sexp(bi, ev):
            c = ctx[bi]
            s = c["s"]
            s4 = c["s4"]
            em.wait("scalar", ev)
            for ui, (kind, q0, nq, k0, nk, vj) in enumerate(c["units"]):
                em.op("scalar", lambda e, s=s, s4=s4, ui=ui, nq=nq, nk=nk: e.activation(
                    out=T1R[0:nq, s4 * 288 + ui * 144: s4 * 288 + ui * 144 + nk],
                    in_=T1R[0:nq, s4 * 288 + ui * 144: s4 * 288 + ui * 144 + nk],
                    func=ACTF.Exp, bias=SST[0:nq, s4 * 8 + ui: s4 * 8 + ui + 1],
                    scale=1.0))

        def S3tr(bi, ev):
            c = ctx[bi]
            s = c["s"]
            s4 = c["s4"]
            ptb = BK_PT[s]
            em.wait("tensor", ev)
            bk.write("tensor", ptb)
            evt = None
            for ui, (kind, q0, nq, k0, nk, vj) in enumerate(c["units"]):
                if kind == "s0":
                    evt = em.op("tensor", lambda e, s=s, s4=s4: e.transpose(
                        PT[s][0:32, 0:32], PS0R[0:32, 0:32], IDB[0:32, 0:32]),
                        inc="t")
                    state["s0_tr_ev"] = evt
                elif kind == "last":
                    evt = em.op("tensor", lambda e, s=s, s4=s4: e.transpose(
                        PT[s][0:128, 0:128], PR[:, s4 * 320: s4 * 320 + 128],
                        IDB[:, :]), inc="t")
                else:
                    em.op("tensor", lambda e, s=s, s4=s4, ui=ui: e.transpose(
                        PT[s][0:128, ui * 256: ui * 256 + 128],
                        PR[:, s4 * 320 + ui * 160: s4 * 320 + ui * 160 + 128],
                        IDB[:, :]))
                    evt = em.op("tensor", lambda e, s=s, s4=s4, ui=ui: e.transpose(
                        PT[s][0:32, ui * 256 + 128: ui * 256 + 256],
                        PR[:, s4 * 320 + ui * 160 + 128: s4 * 320 + ui * 160 + 160],
                        IDB[:, :]), inc="t")
            c["ev_tr"] = evt
            bk.wrote(ptb, evt)

        def S3ev(bi):
            c = ctx[bi]
            s = c["s"]
            s4 = c["s4"]
            em.wait("vector", c["ev_tr"])
            em.wait("vector", (ctx[bi - 2]["ev_pv"] if bi >= 2 else None))
            kind = c["units"][0][0]
            if kind == "s0":
                em.op("vector", lambda e, s=s, s4=s4: e.tensor_copy(
                    PTSB[0:32, s * 512: s * 512 + 32], PT[s][0:32, 0:32]), dep=False)
            elif kind == "last":
                em.op("vector", lambda e, s=s, s4=s4: e.tensor_copy(
                    PTSB[0:128, s * 512: s * 512 + 128], PT[s][0:128, 0:128]),
                    dep=False)
            else:
                w = 256 * c["nu"]
                em.op("vector", lambda e, s=s, s4=s4, w=w: e.tensor_copy(
                    PTSB[0:128, s * 512: s * 512 + w], PT[s][0:128, 0:w]), dep=False)
            bk.did_read(BK_PT[s], "vector", None)

        def S3pv(bi, evB, evC):
            c = ctx[bi]
            s = c["s"]
            s4 = c["s4"]
            pvb = BK_PV[s]
            em.waits("tensor", [evB, evC,
                                state["v_done"]["vector"], state["v_done"]["scalar"]])
            bk.write("tensor", pvb)
            p0 = c["p0"]
            dvc = c["dvc"]
            tp = (0, p0)
            ev = None
            for ui, (kind, q0, nq, k0, nk, vj) in enumerate(c["units"]):
                if kind == "s0":
                    ev = em.op("tensor", lambda e, s=s, s4=s4, p0=p0, dvc=dvc, tp=tp: e.matmul(
                        PV[s][p0:p0 + 64, 0:8],
                        lhsT=VB[0:32, dvc: dvc + 64],
                        rhs=PTSB[0:32, s * 512: s * 512 + 8],
                        start=True, stop=True, tile_position=tp), inc="t")
                elif kind == "last":
                    ev = em.op("tensor", lambda e, s=s, s4=s4, p0=p0, dvc=dvc, vj=vj, nq=nq, tp=tp: e.matmul(
                        PV[s][p0:p0 + 64, 0:nq],
                        lhsT=VB[:, vj * 2048 + dvc: vj * 2048 + dvc + 64],
                        rhs=PTSB[0:128, s * 512: s * 512 + nq],
                        start=True, stop=True, tile_position=tp), inc="t")
                else:
                    em.op("tensor", lambda e, s=s, s4=s4, ui=ui, p0=p0, dvc=dvc, vj=vj, tp=tp: e.matmul(
                        PV[s][p0:p0 + 64, ui * 128: ui * 128 + 128],
                        lhsT=VB[:, vj * 2048 + dvc: vj * 2048 + dvc + 64],
                        rhs=PTSB[0:128, s * 512 + ui * 256: s * 512 + ui * 256 + 128],
                        start=True, stop=False, tile_position=tp))
                    ev = em.op("tensor", lambda e, s=s, s4=s4, ui=ui, p0=p0, dvc=dvc, vj=vj, tp=tp: e.matmul(
                        PV[s][p0:p0 + 64, ui * 128: ui * 128 + 128],
                        lhsT=VB[0:32, (vj + 1) * 2048 + dvc: (vj + 1) * 2048 + dvc + 64],
                        rhs=PTSB[0:32, s * 512 + ui * 256 + 128: s * 512 + ui * 256 + 256],
                        start=False, stop=True, tile_position=tp),
                        inc=("t" if ui == c["nu"] - 1 else None))
            c["ev_pv"] = ev
            bk.wrote(pvb, ev)

        def S4(bi):
            c = ctx[bi]
            s = c["s"]
            s4 = c["s4"]
            em.wait("scalar", c["ev_pv"])
            p0 = c["p0"]
            dv0 = p0
            ch = c["head"] // 2
            us = c["units"]
            if c["nu"] == 2:
                q0 = us[0][1]
                if tile == 0 and us[1][5] == 7:
                    em.op("scalar", lambda e, s=s, s4=s4, p0=p0, dv0=dv0, ch=ch, q0=q0: e.activation(
                        out=ATT[dv0:dv0 + 64, ch * 1024 + q0: ch * 1024 + 1024],
                        in_=PV[s][p0:p0 + 64, 0:248], func=ACTF.Copy, bias=0.0,
                        scale=1.0))
                    em.op("scalar", lambda e, s=s, s4=s4, p0=p0, dv0=dv0, ch=ch: e.activation(
                        out=SLV[dv0:dv0 + 64, ch * 8: ch * 8 + 8],
                        in_=PV[s][p0:p0 + 64, 248:256], func=ACTF.Copy, bias=0.0,
                        scale=1.0))
                else:
                    em.op("scalar", lambda e, s=s, s4=s4, p0=p0, dv0=dv0, ch=ch, q0=q0: e.activation(
                        out=ATT[dv0:dv0 + 64, ch * 1024 + q0: ch * 1024 + q0 + 256],
                        in_=PV[s][p0:p0 + 64, 0:256], func=ACTF.Copy, bias=0.0,
                        scale=1.0))
            else:
                kind, q0, nq, k0, nk, vj = us[0]
                if tile == 0 and kind == "main" and vj == 7:
                    em.op("scalar", lambda e, s=s, s4=s4, p0=p0, dv0=dv0, ch=ch, q0=q0: e.activation(
                        out=ATT[dv0:dv0 + 64, ch * 1024 + q0: ch * 1024 + 1024],
                        in_=PV[s][p0:p0 + 64, 0:120], func=ACTF.Copy, bias=0.0,
                        scale=1.0))
                    em.op("scalar", lambda e, s=s, s4=s4, p0=p0, dv0=dv0, ch=ch: e.activation(
                        out=SLV[dv0:dv0 + 64, ch * 8: ch * 8 + 8],
                        in_=PV[s][p0:p0 + 64, 120:128], func=ACTF.Copy, bias=0.0,
                        scale=1.0))
                else:
                    em.op("scalar", lambda e, s=s, s4=s4, p0=p0, dv0=dv0, ch=ch, q0=q0, nq=nq: e.activation(
                        out=ATT[dv0:dv0 + 64, ch * 1024 + q0: ch * 1024 + q0 + nq],
                        in_=PV[s][p0:p0 + 64, 0:nq], func=ACTF.Copy, bias=0.0,
                        scale=1.0))
            bk.did_read(BK_PV[s], "scalar", None)

        qk_i = 0
        for i in range(NB + 6):
            if 1 <= i <= NB:
                ctx_prev = ctx[i - 1]
            if 1 <= i <= NB and i < 2:
                pass
            if 1 <= i <= NB:
                S1(i - 1)
            if 2 <= i <= NB + 1:
                S2sum(i - 2)
            if 1 <= i <= NB and ctx[i - 1]["units"][0][0] in ("s0", "last"):
                if i < 2:
                    em.op("vector", lambda e: e.drain())
                S1tt(i - 1)
                if i < 3:
                    em.op("vector", lambda e: e.drain())
                S1rmax(i - 1)
            elif 1 <= i <= NB:
                if i < 2:
                    em.op("vector", lambda e: e.drain())
                S1rmax(i - 1)
            if 3 <= i <= NB + 2 and ctx[i - 3]["units"][0][0] == "main":
                S2corr(i - 3)
            evA = em.fence("vector", "v") if i <= NB + 2 else None
            incA[i] = evA
            ops_after_recip = 0
            if 3 <= i <= NB + 2:
                if evA is None:
                    em.op("vector", lambda e: e.drain())
                S2recip(i - 3)
            if 4 <= i <= NB + 3:
                S3ev(i - 4)
                ops_after_recip = 1
            if 3 <= i <= NB + 2:
                S2pnorm(i - 3, need_drain=(ops_after_recip == 0))
            evB = em.fence("vector", "v") if i >= 3 else None
            incB[i] = evB
            if 1 <= i <= NB:
                Sexp(i - 1, evA)
            if 5 <= i <= NB + 4:
                S4(i - 5)
            evC = em.fence("scalar", "a") if i >= 1 else None
            incC[i] = evC
            if i < NB:
                ctx[i] = mkctx(i)
                S0(i, evA)
            if 3 <= i <= NB + 2:
                S3tr(i - 3, evB)
            if 4 <= i <= NB + 3:
                S3pv(i - 4, evB, evC)
            while qk_i < len(qkq) and qkq[qk_i][0] <= i:
                _, pnext, gidx = qkq[qk_i]
                kind = gidx // len(nsl)
                off, n = nsl[gidx % len(nsl)]
                qk_group(tile, pnext, kind, off, n, gidx % len(nsl) == len(nsl) - 1)
                qk_i += 1
        state["attn_last_ev"] = incC[NB + 5]

    def proj_phase(tile):
        base = tile * TILE
        em.wait("tensor", state.get("attn_last_ev"))
        if tile == 1:
            em.wait("tensor", state.get("sliver_ev"))
        for n in range(4):
            sl = n % 2
            load_wb(sl, wp_d[n])
            ev_w = state["wb"][sl]
            for t in range(8):
                pm = (n * 8 + t) % 2
                pmb = BK_PM[pm]
                bk.write("tensor", pmb)
                em.wait("tensor", ev_w)
                for c in range(DC):
                    em.op("tensor", lambda e, c=c, t=t, sl=sl, pm=pm: e.matmul(
                        PM[pm][0:128, 0:512],
                        lhsT=ATT[:, c * 1024 + t * 128: c * 1024 + (t + 1) * 128],
                        rhs=WB[:, sl * DC * 512 + c * 512: sl * DC * 512 + (c + 1) * 512],
                        start=(c == 0), stop=False))
                ev_mm = em.op("tensor", lambda e, n=n, pm=pm: e.matmul(
                    PM[pm][0:128, 0:512], lhsT=ONE[0:1, 0:128],
                    rhs=PBB[0:1, n * 512:(n + 1) * 512],
                    start=False, stop=True), inc="t")
                bk.wrote(pmb, ev_mm)
                state["wb_cons"][sl] = ev_mm
                xs = t % 2
                em.wait("sync", state["xsl_cons"][xs])
                ev_x = em.op("sync", lambda e, t=t, n=n, xs=xs, base=base: e.dma_start(
                    out=XSL[:, xs * 512:(xs + 1) * 512],
                    in_=x_d[base + t * 128: base + (t + 1) * 128, n * 512:(n + 1) * 512]),
                    inc="xd", dma=True)
                em.waits("vector", [ev_mm, ev_x])
                em.wait("vector", state["xps_cons"][xs])
                ev_xp = em.op("vector", lambda e, xs=xs, pm=pm: e.tensor_tensor(
                    out=XPS[:, xs * 512:(xs + 1) * 512],
                    in0=PM[pm][0:128, 0:512], in1=XSL[:, xs * 512:(xs + 1) * 512],
                    op=ALU.add), inc="v")
                bk.did_read(pmb, "vector", ev_xp)
                state["xsl_cons"][xs] = ev_xp
                em.wait("scalar", ev_xp)
                ev_o = em.op("scalar", lambda e, t=t, n=n, xs=xs, base=base: e.dma_start(
                    out=y_d[base + t * 128: base + (t + 1) * 128, n * 512:(n + 1) * 512],
                    in_=XPS[:, xs * 512:(xs + 1) * 512]), inc="od", dma=True)
                state["xps_cons"][xs] = ev_o
                if n == 3:
                    state["y_written"][tile * 8 + t] = ev_o

    def ffn_mlp(tile, which):
        src, scw = (XNT, CW) if which == 0 else (ATT, 1024)
        dst = ATT if which == 0 else VB
        w_dd = w1_d if which == 0 else w3_d

        ev_e = None
        for m in range(16):
            wsl = m % 2
            load_wa(wsl, w_dd[m])
            ev_w = state["wa"][wsl]
            for n in range(2):
                pm = (m * 2 + n) % 2
                pmb = BK_PM[pm]
                bk.write("tensor", pmb)
                em.wait("tensor", ev_w)
                if which == 0:
                    em.wait("tensor", state["ln2_chunk_evs"][4 * n + 3])
                ev_mm = None
                for c in range(DC):
                    ev_mm = em.op("tensor", lambda e, c=c, n=n, wsl=wsl, pm=pm, src=src, scw=scw: e.matmul(
                        PM[pm][0:128, 0:512],
                        lhsT=WA[:, wsl * 2048 + c * 128: wsl * 2048 + (c + 1) * 128],
                        rhs=src[:, c * scw + n * 512: c * scw + n * 512 + 512],
                        start=(c == 0), stop=(c == DC - 1)),
                        inc=("t" if c == DC - 1 else None))
                bk.wrote(pmb, ev_mm)
                state["wa_cons"][wsl] = ev_mm
                em.wait("scalar", ev_mm)
                ev_e = em.op("scalar", lambda e, m=m, n=n, pm=pm, dst=dst: e.activation(
                    out=dst[:, m * 1024 + n * 512: m * 1024 + (n + 1) * 512],
                    in_=PM[pm][0:128, 0:512], func=ACTF.Silu, bias=ZAP[0:128, 0:1], scale=1.0),
                    inc="a")
                bk.did_read(pmb, "scalar", ev_e)
        state["mlp_ev"] = ev_e

    def ffn_out(tile):
        sc_ps = [None, None, PV[0], PV[1]]
        sc_bk = [BK_PS[0], BK_PS[1], BK_PV[0], BK_PV[1]]
        ev_h3_gate = state.get("h1_mm_last")
        ev_acc = None
        for n in range(4):
            sl = n % 2
            load_wb(sl, w2_d[n])
            ev_w = state["wb"][sl]
            bk.write("tensor", sc_bk[n])
            ev_s = None
            for t in range(8):
                pm = (n * 8 + t) % 2
                pmb = BK_PM[pm]
                bk.write("tensor", pmb)
                em.wait("tensor", ev_w)
                ev_mm = None
                for c in range(DC):
                    ev_mm = em.op("tensor", lambda e, c=c, t=t, sl=sl, pm=pm: e.matmul(
                        PM[pm][0:128, 0:512],
                        lhsT=VB[:, c * 1024 + t * 128: c * 1024 + (t + 1) * 128],
                        rhs=WB[:, sl * DC * 512 + c * 512: sl * DC * 512 + (c + 1) * 512],
                        start=(c == 0), stop=(c == DC - 1)),
                        inc=("t" if c == DC - 1 else None))
                bk.wrote(pmb, ev_mm)
                state["wb_cons"][sl] = ev_mm
                em.wait("scalar", ev_mm)
                em.wait("scalar", ev_h3_gate)
                ev_h3 = em.op("scalar", lambda e, t=t, n=n, pm=pm: e.activation(
                    out=XNT[:, t * 2048 + n * 512: t * 2048 + (n + 1) * 512],
                    in_=PM[pm][0:128, 0:512], func=ACTF.Copy, bias=0.0, scale=1.0),
                    inc="a")
                bk.did_read(pmb, "scalar", ev_h3)
                em.waits("tensor", [ev_h3, state.get("gates_ev")])
                ev_s = em.op("tensor", lambda e, t=t, n=n: e.matmul(
                    (PSW[0:8, 512 * n:512 * (n + 1)] if n < 2 else sc_ps[n][0:8, 0:512]),
                    lhsT=GT[:, t * 8:(t + 1) * 8],
                    rhs=XNT[:, t * 2048 + n * 512: t * 2048 + (n + 1) * 512],
                    start=(t == 0), stop=(t == 7)), inc="t")
            bk.wrote(sc_bk[n], ev_s)
            em.wait("vector", ev_s)
            src_ps = (lambda n=n: PSW[0:8, 512 * n:512 * (n + 1)] if n < 2 else sc_ps[n][0:8, 0:512])()
            if tile == 0:
                ev_acc = em.op("vector", lambda e, n=n, sp=src_ps: e.tensor_copy(
                    FIN[0:8, n * 512:(n + 1) * 512], sp), inc="v")
            else:
                ev_acc = em.op("vector", lambda e, n=n, sp=src_ps: e.tensor_tensor(
                    out=FIN[0:8, n * 512:(n + 1) * 512],
                    in0=FIN[0:8, n * 512:(n + 1) * 512],
                    in1=sp, op=ALU.add), inc="v")
            bk.did_read(sc_bk[n], "vector", ev_acc)
        state["acc_ev"] = ev_acc

    def finale():
        em.wait("gpsimd", state["acc_ev"])
        ev1 = em.op("gpsimd", lambda e: e.dma_start(
            out=cc_i[:, :], in_=FIN[0:8, 0:D]), inc="gd", dma=True)
        em.wait("gpsimd", ev1)
        ev2 = em.op("gpsimd", lambda e: e.collective_compute(
            "AllReduce", ALU.add, replica_groups=[list(range(8))],
            ins=[cc_i[:, :]], outs=[cc_o[:, :]]), inc="g")
        em.wait("gpsimd", ev2)
        em.waits("gpsimd", [state["xsl_cons"][0], state["xsl_cons"][1],
                            state["xps_cons"][0], state["xps_cons"][1]])
        ev3a = em.op("gpsimd", lambda e: e.dma_start(
            out=XSL[0:8, 0:1024], in_=cc_o[:, 0:1024]), inc="gd", dma=True)
        ev3b = em.op("gpsimd", lambda e: e.dma_start(
            out=XPS[0:8, 0:1024], in_=cc_o[:, 1024:2048]), inc="gd", dma=True)
        em.wait("sync", ev1)
        em.wait("sync", state["y_written"][0])
        ev4 = em.op("sync", lambda e: e.dma_start(
            out=FIN[0:8, 0:D], in_=y_d[0:8, :]), inc="xd", dma=True)
        em.waits("vector", [ev3a, ev3b, ev4])
        em.op("vector", lambda e: e.tensor_scalar(
            out=XSL[0:8, 0:1024], in0=XSL[0:8, 0:1024], scalar1=CM[0:8, 0:1],
            scalar2=None, op0=ALU.mult))
        em.op("vector", lambda e: e.tensor_scalar(
            out=XPS[0:8, 0:1024], in0=XPS[0:8, 0:1024], scalar1=CM[0:8, 0:1],
            scalar2=None, op0=ALU.mult))
        em.op("vector", lambda e: e.tensor_tensor(
            out=XSL[0:8, 0:1024], in0=XSL[0:8, 0:1024], in1=FIN[0:8, 0:1024],
            op=ALU.add))
        ev5 = em.op("vector", lambda e: e.tensor_tensor(
            out=XPS[0:8, 0:1024], in0=XPS[0:8, 0:1024], in1=FIN[0:8, 1024:2048],
            op=ALU.add), inc="v")
        em.wait("scalar", ev5)
        em.op("scalar", lambda e: e.dma_start(
            out=y_d[0:8, 0:1024], in_=XSL[0:8, 0:1024]), inc="od", dma=True)
        em.op("scalar", lambda e: e.dma_start(
            out=y_d[0:8, 1024:2048], in_=XPS[0:8, 0:1024]), inc="od", dma=True)
        em.op("scalar", lambda e, v=em.counts["od"]: e.wait_ge(em.sems["od"], v))

    def emit_all(nsteps):
        cnt = [0]
        def step(f):
            if nsteps is None or cnt[0] < nsteps:
                f()
            cnt[0] += 1
        for tile in (0, 1):
            step(lambda tile=tile: ln_phase(tile, is_ln2=False))
            step(lambda tile=tile: v_phase(tile))
            step(lambda tile=tile: qk_attn_phase(tile))
            step(lambda tile=tile: proj_phase(tile))
            step(lambda tile=tile: ln_phase(tile, is_ln2=True))
            def _mlp0(tile=tile):
                ffn_mlp(tile, 0)
                state["h1_mm_last"] = state["wa_cons"][1]
            step(_mlp0)
            step(lambda tile=tile: ffn_mlp(tile, 1))
            step(lambda tile=tile: ffn_out(tile))
        step(finale)
    emit_all(steps)

    TENS = {"XNT": XNT, "VB": VB, "ATT": ATT, "QKB": QKB, "GT": GT, "T1R": T1R,
            "SLV": SLV, "FIN": FIN, "STT": STT, "RPS": RPS, "XCH": XCH,
            "SST": SST, "PR": PR, "PTSB": PTSB}
    if dbg:
        for s in ("t", "v", "a", "wd", "xd", "od", "gd", "g"):
            em.wait("scalar", (s, em.counts[s]))
        for name in dbg:
            th = TENS[name]
            shp = [th.shape[0], th.shape[1]]
            dt = th.ap().dtype
            dd = nc.dram_tensor("dbg_" + name, shp, dt, kind="ExternalOutput").ap()
            em.op("scalar", lambda e, dd=dd, th=th: e.dma_start(
                out=dd[:, :], in_=th[:, :]), inc="od", dma=True)
        em.op("scalar", lambda e, v=em.counts["od"]: e.wait_ge(em.sems["od"], v))

    with nc.Block() as block:
        @block.sync
        def _(e):
            for f in em.ops["sync"]:
                f(e)

        @block.tensor
        def _(e):
            for f in em.ops["tensor"]:
                f(e)

        @block.vector
        def _(e):
            for f in em.ops["vector"]:
                f(e)

        @block.scalar
        def _(e):
            for f in em.ops["scalar"]:
                f(e)

        @block.gpsimd
        def _(e):
            for f in em.ops["gpsimd"]:
                f(e)

    es.close()
    return nc



def _host_consts():
    mm_main = np.zeros((128, 144), np.float32)
    b01 = np.zeros((128, 144), np.float32)
    for r in range(128):
        for c in range(r, r + 16):
            d = abs(c - r - 8)
            mm_main[r, c] = SCALE * (WIN - d / 2.0) / WIN
            b01[r, c] = 1.0
    mm2 = np.concatenate([mm_main, mm_main], axis=1)
    mm_s0 = np.zeros((8, 16), np.float32)
    am_s0 = np.full((8, 16), NEG, np.float32)
    for r in range(8):
        for c in range(0, r + 8):
            d = abs(c - r)
            mm_s0[r, c] = SCALE * (WIN - d / 2.0) / WIN
            am_s0[r, c] = 0.0
    mm_last = np.zeros((120, 128), np.float32)
    am_last = np.full((120, 128), NEG, np.float32)
    for r in range(120):
        for c in range(r, min(r + 16, 128)):
            d = abs(c - r - 8)
            mm_last[r, c] = SCALE * (WIN - d / 2.0) / WIN
            am_last[r, c] = 0.0
    return mm2, b01, mm_s0, am_s0, mm_last, am_last


def _prep_weights(qkv_w, proj_w, proj_b, w1, w3, w2, wr):
    f = np.float32
    qkv_w = np.asarray(qkv_w, f)
    qk = np.ascontiguousarray(qkv_w[0:4096])
    T = qk.T
    Bq = T.reshape(16, 128, 32, 128).transpose(2, 1, 0, 3)
    order = []
    for p in range(16):
        order += [p, 16 + p]
    wqk = np.ascontiguousarray(Bq[order]).reshape(32, 128, 16 * 128)

    def nslice(w):
        T = np.asarray(w, f).T
        return np.ascontiguousarray(
            T.reshape(16, 128, 4, 512).transpose(2, 1, 0, 3)).reshape(4, 128, 16 * 512)

    wv = nslice(qkv_w[4096:6144])
    wp = nslice(proj_w)
    w2n = nslice(w2)

    def mblk(w):
        T = np.asarray(w, f).T
        return np.ascontiguousarray(
            T.reshape(16, 128, 16, 128).transpose(2, 1, 0, 3)).reshape(16, 128, 16 * 128)

    w1m = mblk(w1)
    w3m = mblk(w3)
    wrh = np.ascontiguousarray(
        np.asarray(wr, f).T.reshape(16, 128, 8).transpose(1, 0, 2)).reshape(128, 128)
    pb = np.ascontiguousarray(np.asarray(proj_b, f).reshape(1, D))
    return wqk, wv, wp, w1m, w3m, w2n, wrh, pb


def kernel(L0_attn_x, L0_attn_qkv_w, L0_attn_proj_w, L0_attn_proj_b,
           L0_ffn_w1, L0_ffn_w3, L0_ffn_w2, L0_ffn_wr):
    x = np.ascontiguousarray(np.asarray(L0_attn_x, np.float32))
    wqk, wv, wp, w1m, w3m, w2n, wrh, pb = _prep_weights(
        L0_attn_qkv_w, L0_attn_proj_w, L0_attn_proj_b, L0_ffn_w1, L0_ffn_w3,
        L0_ffn_w2, L0_ffn_wr)
    mm2, b01, mm_s0, am_s0, mm_last, am_last = _host_consts()
    ident = np.eye(128, dtype=np.float32)
    nc = build_nc()
    in_maps = []
    for c in range(8):
        cm = np.full((8, 1), 1.0 if c == 0 else 0.0, np.float32)
        in_maps.append({
            "x": np.ascontiguousarray(x[c]),
            "wqk": wqk, "wv": wv, "wp": wp, "w1": w1m, "w3": w3m, "w2": w2n,
            "wr": wrh, "pb": pb,
            "mm2": mm2, "b01": b01, "mm_s0": mm_s0,
            "am_s0": am_s0, "mm_last": mm_last, "am_last": am_last,
            "ident": ident, "coremask": cm,
        })
    res = run_bass_kernel_spmd(nc, in_maps, core_ids=list(range(8)))
    return np.stack([np.asarray(res.results[c]["y"]) for c in range(8)], axis=0)


# version 04
# speedup vs baseline: 1.4990x; 1.0242x over previous
import numpy as np
import concourse.bass as bass
import concourse.mybir as mybir
from concourse.bass_utils import run_bass_kernel_spmd
from contextlib import ExitStack

F32 = mybir.dt.float32
BF16 = mybir.dt.bfloat16
ALU = mybir.AluOpType
ACTF = mybir.ActivationFunctionType
AX = mybir.AxisListType

D = 2048
DC = 16
NT = 2048
TILE = 1024
H = 32
HD = 64
CW = 1040
NEG = -1.0e30
EPS = 1e-5
SCALE = HD ** -0.5
WIN = 8

ENGINES = ("sync", "tensor", "vector", "scalar", "gpsimd")


class Em:
    def __init__(self, nc, es):
        self.nc = nc
        self.ops = {e: [] for e in ENGINES}
        self.sems = {}
        self.counts = {}
        self.last_wait = {e: {} for e in ENGINES}
        self._vec_dirty = False
        for name in ("t", "v", "a", "g", "wd", "xd", "od", "gd"):
            self.sems[name] = es.enter_context(nc.semaphore("sem_" + name))
            self.counts[name] = 0

    def fence(self, eng, inc):
        self.counts[inc] += 1
        val = self.counts[inc]
        h = self.sems[inc]
        self.ops[eng].append(lambda e, h=h: e.drain().then_inc(h, 1))
        if eng == "vector":
            self._vec_dirty = False
        return (inc, val)

    def wait(self, eng, ev):
        if ev is None:
            return
        s, val = ev
        if val <= 0:
            return
        if self.last_wait[eng].get(s, 0) >= val:
            return
        self.last_wait[eng][s] = val
        h = self.sems[s]
        self.ops[eng].append(lambda e, h=h, val=val: e.wait_ge(h, val))

    def waits(self, eng, evs):
        for ev in evs:
            self.wait(eng, ev)

    def op(self, eng, fn, inc=None, dma=False, dep=True):
        if eng == "vector":
            if dep and self._vec_dirty:
                self.ops[eng].append(lambda e: e.drain())
            self._vec_dirty = True
        if inc is None:
            self.ops[eng].append(fn)
            return None
        n = 16 if dma else 1
        self.counts[inc] += n
        val = self.counts[inc]
        h = self.sems[inc]
        if not dma and eng in ("vector", "scalar"):
            self.ops[eng].append(fn)
            self.ops[eng].append(lambda e, h=h, n=n: e.drain().then_inc(h, n))
            if eng == "vector":
                self._vec_dirty = False
        else:
            self.ops[eng].append(lambda e, fn=fn, h=h, n=n: fn(e).then_inc(h, n))
        return (inc, val)


class Banks:
    def __init__(self, em):
        self.em = em
        self.w = {}
        self.r = {}

    def write(self, eng, bank):
        for ev in self.r.get(bank, {}).values():
            self.em.wait(eng, ev)
        self.em.wait(eng, self.w.get(bank))

    def wrote(self, bank, ev):
        self.w[bank] = ev
        self.r[bank] = {}

    def read(self, eng, bank):
        self.em.wait(eng, self.w.get(bank))

    def did_read(self, bank, eng, ev):
        self.r.setdefault(bank, {})[eng] = ev


def build_nc(steps=None, dbg=()):
    nc = bass.Bass("TRN2", target_bir_lowering=False)
    es = ExitStack()

    x_d = nc.dram_tensor("x", [NT, D], F32, kind="ExternalInput").ap()
    wqk_d = nc.dram_tensor("wqk", [32, 128, DC * 128], F32, kind="ExternalInput").ap()
    wv_d = nc.dram_tensor("wv", [4, 128, DC * 512], F32, kind="ExternalInput").ap()
    wp_d = nc.dram_tensor("wp", [4, 128, DC * 512], F32, kind="ExternalInput").ap()
    w1_d = nc.dram_tensor("w1", [16, 128, DC * 128], F32, kind="ExternalInput").ap()
    w3_d = nc.dram_tensor("w3", [16, 128, DC * 128], F32, kind="ExternalInput").ap()
    w2_d = nc.dram_tensor("w2", [4, 128, DC * 512], F32, kind="ExternalInput").ap()
    wr_d = nc.dram_tensor("wr", [128, DC * 8], F32, kind="ExternalInput").ap()
    pb_d = nc.dram_tensor("pb", [1, D], F32, kind="ExternalInput").ap()
    mm2_d = nc.dram_tensor("mm2", [128, 288], F32, kind="ExternalInput").ap()
    b01_d = nc.dram_tensor("b01", [128, 144], F32, kind="ExternalInput").ap()
    mms_d = nc.dram_tensor("mm_s0", [8, 16], F32, kind="ExternalInput").ap()
    ams_d = nc.dram_tensor("am_s0", [8, 16], F32, kind="ExternalInput").ap()
    mml_d = nc.dram_tensor("mm_last", [120, 128], F32, kind="ExternalInput").ap()
    aml_d = nc.dram_tensor("am_last", [120, 128], F32, kind="ExternalInput").ap()
    id_d = nc.dram_tensor("ident", [128, 128], F32, kind="ExternalInput").ap()
    cm_d = nc.dram_tensor("coremask", [8, 1], F32, kind="ExternalInput").ap()
    y_d = nc.dram_tensor("y", [NT, D], F32, kind="ExternalOutput").ap()
    cc_i = nc.dram_tensor("cc_i", [8, D], F32)
    cc_o = nc.dram_tensor("cc_o", [8, D], F32)
    dbg_d = {}

    sb = lambda name, shp, dt: es.enter_context(nc.sbuf_tensor(name, shp, dt))
    XNT = sb("XNT", [128, DC * CW], BF16)
    VB = sb("VB", [128, 9 * 2048], BF16)
    ATT = sb("ATT", [128, DC * 1024], BF16)
    SLV = sb("SLV", [128, DC * 8], BF16)
    QKB = sb("QKB", [128, 4 * CW], BF16)
    WA = sb("WA", [128, 2 * 2048], BF16)
    WB = sb("WB", [128, 2 * DC * 512], BF16)
    XCH = sb("XCH", [128, 2 * 2048], F32)
    XBK2 = sb("XBK2", [128, 2 * 128], F32)
    F32R = sb("F32R", [128, 2 * 128], F32)
    XSL = sb("XSL", [128, 2 * 512], F32)
    XPS = sb("XPS", [128, 2 * 512], F32)
    T1R = sb("T1R", [128, 4 * 288], F32)
    PR = sb("PR", [128, 4 * 320], BF16)
    PS0R = sb("PS0R", [32, 32], BF16)
    PTSB = sb("PTSB", [128, 2 * 512], BF16)
    STT = sb("STT", [128, 32], F32)
    SST = sb("SST", [128, 32], F32)
    RPS = sb("RPS", [128, 40], F32)
    GT = sb("GT", [128, 8 * 8], BF16)
    MM2 = sb("MM2", [128, 288], BF16)
    B01 = sb("B01", [128, 144], BF16)
    MMS = sb("MMS", [8, 16], F32)
    AMS = sb("AMS", [8, 16], F32)
    MML = sb("MML", [120, 128], BF16)
    AML = sb("AML", [120, 128], BF16)
    IDB = sb("IDB", [128, 128], BF16)
    IDF = sb("IDF", [128, 128], F32)
    ONE = sb("ONE", [1, 128], BF16)
    PBB = sb("PBB", [1, D], BF16)
    WR = sb("WR", [128, DC * 8], F32)
    CM = sb("CM", [8, 1], F32)
    ZAP = sb("ZAP", [128, 1], F32)
    EPSAP = sb("EPSAP", [128, 1], F32)
    SQSL = sb("SQSL", [128, 512], F32)
    FIN = sb("FIN", [8, D], F32)

    ps = lambda name, shp, dt: es.enter_context(nc.psum_tensor(name, shp, dt))
    PM = [ps("PM0", [128, 512], F32), ps("PM1", [128, 512], F32)]
    PSW = ps("PSW", [128, 1024], F32)
    PV = [ps("PV0", [128, 512], F32), ps("PV1", [128, 512], F32)]
    PT = [ps("PT0", [128, 1024], BF16), ps("PT1", [128, 1024], BF16)]
    BK_PM, BK_PS, BK_PV, BK_PT = (0, 1), ("PSW", "PSW"), (4, 5), (6, 7)
    PS = [PSW, PSW]

    em = Em(nc, es)
    bk = Banks(em)

    ld_evs = []
    for dst, src in ((MMS, mms_d), (AMS, ams_d),
                     (IDF, id_d), (WR, wr_d),
                     (CM, cm_d)):
        ld_evs.append(em.op("sync", lambda e, d=dst, s=src: e.dma_start(
            out=d[:, :], in_=s[:, :]), inc="xd", dma=True))
    ld_evs.append(em.op("gpsimd", lambda e: e.dma_start(
        out=IDB[:, :], in_=id_d[:, :]), inc="wd", dma=True))
    ld_evs.append(em.op("gpsimd", lambda e: e.dma_start(
        out=PBB[:, :], in_=pb_d[:, :]), inc="wd", dma=True))
    ld_evs.append(em.op("gpsimd", lambda e: e.dma_start(
        out=MM2[:, :], in_=mm2_d[:, :]), inc="wd", dma=True))
    ld_evs.append(em.op("gpsimd", lambda e: e.dma_start(
        out=MML[:, :], in_=mml_d[:, :]), inc="wd", dma=True))
    ld_evs.append(em.op("gpsimd", lambda e: e.dma_start(
        out=AML[:, :], in_=aml_d[:, :]), inc="wd", dma=True))
    ld_evs.append(em.op("gpsimd", lambda e: e.dma_start(
        out=B01[:, :], in_=b01_d[:, :]), inc="wd", dma=True))
    em.waits("vector", ld_evs)
    em.op("vector", lambda e: e.memset(ZAP[:, :], 0.0))
    em.op("vector", lambda e: e.memset(EPSAP[:, :], EPS))
    for s_ in range(4):
        for u_ in range(2):
            em.op("vector", lambda e, o=s_ * 320 + u_ * 160: e.memset(
                PR[:, o + 144: o + 160], 0.0))
    em.op("vector", lambda e: e.memset(PS0R[:, :], 0.0))
    em.op("vector", lambda e: e.memset(VB[0:32, 8 * 2048:9 * 2048], 0.0))
    ev_memset = em.op("vector", lambda e: e.memset(ONE[:, :], 1.0), inc="v")
    em.waits("tensor", ld_evs + [ev_memset])
    em.waits("scalar", ld_evs + [ev_memset])

    state = {
        "wa": [None, None], "wb": [None, None],
        "wa_cons": [None, None], "wb_cons": [None, None],
        "xch_cons": [None, None],
        "qk_cons": [None, None, None, None],
        "xsl_cons": [None, None], "xps_cons": [None, None],
        "t1_cons": [None, None], "pr_cons": [None, None],
        "ptm_cons": [None, None], "pth_cons": [None, None],
        "xbk_cons": [None, None], "xbk2_cons": [None, None],
        "f32r_cons": [None, None],
        "y_written": {},
        "ln_ev": {"vector": None, "scalar": None},
        "ln1_chunk_evs": {},
        "ln2_chunk_evs": {},
        "v_done": {"vector": None, "scalar": None},
    }

    def load_wa(slot, src_ap):
        em.wait("gpsimd", state["wa_cons"][slot])
        ev = em.op("gpsimd", lambda e, s=src_ap, sl=slot: e.dma_start(
            out=WA[:, sl * 2048:(sl + 1) * 2048], in_=s,
            max_dma_last_dim=8192), inc="wd", dma=True)
        state["wa"][slot] = ev
        return ev

    def load_wb(slot, src_ap):
        em.wait("gpsimd", state["wb_cons"][slot])
        ev = em.op("gpsimd", lambda e, s=src_ap, sl=slot: e.dma_start(
            out=WB[:, sl * DC * 512:(sl + 1) * DC * 512], in_=s,
            max_dma_last_dim=8192), inc="wd", dma=True)
        state["wb"][slot] = ev
        return ev

    def ln_chunk(tile, i, nrow, rows_src, is_ln2):
        sl = i % 2
        em.wait("sync", state["xch_cons"][sl])
        if is_ln2:
            em.wait("sync", state["y_written"][rows_src // 128])
            src = y_d[rows_src:rows_src + nrow, :]
        else:
            src = x_d[rows_src:rows_src + nrow, :]
        ev_x = em.op("sync", lambda e, s=src, sl=sl, n=nrow: e.dma_start(
            out=XCH[0:n, sl * 2048: sl * 2048 + 2048], in_=s), inc="xd", dma=True)
        em.wait("vector", ev_x)
        em.op("vector", lambda e, sl=sl, n=nrow: e.reduce_sum(
            STT[0:n, 23:24], XCH[0:n, sl * 2048: sl * 2048 + 2048], axis=AX.X))
        em.op("vector", lambda e, n=nrow: e.tensor_scalar(
            out=STT[0:n, 24:25], in0=STT[0:n, 23:24], scalar1=1.0 / D,
            scalar2=None, op0=ALU.mult))
        for q in range(4):
            em.op("vector", lambda e, q=q, sl=sl, n=nrow: e.scalar_tensor_tensor(
                out=SQSL[0:n, 0:512],
                in0=XCH[0:n, sl * 2048 + q * 512: sl * 2048 + (q + 1) * 512],
                scalar=1.0,
                in1=XCH[0:n, sl * 2048 + q * 512: sl * 2048 + (q + 1) * 512],
                op0=ALU.mult, op1=ALU.mult))
            em.op("vector", lambda e, q=q, n=nrow: e.reduce_sum(
                STT[0:n, 16 + q:17 + q], SQSL[0:n, 0:512], axis=AX.X))
        em.op("vector", lambda e, n=nrow: e.tensor_tensor(
            out=STT[0:n, 20:21], in0=STT[0:n, 16:17], in1=STT[0:n, 17:18], op=ALU.add))
        em.op("vector", lambda e, n=nrow: e.tensor_tensor(
            out=STT[0:n, 21:22], in0=STT[0:n, 18:19], in1=STT[0:n, 19:20], op=ALU.add))
        em.op("vector", lambda e, n=nrow: e.tensor_tensor(
            out=STT[0:n, 22:23], in0=STT[0:n, 20:21], in1=STT[0:n, 21:22], op=ALU.add))
        em.op("vector", lambda e, n=nrow: e.tensor_scalar(
            out=STT[0:n, 26:27], in0=STT[0:n, 22:23], scalar1=1.0 / D,
            scalar2=None, op0=ALU.mult))
        em.op("vector", lambda e, n=nrow: e.tensor_tensor(
            out=STT[0:n, 18:19], in0=STT[0:n, 24:25], in1=STT[0:n, 24:25], op=ALU.mult))
        ev_mv = em.op("vector", lambda e, n=nrow: e.tensor_tensor(
            out=STT[0:n, 25:26], in0=STT[0:n, 26:27], in1=STT[0:n, 18:19],
            op=ALU.subtract), inc="v")
        em.wait("scalar", ev_mv)
        ev_sd = em.op("scalar", lambda e, n=nrow: e.activation(
            out=STT[0:n, 27:28], in_=STT[0:n, 25:26], func=ACTF.Sqrt,
            bias=EPSAP[0:n, 0:1], scale=1.0), inc="a")
        em.wait("vector", ev_sd)
        em.op("vector", lambda e, n=nrow: e.reciprocal(
            STT[0:n, 28:29], STT[0:n, 27:28]))
        ev_nb = em.op("vector", lambda e, n=nrow: e.tensor_scalar(
            out=STT[0:n, 29:30], in0=STT[0:n, 24:25], scalar1=STT[0:n, 28:29],
            scalar2=-1.0, op0=ALU.mult, op1=ALU.mult), inc="v")
        em.wait("scalar", ev_nb)

        t0 = i * 128
        ev_nm = None
        chunk_evs = []
        if not is_ln2:
            em.wait("scalar", state.get("xn_cons", [None, None])[sl])
            em.wait("scalar", state.get("ffn_h2_done"))
            ev_nm = em.op("scalar", lambda e, sl=sl, n=nrow: e.activation(
                out=ATT[0:n, sl * 2048: sl * 2048 + 2048],
                in_=XCH[0:n, sl * 2048: sl * 2048 + 2048],
                func=ACTF.Identity, bias=STT[0:n, 29:30], scale=STT[0:n, 28:29]),
                inc="a")
            state.setdefault("xn_cons", [None, None])
            for c in range(DC):
                bs = c % 2
                ptb = BK_PT[bs]
                bk.write("tensor", ptb)
                em.wait("tensor", ev_nm)
                ev_tr = em.op("tensor", lambda e, bs=bs, c=c, sl=sl, n=nrow: e.transpose(
                    PT[bs][0:128, 0:n],
                    ATT[0:n, sl * 2048 + c * 128: sl * 2048 + (c + 1) * 128],
                    IDB[0:n, 0:n]), inc="t")
                bk.wrote(ptb, ev_tr)
                state["xn_cons"][sl] = ev_tr
                eng = "vector" if c % 2 == 0 else "scalar"
                em.wait(eng, ev_tr)
                if eng == "vector":
                    ev_e = em.op("vector", lambda e, bs=bs, c=c, t0=t0, n=nrow: e.tensor_copy(
                        XNT[:, c * CW + t0: c * CW + t0 + n], PT[bs][0:128, 0:n]),
                        inc="v")
                else:
                    ev_e = em.op("scalar", lambda e, bs=bs, c=c, t0=t0, n=nrow: e.activation(
                        out=XNT[:, c * CW + t0: c * CW + t0 + n],
                        in_=PT[bs][0:128, 0:n], func=ACTF.Copy, bias=0.0, scale=1.0),
                        inc="a")
                bk.did_read(ptb, eng, ev_e)
                state["ln_ev"][eng] = ev_e
                chunk_evs.append(ev_e)
            state["ln1_chunk_evs"][i] = chunk_evs[-2:]
            state["xch_cons"][sl] = ev_nm
            return
        for c in range(DC):
            bs = c % 2
            dst = XBK2
            em.wait("scalar", state["xbk2_cons"][bs])
            ev_nm = em.op("scalar", lambda e, c=c, sl=sl, bs=bs, n=nrow, d=dst: e.activation(
                out=d[0:n, bs * 128:bs * 128 + 128],
                in_=XCH[0:n, sl * 2048 + c * 128: sl * 2048 + (c + 1) * 128],
                func=ACTF.Identity, bias=STT[0:n, 29:30], scale=STT[0:n, 28:29]),
                inc="a")
            if is_ln2:
                pmb = BK_PM[bs]
                bk.write("tensor", pmb)
                em.wait("tensor", ev_nm)
                ev_tr = em.op("tensor", lambda e, bs=bs, n=nrow: e.transpose(
                    PM[bs][0:128, 0:n], XBK2[0:n, bs * 128:bs * 128 + 128],
                    IDF[0:n, 0:n]), inc="t")
                bk.wrote(pmb, ev_tr)
                state["xbk2_cons"][bs] = ev_tr
                em.wait("vector", ev_tr)
                em.wait("vector", state["f32r_cons"][bs])
                ev_f = em.op("vector", lambda e, bs=bs, n=nrow: e.tensor_copy(
                    F32R[:, bs * 128:bs * 128 + n], PM[bs][0:128, 0:n]), inc="v")
                bk.did_read(pmb, "vector", ev_f)
                em.wait("scalar", ev_f)
                ev_cb = em.op("scalar", lambda e, bs=bs, c=c, t0=t0, n=nrow: e.activation(
                    out=XNT[:, c * CW + t0: c * CW + t0 + n],
                    in_=PM[bs][0:128, 0:n], func=ACTF.Copy, bias=0.0, scale=1.0),
                    inc="a")
                bk.did_read(pmb, "scalar", ev_cb)
                state["ln_ev"]["scalar"] = ev_cb
                bk.write("tensor", BK_PS[0])
                em.wait("tensor", ev_f)
                ev_rmm = em.op("tensor", lambda e, bs=bs, c=c, i=i, n=nrow: e.matmul(
                    PSW[0:n, 8 * i: 8 * i + 8],
                    lhsT=F32R[:, bs * 128:bs * 128 + n],
                    rhs=WR[:, 8 * c:8 * c + 8],
                    start=(c == 0), stop=(c == DC - 1)), inc="t")
                state["f32r_cons"][bs] = ev_rmm
                if c == DC - 1:
                    bk.wrote(BK_PS[0], ev_rmm)
        state["ln2_chunk_evs"][i] = ev_cb
        state["xch_cons"][sl] = ev_nm

    def ln_phase(tile, is_ln2):
        base = tile * TILE
        ntok = 1040 if (tile == 0 and not is_ln2) else 1024
        nch = (ntok + 127) // 128
        for i in range(nch):
            nrow = min(128, ntok - i * 128)
            ln_chunk(tile, i, nrow, base + i * 128, is_ln2)
        if is_ln2:
            router_post(tile)

    def router_post(tile):
        RL = T1R
        bk.read("vector", BK_PS[0])
        em.wait("vector", state["t1_cons"][0])
        em.wait("vector", state["t1_cons"][1])
        ev_rl = em.op("vector", lambda e: e.tensor_copy(
            RL[:, 0:128], PSW[0:128, 0:128]), inc="v")
        bk.did_read(BK_PS[0], "vector", ev_rl)
        ev_g = None
        for i in range(8):
            em.op("vector", lambda e, i=i: e.reduce_max(
                RPS[:, 32:33], RL[:, 8 * i:8 * i + 8], axis=AX.X, negate=True))
            ev1 = em.op("vector", lambda e, i=i: e.tensor_scalar(
                out=RPS[:, 0:8], in0=RL[:, 8 * i:8 * i + 8], scalar1=RPS[:, 32:33],
                scalar2=None, op0=ALU.add), inc="v")
            em.wait("scalar", ev1)
            ev2 = em.op("scalar", lambda e: e.activation(
                out=RPS[:, 0:8], in_=RPS[:, 0:8], func=ACTF.Exp, bias=ZAP[:, 0:1],
                scale=1.0), inc="a")
            em.wait("vector", ev2)
            em.op("vector", lambda e: e.reduce_sum(
                RPS[:, 33:34], RPS[:, 0:8], axis=AX.X))
            em.op("vector", lambda e: e.reciprocal(RPS[:, 34:35], RPS[:, 33:34]))
            em.op("vector", lambda e: e.tensor_scalar(
                out=RPS[:, 0:8], in0=RPS[:, 0:8], scalar1=RPS[:, 34:35],
                scalar2=None, op0=ALU.mult))
            em.op("vector", lambda e: e.max(RPS[:, 8:16], RPS[:, 0:8]))
            em.op("vector", lambda e: e.tensor_scalar(
                out=RPS[:, 16:24], in0=RPS[:, 0:8], scalar1=RPS[:, 9:10],
                scalar2=None, op0=ALU.is_ge))
            em.op("vector", lambda e: e.tensor_tensor(
                out=RPS[:, 35:36], in0=RPS[:, 8:9], in1=RPS[:, 9:10], op=ALU.add))
            em.op("vector", lambda e: e.reciprocal(RPS[:, 36:37], RPS[:, 35:36]))
            ev_g = em.op("vector", lambda e, i=i: e.scalar_tensor_tensor(
                out=GT[:, 8 * i:8 * i + 8], in0=RPS[:, 16:24], scalar=RPS[:, 36:37],
                in1=RPS[:, 0:8], op0=ALU.mult, op1=ALU.mult), inc="v")
        state["gates_ev"] = ev_g
        state["rl_ev"] = ev_rl

    def v_phase(tile):
        ntok = 1040 if tile == 0 else 1024
        nj = 9 if tile == 0 else 8
        ev_e = None
        for n in range(4):
            sl = n % 2
            load_wb(sl, wv_d[n])
            ev_w = state["wb"][sl]
            for j in range(nj):
                nrow = min(128, ntok - j * 128)
                pm = (n * 9 + j) % 2
                pmb = BK_PM[pm]
                bk.write("tensor", pmb)
                em.wait("tensor", ev_w)
                em.waits("tensor", state["ln1_chunk_evs"][j])
                ev_mm = None
                for c in range(DC):
                    ev_mm = em.op("tensor", lambda e, c=c, j=j, sl=sl, pm=pm, nr=nrow: e.matmul(
                        PM[pm][0:nr, 0:512],
                        lhsT=XNT[:, c * CW + j * 128: c * CW + j * 128 + nr],
                        rhs=WB[:, sl * DC * 512 + c * 512: sl * DC * 512 + (c + 1) * 512],
                        start=(c == 0), stop=(c == DC - 1)),
                        inc=("t" if c == DC - 1 else None))
                bk.wrote(pmb, ev_mm)
                state["wb_cons"][sl] = ev_mm
                eng = "vector" if j % 2 == 0 else "scalar"
                em.wait(eng, ev_mm)
                if eng == "vector":
                    ev_e = em.op("vector", lambda e, j=j, n=n, pm=pm, nr=nrow: e.tensor_copy(
                        VB[0:nr, j * 2048 + n * 512: j * 2048 + (n + 1) * 512],
                        PM[pm][0:nr, 0:512]), inc="v")
                else:
                    ev_e = em.op("scalar", lambda e, j=j, n=n, pm=pm, nr=nrow: e.activation(
                        out=VB[0:nr, j * 2048 + n * 512: j * 2048 + (n + 1) * 512],
                        in_=PM[pm][0:nr, 0:512], func=ACTF.Copy, bias=0.0,
                        scale=1.0), inc="a")
                bk.did_read(pmb, eng, ev_e)
                state["v_done"][eng] = ev_e

    def units_for_tile(tile):
        if tile == 0:
            us = [("s0", 0, 8, 0, 16, 0)]
            us += [("main", 8 + 128 * j, 128, 128 * j, 144, j) for j in range(8)]
        else:
            us = [("main", 8 + 128 * j, 128, 128 * j, 144, j) for j in range(7)]
            us.append(("last", 904, 120, 896, 128, 7))
        return us

    def qk_group(tile, pair, kind, off, n, last):
        sl = pair % 2
        ring = kind * 2 + sl
        blk = 2 * pair + kind
        wsl = blk % 2
        if off == 0:
            load_wa(wsl, wqk_d[blk])
        ev_w = state["wa"][wsl]
        state["qk_pm"] = (state.get("qk_pm", 0) + 1) % 2
        pm = state["qk_pm"]
        pmb = BK_PM[pm]
        bk.write("tensor", pmb)
        em.wait("tensor", ev_w)
        ev_mm = None
        for c in range(DC):
            ev_mm = em.op("tensor", lambda e, c=c, off=off, n=n, wsl=wsl, pm=pm: e.matmul(
                PM[pm][0:128, 0:n],
                lhsT=WA[:, wsl * 2048 + c * 128: wsl * 2048 + (c + 1) * 128],
                rhs=XNT[:, c * CW + off: c * CW + off + n],
                start=(c == 0), stop=(c == DC - 1)),
                inc=("t" if c == DC - 1 else None))
        bk.wrote(pmb, ev_mm)
        state["wa_cons"][wsl] = ev_mm
        eng = "vector" if kind == 0 else "scalar"
        em.wait(eng, ev_mm)
        em.wait(eng, state["qk_cons"][ring])
        if eng == "vector":
            ev_e = em.op("vector", lambda e, ring=ring, off=off, n=n, pm=pm: e.tensor_copy(
                QKB[:, ring * CW + off: ring * CW + off + n], PM[pm][0:128, 0:n]),
                inc="v")
        else:
            ev_e = em.op("scalar", lambda e, ring=ring, off=off, n=n, pm=pm: e.activation(
                out=QKB[:, ring * CW + off: ring * CW + off + n],
                in_=PM[pm][0:128, 0:n], func=ACTF.Copy, bias=0.0, scale=1.0),
                inc="a")
        bk.did_read(pmb, eng, ev_e)
        if last:
            state.setdefault("qk_ready", {})[(pair, kind)] = ev_e

    def head_bundles(tile, head):
        import os as _os3
        solo_last = int(_os3.environ.get("K_SOLO_LAST", "0"))
        us = units_for_tile(tile)
        bs, i = [], 0
        while i < len(us):
            can_pair = (us[i][0] == "main" and i + 1 < len(us)
                        and us[i + 1][0] == "main")
            if solo_last and tile == 0 and us[i][5] == 6 and us[i][0] == "main":
                can_pair = False
            if can_pair:
                bs.append((head, [us[i], us[i + 1]]))
                i += 2
            else:
                bs.append((head, [us[i]]))
                i += 1
        return bs

    def qk_attn_phase(tile):
        ncols = 1040 if tile == 0 else 1024
        nsl = [(0, 512), (512, 512), (1024, 16)] if ncols == 1040 else [(0, 512), (512, 512)]
        if tile == 1:
            for c in range(DC):
                ev_sv = em.op("vector", lambda e, c=c: e.tensor_copy(
                    ATT[:, c * 1024: c * 1024 + 8], SLV[:, c * 8: c * 8 + 8]),
                    inc="v", dep=(c == 0))
            state["sliver_ev"] = ev_sv
        for kind in (0, 1):
            for gi, (off, n) in enumerate(nsl):
                qk_group(tile, 0, kind, off, n, gi == len(nsl) - 1)

        bundles = []
        for pair in range(16):
            for hh in range(2):
                bundles += head_bundles(tile, 2 * pair + hh)
        import os as _os
        per_pair = len(bundles) // 16
        _lim = int(_os.environ.get("K_DBG_NBUND", "0"))
        if _lim:
            bundles = bundles[:_lim]
        NB = len(bundles)
        qkq = []
        maxpair = (NB - 1) // per_pair if _lim else 15
        import os as _os2
        _noil = int(_os2.environ.get("K_NO_QK_INTERLEAVE", "0"))
        for pair in range(min(15, maxpair)):
            ngr = 2 * len(nsl)
            for g in range(ngr):
                pos = pair * per_pair + (0 if _noil else (g * per_pair) // ngr)
                qkq.append((pos, pair + 1, g))
        ctx = [None] * NB
        incA = {}
        incB = {}
        incC = {}

        def mkctx(bi):
            head, us = bundles[bi]
            c = dict(head=head, units=us, s=bi % 2, s4=bi % 4, par=head % 2,
                     pair=head // 2, dvc=head * 64, nu=len(us))
            c["p0"] = 64 * c["par"]
            return c

        def S0(bi, evA):
            c = ctx[bi]
            em.wait("tensor", evA)
            em.waits("tensor", [state["qk_ready"][(c["pair"], 0)],
                                state["qk_ready"][(c["pair"], 1)]])
            bk.write("tensor", "PSW")
            rsl = c["pair"] % 2
            p0 = c["p0"]
            ev = None
            for ui, (kind, q0, nq, k0, nk, vj) in enumerate(c["units"]):
                qr = QKB[p0:p0 + 64, rsl * CW + q0: rsl * CW + q0 + nq]
                kr = QKB[p0:p0 + 64, (2 + rsl) * CW + k0: (2 + rsl) * CW + k0 + nk]
                ev = em.op("tensor", lambda e, qr=qr, kr=kr, ui=ui, nq=nq, nk=nk: e.matmul(
                    PSW[0:nq, ui * 512: ui * 512 + nk], lhsT=qr, rhs=kr,
                    start=True, stop=True),
                    inc=("t" if ui == c["nu"] - 1 else None))
            c["ev_sc"] = ev
            bk.wrote("PSW", ev)
            state["qk_cons"][rsl] = ev
            state["qk_cons"][2 + rsl] = ev

        def S1(bi):
            c = ctx[bi]
            s = c["s"]
            s4 = c["s4"]
            em.wait("vector", c["ev_sc"])
            em.wait("vector", incC.get(bi - 1))
            if c["nu"] == 2:
                pv = PSW[0:128, 0:1024].rearrange("p (u x) -> p u x", u=2)[:, :, 0:144]
                em.op("vector", lambda e, pv=pv, s=s: e.scalar_tensor_tensor(
                    out=T1R[:, s4 * 288: s4 * 288 + 288], in0=pv, scalar=1.0,
                    in1=MM2[:, 0:288], op0=ALU.mult, op1=ALU.mult), dep=False)
            else:
                kind, q0, nq, k0, nk, vj = c["units"][0]
                mm = {"main": MM2, "s0": MMS, "last": MML}[kind]
                em.op("vector", lambda e, s=s, s4=s4, nq=nq, nk=nk, mm=mm: e.scalar_tensor_tensor(
                    out=T1R[0:nq, s4 * 288: s4 * 288 + nk], in0=PSW[0:nq, 0:nk],
                    scalar=1.0, in1=mm[0:nq, 0:nk], op0=ALU.mult, op1=ALU.mult),
                    dep=False)
            bk.did_read("PSW", "vector", None)

        def S1tt(bi):
            c = ctx[bi]
            s = c["s"]
            s4 = c["s4"]
            kind, q0, nq, k0, nk, vj = c["units"][0]
            am = {"s0": AMS, "last": AML}[kind]
            em.op("vector", lambda e, s=s, s4=s4, nq=nq, nk=nk, am=am: e.tensor_tensor(
                out=T1R[0:nq, s4 * 288: s4 * 288 + nk],
                in0=T1R[0:nq, s4 * 288: s4 * 288 + nk],
                in1=am[0:nq, 0:nk], op=ALU.add), dep=False)

        def S1rmax(bi):
            c = ctx[bi]
            s = c["s"]
            s4 = c["s4"]
            if c["nu"] == 2:
                tv = T1R[:, s4 * 288:(s4 + 1) * 288].rearrange("p (u x) -> p u x", u=2)
                em.op("vector", lambda e, tv=tv, s=s: e.reduce_max(
                    SST[:, s4 * 8: s4 * 8 + 2], tv, axis=AX.X, negate=True), dep=False)
            else:
                kind, q0, nq, k0, nk, vj = c["units"][0]
                em.op("vector", lambda e, s=s, s4=s4, nq=nq, nk=nk: e.reduce_max(
                    SST[0:nq, s4 * 8: s4 * 8 + 1], T1R[0:nq, s4 * 288: s4 * 288 + nk],
                    axis=AX.X, negate=True), dep=False)

        def S2sum(bi):
            c = ctx[bi]
            s = c["s"]
            s4 = c["s4"]
            em.wait("vector", incC.get(bi + 1))
            if c["nu"] == 2:
                tv = T1R[:, s4 * 288:(s4 + 1) * 288].rearrange("p (u x) -> p u x", u=2)
                em.op("vector", lambda e, tv=tv, s=s: e.reduce_sum(
                    SST[:, s4 * 8 + 2: s4 * 8 + 4], tv, axis=AX.X), dep=False)
            else:
                kind, q0, nq, k0, nk, vj = c["units"][0]
                em.op("vector", lambda e, s=s, s4=s4, nq=nq, nk=nk: e.reduce_sum(
                    SST[0:nq, s4 * 8 + 2: s4 * 8 + 3], T1R[0:nq, s4 * 288: s4 * 288 + nk],
                    axis=AX.X), dep=False)

        def S2corr(bi):
            c = ctx[bi]
            s = c["s"]
            s4 = c["s4"]
            for ui, (kind, q0, nq, k0, nk, vj) in enumerate(c["units"]):
                em.op("vector", lambda e, s=s, s4=s4, ui=ui: e.scalar_tensor_tensor(
                    out=SST[:, s4 * 8 + 4 + ui: s4 * 8 + 5 + ui],
                    in0=T1R[:, s4 * 288 + ui * 144 + 143: s4 * 288 + ui * 144 + 144],
                    scalar=-128.0, in1=SST[:, s4 * 8 + 2 + ui: s4 * 8 + 3 + ui],
                    op0=ALU.mult, op1=ALU.add), dep=False)

        def S2recip(bi):
            c = ctx[bi]
            s = c["s"]
            s4 = c["s4"]
            main = c["units"][0][0] == "main"
            base = s4 * 8 + (4 if main else 2)
            nq = 128 if c["nu"] == 2 else c["units"][0][2]
            em.op("vector", lambda e, s=s, s4=s4, base=base, nu=c["nu"], nq=nq: e.reciprocal(
                SST[0:nq, s4 * 8 + 6: s4 * 8 + 6 + nu], SST[0:nq, base: base + nu]),
                dep=False)

        def S2pnorm(bi, need_drain):
            c = ctx[bi]
            s = c["s"]
            s4 = c["s4"]
            em.wait("vector", (ctx[bi - 2].get("ev_tr") if bi >= 2 else None))
            if c["units"][0][0] == "s0":
                em.wait("vector", state.get("s0_tr_ev"))
            for ui, (kind, q0, nq, k0, nk, vj) in enumerate(c["units"]):
                if kind == "main":
                    em.op("vector", lambda e, s4=s4, ui=ui, nq=nq: e.scalar_tensor_tensor(
                        out=PR[0:nq, s4 * 320 + ui * 160: s4 * 320 + ui * 160 + 144],
                        in0=T1R[0:nq, s4 * 288 + ui * 144: s4 * 288 + ui * 144 + 144],
                        scalar=SST[0:nq, s4 * 8 + 6 + ui: s4 * 8 + 7 + ui],
                        in1=B01[0:nq, 0:144], op0=ALU.mult, op1=ALU.mult),
                        dep=(need_drain and ui == 0))
                elif kind == "s0":
                    em.op("vector", lambda e, s=s, s4=s4: e.tensor_scalar(
                        out=PS0R[0:8, 0:16], in0=T1R[0:8, s4 * 288: s4 * 288 + 16],
                        scalar1=SST[0:8, s4 * 8 + 6: s4 * 8 + 7], scalar2=None,
                        op0=ALU.mult), dep=need_drain)
                else:
                    em.op("vector", lambda e, s=s, s4=s4, nq=nq, nk=nk: e.tensor_scalar(
                        out=PR[0:nq, s4 * 320: s4 * 320 + nk],
                        in0=T1R[0:nq, s4 * 288: s4 * 288 + nk],
                        scalar1=SST[0:nq, s4 * 8 + 6: s4 * 8 + 7], scalar2=None,
                        op0=ALU.mult), dep=need_drain)

        def Sexp(bi, ev):
            c = ctx[bi]
            s = c["s"]
            s4 = c["s4"]
            em.wait("scalar", ev)
            for ui, (kind, q0, nq, k0, nk, vj) in enumerate(c["units"]):
                em.op("scalar", lambda e, s=s, s4=s4, ui=ui, nq=nq, nk=nk: e.activation(
                    out=T1R[0:nq, s4 * 288 + ui * 144: s4 * 288 + ui * 144 + nk],
                    in_=T1R[0:nq, s4 * 288 + ui * 144: s4 * 288 + ui * 144 + nk],
                    func=ACTF.Exp, bias=SST[0:nq, s4 * 8 + ui: s4 * 8 + ui + 1],
                    scale=1.0))

        def S3tr(bi, ev):
            c = ctx[bi]
            s = c["s"]
            s4 = c["s4"]
            ptb = BK_PT[s]
            em.wait("tensor", ev)
            bk.write("tensor", ptb)
            evt = None
            for ui, (kind, q0, nq, k0, nk, vj) in enumerate(c["units"]):
                if kind == "s0":
                    evt = em.op("tensor", lambda e, s=s, s4=s4: e.transpose(
                        PT[s][0:32, 0:32], PS0R[0:32, 0:32], IDB[0:32, 0:32]),
                        inc="t")
                    state["s0_tr_ev"] = evt
                elif kind == "last":
                    evt = em.op("tensor", lambda e, s=s, s4=s4: e.transpose(
                        PT[s][0:128, 0:128], PR[:, s4 * 320: s4 * 320 + 128],
                        IDB[:, :]), inc="t")
                else:
                    em.op("tensor", lambda e, s=s, s4=s4, ui=ui: e.transpose(
                        PT[s][0:128, ui * 256: ui * 256 + 128],
                        PR[:, s4 * 320 + ui * 160: s4 * 320 + ui * 160 + 128],
                        IDB[:, :]))
                    evt = em.op("tensor", lambda e, s=s, s4=s4, ui=ui: e.transpose(
                        PT[s][0:32, ui * 256 + 128: ui * 256 + 256],
                        PR[:, s4 * 320 + ui * 160 + 128: s4 * 320 + ui * 160 + 160],
                        IDB[:, :]), inc="t")
            c["ev_tr"] = evt
            bk.wrote(ptb, evt)

        def S3ev(bi):
            c = ctx[bi]
            s = c["s"]
            s4 = c["s4"]
            em.wait("vector", c["ev_tr"])
            em.wait("vector", (ctx[bi - 2]["ev_pv"] if bi >= 2 else None))
            kind = c["units"][0][0]
            if kind == "s0":
                em.op("vector", lambda e, s=s, s4=s4: e.tensor_copy(
                    PTSB[0:32, s * 512: s * 512 + 32], PT[s][0:32, 0:32]), dep=False)
            elif kind == "last":
                em.op("vector", lambda e, s=s, s4=s4: e.tensor_copy(
                    PTSB[0:128, s * 512: s * 512 + 128], PT[s][0:128, 0:128]),
                    dep=False)
            else:
                w = 256 * c["nu"]
                em.op("vector", lambda e, s=s, s4=s4, w=w: e.tensor_copy(
                    PTSB[0:128, s * 512: s * 512 + w], PT[s][0:128, 0:w]), dep=False)
            bk.did_read(BK_PT[s], "vector", None)

        def S3pv(bi, evB, evC):
            c = ctx[bi]
            s = c["s"]
            s4 = c["s4"]
            pvb = BK_PV[s]
            em.waits("tensor", [evB, evC,
                                state["v_done"]["vector"], state["v_done"]["scalar"]])
            bk.write("tensor", pvb)
            p0 = c["p0"]
            dvc = c["dvc"]
            tp = (0, p0)
            ev = None
            for ui, (kind, q0, nq, k0, nk, vj) in enumerate(c["units"]):
                if kind == "s0":
                    ev = em.op("tensor", lambda e, s=s, s4=s4, p0=p0, dvc=dvc, tp=tp: e.matmul(
                        PV[s][p0:p0 + 64, 0:8],
                        lhsT=VB[0:32, dvc: dvc + 64],
                        rhs=PTSB[0:32, s * 512: s * 512 + 8],
                        start=True, stop=True, tile_position=tp), inc="t")
                elif kind == "last":
                    ev = em.op("tensor", lambda e, s=s, s4=s4, p0=p0, dvc=dvc, vj=vj, nq=nq, tp=tp: e.matmul(
                        PV[s][p0:p0 + 64, 0:nq],
                        lhsT=VB[:, vj * 2048 + dvc: vj * 2048 + dvc + 64],
                        rhs=PTSB[0:128, s * 512: s * 512 + nq],
                        start=True, stop=True, tile_position=tp), inc="t")
                else:
                    em.op("tensor", lambda e, s=s, s4=s4, ui=ui, p0=p0, dvc=dvc, vj=vj, tp=tp: e.matmul(
                        PV[s][p0:p0 + 64, ui * 128: ui * 128 + 128],
                        lhsT=VB[:, vj * 2048 + dvc: vj * 2048 + dvc + 64],
                        rhs=PTSB[0:128, s * 512 + ui * 256: s * 512 + ui * 256 + 128],
                        start=True, stop=False, tile_position=tp))
                    ev = em.op("tensor", lambda e, s=s, s4=s4, ui=ui, p0=p0, dvc=dvc, vj=vj, tp=tp: e.matmul(
                        PV[s][p0:p0 + 64, ui * 128: ui * 128 + 128],
                        lhsT=VB[0:32, (vj + 1) * 2048 + dvc: (vj + 1) * 2048 + dvc + 64],
                        rhs=PTSB[0:32, s * 512 + ui * 256 + 128: s * 512 + ui * 256 + 256],
                        start=False, stop=True, tile_position=tp),
                        inc=("t" if ui == c["nu"] - 1 else None))
            c["ev_pv"] = ev
            bk.wrote(pvb, ev)

        def S4(bi):
            c = ctx[bi]
            s = c["s"]
            s4 = c["s4"]
            em.wait("scalar", c["ev_pv"])
            p0 = c["p0"]
            dv0 = p0
            ch = c["head"] // 2
            us = c["units"]
            if c["nu"] == 2:
                q0 = us[0][1]
                if tile == 0 and us[1][5] == 7:
                    em.op("scalar", lambda e, s=s, s4=s4, p0=p0, dv0=dv0, ch=ch, q0=q0: e.activation(
                        out=ATT[dv0:dv0 + 64, ch * 1024 + q0: ch * 1024 + 1024],
                        in_=PV[s][p0:p0 + 64, 0:248], func=ACTF.Copy, bias=0.0,
                        scale=1.0))
                    em.op("scalar", lambda e, s=s, s4=s4, p0=p0, dv0=dv0, ch=ch: e.activation(
                        out=SLV[dv0:dv0 + 64, ch * 8: ch * 8 + 8],
                        in_=PV[s][p0:p0 + 64, 248:256], func=ACTF.Copy, bias=0.0,
                        scale=1.0))
                else:
                    em.op("scalar", lambda e, s=s, s4=s4, p0=p0, dv0=dv0, ch=ch, q0=q0: e.activation(
                        out=ATT[dv0:dv0 + 64, ch * 1024 + q0: ch * 1024 + q0 + 256],
                        in_=PV[s][p0:p0 + 64, 0:256], func=ACTF.Copy, bias=0.0,
                        scale=1.0))
            else:
                kind, q0, nq, k0, nk, vj = us[0]
                if tile == 0 and kind == "main" and vj == 7:
                    em.op("scalar", lambda e, s=s, s4=s4, p0=p0, dv0=dv0, ch=ch, q0=q0: e.activation(
                        out=ATT[dv0:dv0 + 64, ch * 1024 + q0: ch * 1024 + 1024],
                        in_=PV[s][p0:p0 + 64, 0:120], func=ACTF.Copy, bias=0.0,
                        scale=1.0))
                    em.op("scalar", lambda e, s=s, s4=s4, p0=p0, dv0=dv0, ch=ch: e.activation(
                        out=SLV[dv0:dv0 + 64, ch * 8: ch * 8 + 8],
                        in_=PV[s][p0:p0 + 64, 120:128], func=ACTF.Copy, bias=0.0,
                        scale=1.0))
                else:
                    em.op("scalar", lambda e, s=s, s4=s4, p0=p0, dv0=dv0, ch=ch, q0=q0, nq=nq: e.activation(
                        out=ATT[dv0:dv0 + 64, ch * 1024 + q0: ch * 1024 + q0 + nq],
                        in_=PV[s][p0:p0 + 64, 0:nq], func=ACTF.Copy, bias=0.0,
                        scale=1.0))
            bk.did_read(BK_PV[s], "scalar", None)

        qk_i = 0
        for i in range(NB + 6):
            if 1 <= i <= NB:
                ctx_prev = ctx[i - 1]
            if 1 <= i <= NB and i < 2:
                pass
            if 1 <= i <= NB:
                S1(i - 1)
            if 2 <= i <= NB + 1:
                S2sum(i - 2)
            if 1 <= i <= NB and ctx[i - 1]["units"][0][0] in ("s0", "last"):
                if i < 2:
                    em.op("vector", lambda e: e.drain())
                S1tt(i - 1)
                if i < 3:
                    em.op("vector", lambda e: e.drain())
                S1rmax(i - 1)
            elif 1 <= i <= NB:
                if i < 2:
                    em.op("vector", lambda e: e.drain())
                S1rmax(i - 1)
            if 3 <= i <= NB + 2 and ctx[i - 3]["units"][0][0] == "main":
                S2corr(i - 3)
            evA = em.fence("vector", "v") if i <= NB + 2 else None
            incA[i] = evA
            ops_after_recip = 0
            if 3 <= i <= NB + 2:
                if evA is None:
                    em.op("vector", lambda e: e.drain())
                S2recip(i - 3)
            if 4 <= i <= NB + 3:
                S3ev(i - 4)
                ops_after_recip = 1
            if 3 <= i <= NB + 2:
                S2pnorm(i - 3, need_drain=(ops_after_recip == 0))
            evB = em.fence("vector", "v") if i >= 3 else None
            incB[i] = evB
            if 1 <= i <= NB:
                Sexp(i - 1, evA)
            if 5 <= i <= NB + 4:
                S4(i - 5)
            evC = em.fence("scalar", "a") if i >= 1 else None
            incC[i] = evC
            if i < NB:
                ctx[i] = mkctx(i)
                S0(i, evA)
            if 3 <= i <= NB + 2:
                S3tr(i - 3, evB)
            if 4 <= i <= NB + 3:
                S3pv(i - 4, evB, evC)
            while qk_i < len(qkq) and qkq[qk_i][0] <= i:
                _, pnext, gidx = qkq[qk_i]
                kind = gidx // len(nsl)
                off, n = nsl[gidx % len(nsl)]
                qk_group(tile, pnext, kind, off, n, gidx % len(nsl) == len(nsl) - 1)
                qk_i += 1
        state["attn_last_ev"] = incC[NB + 5]

    def proj_phase(tile):
        base = tile * TILE
        em.wait("tensor", state.get("attn_last_ev"))
        if tile == 1:
            em.wait("tensor", state.get("sliver_ev"))
        for n in range(4):
            sl = n % 2
            load_wb(sl, wp_d[n])
            ev_w = state["wb"][sl]
            for t in range(8):
                pm = (n * 8 + t) % 2
                pmb = BK_PM[pm]
                bk.write("tensor", pmb)
                em.wait("tensor", ev_w)
                for c in range(DC):
                    em.op("tensor", lambda e, c=c, t=t, sl=sl, pm=pm: e.matmul(
                        PM[pm][0:128, 0:512],
                        lhsT=ATT[:, c * 1024 + t * 128: c * 1024 + (t + 1) * 128],
                        rhs=WB[:, sl * DC * 512 + c * 512: sl * DC * 512 + (c + 1) * 512],
                        start=(c == 0), stop=False))
                ev_mm = em.op("tensor", lambda e, n=n, pm=pm: e.matmul(
                    PM[pm][0:128, 0:512], lhsT=ONE[0:1, 0:128],
                    rhs=PBB[0:1, n * 512:(n + 1) * 512],
                    start=False, stop=True), inc="t")
                bk.wrote(pmb, ev_mm)
                state["wb_cons"][sl] = ev_mm
                xs = t % 2
                em.wait("sync", state["xsl_cons"][xs])
                ev_x = em.op("sync", lambda e, t=t, n=n, xs=xs, base=base: e.dma_start(
                    out=XSL[:, xs * 512:(xs + 1) * 512],
                    in_=x_d[base + t * 128: base + (t + 1) * 128, n * 512:(n + 1) * 512]),
                    inc="xd", dma=True)
                em.waits("vector", [ev_mm, ev_x])
                em.wait("vector", state["xps_cons"][xs])
                ev_xp = em.op("vector", lambda e, xs=xs, pm=pm: e.tensor_tensor(
                    out=XPS[:, xs * 512:(xs + 1) * 512],
                    in0=PM[pm][0:128, 0:512], in1=XSL[:, xs * 512:(xs + 1) * 512],
                    op=ALU.add), inc="v")
                bk.did_read(pmb, "vector", ev_xp)
                state["xsl_cons"][xs] = ev_xp
                em.wait("scalar", ev_xp)
                ev_o = em.op("scalar", lambda e, t=t, n=n, xs=xs, base=base: e.dma_start(
                    out=y_d[base + t * 128: base + (t + 1) * 128, n * 512:(n + 1) * 512],
                    in_=XPS[:, xs * 512:(xs + 1) * 512]), inc="od", dma=True)
                state["xps_cons"][xs] = ev_o
                if n == 3:
                    state["y_written"][tile * 8 + t] = ev_o

    def ffn_mlp(tile, which):
        src, scw = (XNT, CW) if which == 0 else (ATT, 1024)
        dst = ATT if which == 0 else VB
        w_dd = w1_d if which == 0 else w3_d

        ev_e = None
        for m in range(16):
            wsl = m % 2
            load_wa(wsl, w_dd[m])
            ev_w = state["wa"][wsl]
            for n in range(2):
                pm = (m * 2 + n) % 2
                pmb = BK_PM[pm]
                bk.write("tensor", pmb)
                em.wait("tensor", ev_w)
                if which == 0:
                    em.wait("tensor", state["ln2_chunk_evs"][4 * n + 3])
                ev_mm = None
                for c in range(DC):
                    ev_mm = em.op("tensor", lambda e, c=c, n=n, wsl=wsl, pm=pm, src=src, scw=scw: e.matmul(
                        PM[pm][0:128, 0:512],
                        lhsT=WA[:, wsl * 2048 + c * 128: wsl * 2048 + (c + 1) * 128],
                        rhs=src[:, c * scw + n * 512: c * scw + n * 512 + 512],
                        start=(c == 0), stop=(c == DC - 1)),
                        inc=("t" if c == DC - 1 else None))
                bk.wrote(pmb, ev_mm)
                state["wa_cons"][wsl] = ev_mm
                em.wait("scalar", ev_mm)
                ev_e = em.op("scalar", lambda e, m=m, n=n, pm=pm, dst=dst: e.activation(
                    out=dst[:, m * 1024 + n * 512: m * 1024 + (n + 1) * 512],
                    in_=PM[pm][0:128, 0:512], func=ACTF.Silu, bias=ZAP[0:128, 0:1], scale=1.0),
                    inc="a")
                bk.did_read(pmb, "scalar", ev_e)
        state["mlp_ev"] = ev_e

    def ffn_out(tile):
        sc_ps = [None, None, PV[0], PV[1]]
        sc_bk = [BK_PS[0], BK_PS[1], BK_PV[0], BK_PV[1]]
        ev_h3_gate = state.get("h1_mm_last")
        ev_acc = None
        for n in range(4):
            sl = n % 2
            load_wb(sl, w2_d[n])
            ev_w = state["wb"][sl]
            bk.write("tensor", sc_bk[n])
            ev_s = None
            for t in range(8):
                pm = (n * 8 + t) % 2
                pmb = BK_PM[pm]
                bk.write("tensor", pmb)
                em.wait("tensor", ev_w)
                ev_mm = None
                for c in range(DC):
                    ev_mm = em.op("tensor", lambda e, c=c, t=t, sl=sl, pm=pm: e.matmul(
                        PM[pm][0:128, 0:512],
                        lhsT=VB[:, c * 1024 + t * 128: c * 1024 + (t + 1) * 128],
                        rhs=WB[:, sl * DC * 512 + c * 512: sl * DC * 512 + (c + 1) * 512],
                        start=(c == 0), stop=(c == DC - 1)),
                        inc=("t" if c == DC - 1 else None))
                bk.wrote(pmb, ev_mm)
                state["wb_cons"][sl] = ev_mm
                em.wait("scalar", ev_mm)
                em.wait("scalar", ev_h3_gate)
                ev_h3 = em.op("scalar", lambda e, t=t, n=n, pm=pm: e.activation(
                    out=XNT[:, t * 2048 + n * 512: t * 2048 + (n + 1) * 512],
                    in_=PM[pm][0:128, 0:512], func=ACTF.Copy, bias=0.0, scale=1.0),
                    inc="a")
                bk.did_read(pmb, "scalar", ev_h3)
                em.waits("tensor", [ev_h3, state.get("gates_ev")])
                ev_s = em.op("tensor", lambda e, t=t, n=n: e.matmul(
                    (PSW[0:8, 512 * n:512 * (n + 1)] if n < 2 else sc_ps[n][0:8, 0:512]),
                    lhsT=GT[:, t * 8:(t + 1) * 8],
                    rhs=XNT[:, t * 2048 + n * 512: t * 2048 + (n + 1) * 512],
                    start=(t == 0), stop=(t == 7)), inc="t")
            bk.wrote(sc_bk[n], ev_s)
            em.wait("vector", ev_s)
            src_ps = (lambda n=n: PSW[0:8, 512 * n:512 * (n + 1)] if n < 2 else sc_ps[n][0:8, 0:512])()
            if tile == 0:
                ev_acc = em.op("vector", lambda e, n=n, sp=src_ps: e.tensor_copy(
                    FIN[0:8, n * 512:(n + 1) * 512], sp), inc="v")
            else:
                ev_acc = em.op("vector", lambda e, n=n, sp=src_ps: e.tensor_tensor(
                    out=FIN[0:8, n * 512:(n + 1) * 512],
                    in0=FIN[0:8, n * 512:(n + 1) * 512],
                    in1=sp, op=ALU.add), inc="v")
            bk.did_read(sc_bk[n], "vector", ev_acc)
        state["acc_ev"] = ev_acc

    def finale():
        em.wait("gpsimd", state["acc_ev"])
        ev1 = em.op("gpsimd", lambda e: e.dma_start(
            out=cc_i[:, :], in_=FIN[0:8, 0:D]), inc="gd", dma=True)
        em.wait("gpsimd", ev1)
        ev2 = em.op("gpsimd", lambda e: e.collective_compute(
            "AllReduce", ALU.add, replica_groups=[list(range(8))],
            ins=[cc_i[:, :]], outs=[cc_o[:, :]]), inc="g")
        em.wait("gpsimd", ev2)
        em.waits("gpsimd", [state["xsl_cons"][0], state["xsl_cons"][1],
                            state["xps_cons"][0], state["xps_cons"][1]])
        ev3a = em.op("gpsimd", lambda e: e.dma_start(
            out=XSL[0:8, 0:1024], in_=cc_o[:, 0:1024]), inc="gd", dma=True)
        ev3b = em.op("gpsimd", lambda e: e.dma_start(
            out=XPS[0:8, 0:1024], in_=cc_o[:, 1024:2048]), inc="gd", dma=True)
        em.wait("sync", ev1)
        em.wait("sync", state["y_written"][0])
        ev4 = em.op("sync", lambda e: e.dma_start(
            out=FIN[0:8, 0:D], in_=y_d[0:8, :]), inc="xd", dma=True)
        em.waits("vector", [ev3a, ev3b, ev4])
        em.op("vector", lambda e: e.tensor_scalar(
            out=XSL[0:8, 0:1024], in0=XSL[0:8, 0:1024], scalar1=CM[0:8, 0:1],
            scalar2=None, op0=ALU.mult))
        em.op("vector", lambda e: e.tensor_scalar(
            out=XPS[0:8, 0:1024], in0=XPS[0:8, 0:1024], scalar1=CM[0:8, 0:1],
            scalar2=None, op0=ALU.mult))
        em.op("vector", lambda e: e.tensor_tensor(
            out=XSL[0:8, 0:1024], in0=XSL[0:8, 0:1024], in1=FIN[0:8, 0:1024],
            op=ALU.add))
        ev5 = em.op("vector", lambda e: e.tensor_tensor(
            out=XPS[0:8, 0:1024], in0=XPS[0:8, 0:1024], in1=FIN[0:8, 1024:2048],
            op=ALU.add), inc="v")
        em.wait("scalar", ev5)
        em.op("scalar", lambda e: e.dma_start(
            out=y_d[0:8, 0:1024], in_=XSL[0:8, 0:1024]), inc="od", dma=True)
        em.op("scalar", lambda e: e.dma_start(
            out=y_d[0:8, 1024:2048], in_=XPS[0:8, 0:1024]), inc="od", dma=True)
        em.op("scalar", lambda e, v=em.counts["od"]: e.wait_ge(em.sems["od"], v))

    def emit_all(nsteps):
        cnt = [0]
        def step(f):
            if nsteps is None or cnt[0] < nsteps:
                f()
            cnt[0] += 1
        for tile in (0, 1):
            step(lambda tile=tile: ln_phase(tile, is_ln2=False))
            step(lambda tile=tile: v_phase(tile))
            step(lambda tile=tile: qk_attn_phase(tile))
            step(lambda tile=tile: proj_phase(tile))
            step(lambda tile=tile: ln_phase(tile, is_ln2=True))
            def _mlp0(tile=tile):
                ffn_mlp(tile, 0)
                state["h1_mm_last"] = state["wa_cons"][1]
            step(_mlp0)
            def _mlp1(tile=tile):
                ffn_mlp(tile, 1)
                state["ffn_h2_done"] = state["wa_cons"][1]
            step(_mlp1)
            step(lambda tile=tile: ffn_out(tile))
        step(finale)
    emit_all(steps)

    TENS = {"XNT": XNT, "VB": VB, "ATT": ATT, "QKB": QKB, "GT": GT, "T1R": T1R,
            "SLV": SLV, "FIN": FIN, "STT": STT, "RPS": RPS, "XCH": XCH,
            "SST": SST, "PR": PR, "PTSB": PTSB}
    if dbg:
        for s in ("t", "v", "a", "wd", "xd", "od", "gd", "g"):
            em.wait("scalar", (s, em.counts[s]))
        for name in dbg:
            th = TENS[name]
            shp = [th.shape[0], th.shape[1]]
            dt = th.ap().dtype
            dd = nc.dram_tensor("dbg_" + name, shp, dt, kind="ExternalOutput").ap()
            em.op("scalar", lambda e, dd=dd, th=th: e.dma_start(
                out=dd[:, :], in_=th[:, :]), inc="od", dma=True)
        em.op("scalar", lambda e, v=em.counts["od"]: e.wait_ge(em.sems["od"], v))

    with nc.Block() as block:
        @block.sync
        def _(e):
            for f in em.ops["sync"]:
                f(e)

        @block.tensor
        def _(e):
            for f in em.ops["tensor"]:
                f(e)

        @block.vector
        def _(e):
            for f in em.ops["vector"]:
                f(e)

        @block.scalar
        def _(e):
            for f in em.ops["scalar"]:
                f(e)

        @block.gpsimd
        def _(e):
            for f in em.ops["gpsimd"]:
                f(e)

    es.close()
    return nc



def _host_consts():
    mm_main = np.zeros((128, 144), np.float32)
    b01 = np.zeros((128, 144), np.float32)
    for r in range(128):
        for c in range(r, r + 16):
            d = abs(c - r - 8)
            mm_main[r, c] = SCALE * (WIN - d / 2.0) / WIN
            b01[r, c] = 1.0
    mm2 = np.concatenate([mm_main, mm_main], axis=1)
    mm_s0 = np.zeros((8, 16), np.float32)
    am_s0 = np.full((8, 16), NEG, np.float32)
    for r in range(8):
        for c in range(0, r + 8):
            d = abs(c - r)
            mm_s0[r, c] = SCALE * (WIN - d / 2.0) / WIN
            am_s0[r, c] = 0.0
    mm_last = np.zeros((120, 128), np.float32)
    am_last = np.full((120, 128), NEG, np.float32)
    for r in range(120):
        for c in range(r, min(r + 16, 128)):
            d = abs(c - r - 8)
            mm_last[r, c] = SCALE * (WIN - d / 2.0) / WIN
            am_last[r, c] = 0.0
    return mm2, b01, mm_s0, am_s0, mm_last, am_last


def _prep_weights(qkv_w, proj_w, proj_b, w1, w3, w2, wr):
    f = np.float32
    qkv_w = np.asarray(qkv_w, f)
    qk = np.ascontiguousarray(qkv_w[0:4096])
    T = qk.T
    Bq = T.reshape(16, 128, 32, 128).transpose(2, 1, 0, 3)
    order = []
    for p in range(16):
        order += [p, 16 + p]
    wqk = np.ascontiguousarray(Bq[order]).reshape(32, 128, 16 * 128)

    def nslice(w):
        T = np.asarray(w, f).T
        return np.ascontiguousarray(
            T.reshape(16, 128, 4, 512).transpose(2, 1, 0, 3)).reshape(4, 128, 16 * 512)

    wv = nslice(qkv_w[4096:6144])
    wp = nslice(proj_w)
    w2n = nslice(w2)

    def mblk(w):
        T = np.asarray(w, f).T
        return np.ascontiguousarray(
            T.reshape(16, 128, 16, 128).transpose(2, 1, 0, 3)).reshape(16, 128, 16 * 128)

    w1m = mblk(w1)
    w3m = mblk(w3)
    wrh = np.ascontiguousarray(
        np.asarray(wr, f).T.reshape(16, 128, 8).transpose(1, 0, 2)).reshape(128, 128)
    pb = np.ascontiguousarray(np.asarray(proj_b, f).reshape(1, D))
    return wqk, wv, wp, w1m, w3m, w2n, wrh, pb


def kernel(L0_attn_x, L0_attn_qkv_w, L0_attn_proj_w, L0_attn_proj_b,
           L0_ffn_w1, L0_ffn_w3, L0_ffn_w2, L0_ffn_wr):
    x = np.ascontiguousarray(np.asarray(L0_attn_x, np.float32))
    wqk, wv, wp, w1m, w3m, w2n, wrh, pb = _prep_weights(
        L0_attn_qkv_w, L0_attn_proj_w, L0_attn_proj_b, L0_ffn_w1, L0_ffn_w3,
        L0_ffn_w2, L0_ffn_wr)
    mm2, b01, mm_s0, am_s0, mm_last, am_last = _host_consts()
    ident = np.eye(128, dtype=np.float32)
    nc = build_nc()
    in_maps = []
    for c in range(8):
        cm = np.full((8, 1), 1.0 if c == 0 else 0.0, np.float32)
        in_maps.append({
            "x": np.ascontiguousarray(x[c]),
            "wqk": wqk, "wv": wv, "wp": wp, "w1": w1m, "w3": w3m, "w2": w2n,
            "wr": wrh, "pb": pb,
            "mm2": mm2, "b01": b01, "mm_s0": mm_s0,
            "am_s0": am_s0, "mm_last": mm_last, "am_last": am_last,
            "ident": ident, "coremask": cm,
        })
    res = run_bass_kernel_spmd(nc, in_maps, core_ids=list(range(8)))
    return np.stack([np.asarray(res.results[c]["y"]) for c in range(8)], axis=0)


# version 05
# speedup vs baseline: 1.5015x; 1.0017x over previous
import numpy as np
import concourse.bass as bass
import concourse.mybir as mybir
from concourse.bass_utils import run_bass_kernel_spmd
from contextlib import ExitStack

F32 = mybir.dt.float32
BF16 = mybir.dt.bfloat16
ALU = mybir.AluOpType
ACTF = mybir.ActivationFunctionType
AX = mybir.AxisListType

D = 2048
DC = 16
NT = 2048
TILE = 1024
H = 32
HD = 64
CW = 1040
NEG = -1.0e30
EPS = 1e-5
SCALE = HD ** -0.5
WIN = 8

ENGINES = ("sync", "tensor", "vector", "scalar", "gpsimd")


class Em:
    def __init__(self, nc, es):
        self.nc = nc
        self.ops = {e: [] for e in ENGINES}
        self.sems = {}
        self.counts = {}
        self.last_wait = {e: {} for e in ENGINES}
        self._vec_dirty = False
        for name in ("t", "v", "a", "g", "wd", "xd", "od", "gd"):
            self.sems[name] = es.enter_context(nc.semaphore("sem_" + name))
            self.counts[name] = 0

    def fence(self, eng, inc):
        self.counts[inc] += 1
        val = self.counts[inc]
        h = self.sems[inc]
        self.ops[eng].append(lambda e, h=h: e.drain().then_inc(h, 1))
        if eng == "vector":
            self._vec_dirty = False
        return (inc, val)

    def wait(self, eng, ev):
        if ev is None:
            return
        s, val = ev
        if val <= 0:
            return
        if self.last_wait[eng].get(s, 0) >= val:
            return
        self.last_wait[eng][s] = val
        h = self.sems[s]
        self.ops[eng].append(lambda e, h=h, val=val: e.wait_ge(h, val))

    def waits(self, eng, evs):
        for ev in evs:
            self.wait(eng, ev)

    def op(self, eng, fn, inc=None, dma=False, dep=True):
        if eng == "vector":
            if dep and self._vec_dirty:
                self.ops[eng].append(lambda e: e.drain())
            self._vec_dirty = True
        if inc is None:
            self.ops[eng].append(fn)
            return None
        n = 16 if dma else 1
        self.counts[inc] += n
        val = self.counts[inc]
        h = self.sems[inc]
        if not dma and eng in ("vector", "scalar"):
            self.ops[eng].append(fn)
            self.ops[eng].append(lambda e, h=h, n=n: e.drain().then_inc(h, n))
            if eng == "vector":
                self._vec_dirty = False
        else:
            self.ops[eng].append(lambda e, fn=fn, h=h, n=n: fn(e).then_inc(h, n))
        return (inc, val)


class Banks:
    def __init__(self, em):
        self.em = em
        self.w = {}
        self.r = {}

    def write(self, eng, bank):
        for ev in self.r.get(bank, {}).values():
            self.em.wait(eng, ev)
        self.em.wait(eng, self.w.get(bank))

    def wrote(self, bank, ev):
        self.w[bank] = ev
        self.r[bank] = {}

    def read(self, eng, bank):
        self.em.wait(eng, self.w.get(bank))

    def did_read(self, bank, eng, ev):
        self.r.setdefault(bank, {})[eng] = ev


def build_nc(steps=None, dbg=()):
    nc = bass.Bass("TRN2", target_bir_lowering=False)
    es = ExitStack()

    x_d = nc.dram_tensor("x", [NT, D], F32, kind="ExternalInput").ap()
    wqk_d = nc.dram_tensor("wqk", [32, 128, DC * 128], F32, kind="ExternalInput").ap()
    wv_d = nc.dram_tensor("wv", [4, 128, DC * 512], F32, kind="ExternalInput").ap()
    wp_d = nc.dram_tensor("wp", [4, 128, DC * 512], F32, kind="ExternalInput").ap()
    w1_d = nc.dram_tensor("w1", [16, 128, DC * 128], F32, kind="ExternalInput").ap()
    w3_d = nc.dram_tensor("w3", [16, 128, DC * 128], F32, kind="ExternalInput").ap()
    w2_d = nc.dram_tensor("w2", [4, 128, DC * 512], F32, kind="ExternalInput").ap()
    wr_d = nc.dram_tensor("wr", [128, DC * 8], F32, kind="ExternalInput").ap()
    pb_d = nc.dram_tensor("pb", [1, D], F32, kind="ExternalInput").ap()
    mm2_d = nc.dram_tensor("mm2", [128, 288], F32, kind="ExternalInput").ap()
    b01_d = nc.dram_tensor("b01", [128, 144], F32, kind="ExternalInput").ap()
    mms_d = nc.dram_tensor("mm_s0", [8, 16], F32, kind="ExternalInput").ap()
    ams_d = nc.dram_tensor("am_s0", [8, 16], F32, kind="ExternalInput").ap()
    mml_d = nc.dram_tensor("mm_last", [120, 128], F32, kind="ExternalInput").ap()
    aml_d = nc.dram_tensor("am_last", [120, 128], F32, kind="ExternalInput").ap()
    id_d = nc.dram_tensor("ident", [128, 128], F32, kind="ExternalInput").ap()
    cm_d = nc.dram_tensor("coremask", [8, 1], F32, kind="ExternalInput").ap()
    y_d = nc.dram_tensor("y", [NT, D], F32, kind="ExternalOutput").ap()
    cc_i = nc.dram_tensor("cc_i", [8, D], F32)
    cc_o = nc.dram_tensor("cc_o", [8, D], F32)
    dbg_d = {}

    sb = lambda name, shp, dt: es.enter_context(nc.sbuf_tensor(name, shp, dt))
    XNT = sb("XNT", [128, DC * CW], BF16)
    VB = sb("VB", [128, 9 * 2048], BF16)
    ATT = sb("ATT", [128, DC * 1024], BF16)
    SLV = sb("SLV", [128, DC * 8], BF16)
    QKB = sb("QKB", [128, 4 * CW], BF16)
    WA = sb("WA", [128, 2 * 2048], BF16)
    WB = sb("WB", [128, 2 * DC * 512], BF16)
    XCH = sb("XCH", [128, 2 * 2048], F32)
    XBK2 = sb("XBK2", [128, 2 * 128], F32)
    F32R = sb("F32R", [128, 2 * 128], F32)
    XSL = sb("XSL", [128, 2 * 512], F32)
    XPS = sb("XPS", [128, 2 * 512], F32)
    T1R = sb("T1R", [128, 4 * 288], F32)
    PR = sb("PR", [128, 4 * 320], BF16)
    PS0R = sb("PS0R", [32, 32], BF16)
    PTSB = sb("PTSB", [128, 2 * 512], BF16)
    STT = sb("STT", [128, 32], F32)
    SST = sb("SST", [128, 32], F32)
    RPS = sb("RPS", [128, 40], F32)
    GT = sb("GT", [128, 8 * 8], BF16)
    MM2 = sb("MM2", [128, 288], BF16)
    B01 = sb("B01", [128, 144], BF16)
    MMS = sb("MMS", [8, 16], F32)
    AMS = sb("AMS", [8, 16], F32)
    MML = sb("MML", [120, 128], BF16)
    AML = sb("AML", [120, 128], BF16)
    IDB = sb("IDB", [128, 128], BF16)
    IDF = sb("IDF", [128, 128], F32)
    ONE = sb("ONE", [1, 128], BF16)
    PBB = sb("PBB", [1, D], BF16)
    WR = sb("WR", [128, DC * 8], F32)
    CM = sb("CM", [8, 1], F32)
    ZAP = sb("ZAP", [128, 1], F32)
    EPSAP = sb("EPSAP", [128, 1], F32)
    SQSL = sb("SQSL", [128, 512], F32)
    FIN = sb("FIN", [8, D], F32)

    ps = lambda name, shp, dt: es.enter_context(nc.psum_tensor(name, shp, dt))
    PM = [ps("PM0", [128, 512], F32), ps("PM1", [128, 512], F32)]
    PSW = ps("PSW", [128, 1024], F32)
    PV = [ps("PV0", [128, 512], F32), ps("PV1", [128, 512], F32)]
    PT = [ps("PT0", [128, 1024], BF16), ps("PT1", [128, 1024], BF16)]
    BK_PM, BK_PS, BK_PV, BK_PT = (0, 1), ("PSW", "PSW"), (4, 5), (6, 7)
    PS = [PSW, PSW]

    em = Em(nc, es)
    bk = Banks(em)

    ld_evs = []
    for dst, src in ((MMS, mms_d), (AMS, ams_d),
                     (IDF, id_d), (WR, wr_d),
                     (CM, cm_d)):
        ld_evs.append(em.op("sync", lambda e, d=dst, s=src: e.dma_start(
            out=d[:, :], in_=s[:, :]), inc="xd", dma=True))
    ld_evs.append(em.op("gpsimd", lambda e: e.dma_start(
        out=IDB[:, :], in_=id_d[:, :]), inc="wd", dma=True))
    ld_evs.append(em.op("gpsimd", lambda e: e.dma_start(
        out=PBB[:, :], in_=pb_d[:, :]), inc="wd", dma=True))
    ld_evs.append(em.op("gpsimd", lambda e: e.dma_start(
        out=MM2[:, :], in_=mm2_d[:, :]), inc="wd", dma=True))
    ld_evs.append(em.op("gpsimd", lambda e: e.dma_start(
        out=MML[:, :], in_=mml_d[:, :]), inc="wd", dma=True))
    ld_evs.append(em.op("gpsimd", lambda e: e.dma_start(
        out=AML[:, :], in_=aml_d[:, :]), inc="wd", dma=True))
    ld_evs.append(em.op("gpsimd", lambda e: e.dma_start(
        out=B01[:, :], in_=b01_d[:, :]), inc="wd", dma=True))
    em.waits("vector", ld_evs)
    em.op("vector", lambda e: e.memset(ZAP[:, :], 0.0))
    em.op("vector", lambda e: e.memset(EPSAP[:, :], EPS))
    for s_ in range(4):
        for u_ in range(2):
            em.op("vector", lambda e, o=s_ * 320 + u_ * 160: e.memset(
                PR[:, o + 144: o + 160], 0.0))
    em.op("vector", lambda e: e.memset(PS0R[:, :], 0.0))
    em.op("vector", lambda e: e.memset(VB[0:32, 8 * 2048:9 * 2048], 0.0))
    ev_memset = em.op("vector", lambda e: e.memset(ONE[:, :], 1.0), inc="v")
    em.waits("tensor", ld_evs + [ev_memset])
    em.waits("scalar", ld_evs + [ev_memset])

    state = {
        "wa": [None, None], "wb": [None, None],
        "wa_cons": [None, None], "wb_cons": [None, None],
        "xch_cons": [None, None],
        "qk_cons": [None, None, None, None],
        "xsl_cons": [None, None], "xps_cons": [None, None],
        "t1_cons": [None, None], "pr_cons": [None, None],
        "ptm_cons": [None, None], "pth_cons": [None, None],
        "xbk_cons": [None, None], "xbk2_cons": [None, None],
        "f32r_cons": [None, None],
        "y_written": {},
        "ln_ev": {"vector": None, "scalar": None},
        "ln1_chunk_evs": {},
        "ln2_chunk_evs": {},
        "v_done": {"vector": None, "scalar": None},
    }

    def load_wa(slot, src_ap):
        em.wait("gpsimd", state["wa_cons"][slot])
        ev = em.op("gpsimd", lambda e, s=src_ap, sl=slot: e.dma_start(
            out=WA[:, sl * 2048:(sl + 1) * 2048], in_=s,
            max_dma_last_dim=8192), inc="wd", dma=True)
        state["wa"][slot] = ev
        return ev

    def load_wb(slot, src_ap):
        em.wait("gpsimd", state["wb_cons"][slot])
        ev = em.op("gpsimd", lambda e, s=src_ap, sl=slot: e.dma_start(
            out=WB[:, sl * DC * 512:(sl + 1) * DC * 512], in_=s,
            max_dma_last_dim=8192), inc="wd", dma=True)
        state["wb"][slot] = ev
        return ev

    def ln_chunk(tile, i, nrow, rows_src, is_ln2):
        sl = i % 2
        em.wait("sync", state["xch_cons"][sl])
        if is_ln2:
            em.wait("sync", state["y_written"][rows_src // 128])
            src = y_d[rows_src:rows_src + nrow, :]
        else:
            src = x_d[rows_src:rows_src + nrow, :]
        ev_x = em.op("sync", lambda e, s=src, sl=sl, n=nrow: e.dma_start(
            out=XCH[0:n, sl * 2048: sl * 2048 + 2048], in_=s), inc="xd", dma=True)
        em.wait("vector", ev_x)
        em.op("vector", lambda e, sl=sl, n=nrow: e.reduce_sum(
            STT[0:n, 23:24], XCH[0:n, sl * 2048: sl * 2048 + 2048], axis=AX.X))
        em.op("vector", lambda e, n=nrow: e.tensor_scalar(
            out=STT[0:n, 24:25], in0=STT[0:n, 23:24], scalar1=1.0 / D,
            scalar2=None, op0=ALU.mult))
        for q in range(4):
            em.op("vector", lambda e, q=q, sl=sl, n=nrow: e.scalar_tensor_tensor(
                out=SQSL[0:n, 0:512],
                in0=XCH[0:n, sl * 2048 + q * 512: sl * 2048 + (q + 1) * 512],
                scalar=1.0,
                in1=XCH[0:n, sl * 2048 + q * 512: sl * 2048 + (q + 1) * 512],
                op0=ALU.mult, op1=ALU.mult))
            em.op("vector", lambda e, q=q, n=nrow: e.reduce_sum(
                STT[0:n, 16 + q:17 + q], SQSL[0:n, 0:512], axis=AX.X))
        em.op("vector", lambda e, n=nrow: e.tensor_tensor(
            out=STT[0:n, 20:21], in0=STT[0:n, 16:17], in1=STT[0:n, 17:18], op=ALU.add))
        em.op("vector", lambda e, n=nrow: e.tensor_tensor(
            out=STT[0:n, 21:22], in0=STT[0:n, 18:19], in1=STT[0:n, 19:20], op=ALU.add))
        em.op("vector", lambda e, n=nrow: e.tensor_tensor(
            out=STT[0:n, 22:23], in0=STT[0:n, 20:21], in1=STT[0:n, 21:22], op=ALU.add))
        em.op("vector", lambda e, n=nrow: e.tensor_scalar(
            out=STT[0:n, 26:27], in0=STT[0:n, 22:23], scalar1=1.0 / D,
            scalar2=None, op0=ALU.mult))
        em.op("vector", lambda e, n=nrow: e.tensor_tensor(
            out=STT[0:n, 18:19], in0=STT[0:n, 24:25], in1=STT[0:n, 24:25], op=ALU.mult))
        ev_mv = em.op("vector", lambda e, n=nrow: e.tensor_tensor(
            out=STT[0:n, 25:26], in0=STT[0:n, 26:27], in1=STT[0:n, 18:19],
            op=ALU.subtract), inc="v")
        em.wait("scalar", ev_mv)
        ev_sd = em.op("scalar", lambda e, n=nrow: e.activation(
            out=STT[0:n, 27:28], in_=STT[0:n, 25:26], func=ACTF.Sqrt,
            bias=EPSAP[0:n, 0:1], scale=1.0), inc="a")
        em.wait("vector", ev_sd)
        em.op("vector", lambda e, n=nrow: e.reciprocal(
            STT[0:n, 28:29], STT[0:n, 27:28]))
        ev_nb = em.op("vector", lambda e, n=nrow: e.tensor_scalar(
            out=STT[0:n, 29:30], in0=STT[0:n, 24:25], scalar1=STT[0:n, 28:29],
            scalar2=-1.0, op0=ALU.mult, op1=ALU.mult), inc="v")
        em.wait("scalar", ev_nb)

        t0 = i * 128
        ev_nm = None
        chunk_evs = []
        if not is_ln2:
            XN = T1R[:, 0:1024].bitcast(BF16)
            em.wait("scalar", state.get("xn_tr_last"))
            em.wait("scalar", state.get("gates_ev"))
            ev_nm = em.op("scalar", lambda e, XN=XN, n=nrow, sl=sl: e.activation(
                out=XN[0:n, 0:2048],
                in_=XCH[0:n, sl * 2048: sl * 2048 + 2048],
                func=ACTF.Identity, bias=STT[0:n, 29:30], scale=STT[0:n, 28:29]),
                inc="a")
            for c in range(DC):
                bs = c % 2
                ptb = BK_PT[bs]
                bk.write("tensor", ptb)
                em.wait("tensor", ev_nm)
                ev_tr = em.op("tensor", lambda e, bs=bs, c=c, XN=XN, n=nrow: e.transpose(
                    PT[bs][0:128, 0:n],
                    XN[0:n, c * 128: (c + 1) * 128],
                    IDB[0:n, 0:n]), inc="t")
                bk.wrote(ptb, ev_tr)
                state["xn_tr_last"] = ev_tr
                eng = "vector" if c % 2 == 0 else "scalar"
                em.wait(eng, ev_tr)
                if eng == "vector":
                    ev_e = em.op("vector", lambda e, bs=bs, c=c, t0=t0, n=nrow: e.tensor_copy(
                        XNT[:, c * CW + t0: c * CW + t0 + n], PT[bs][0:128, 0:n]),
                        inc="v")
                else:
                    ev_e = em.op("scalar", lambda e, bs=bs, c=c, t0=t0, n=nrow: e.activation(
                        out=XNT[:, c * CW + t0: c * CW + t0 + n],
                        in_=PT[bs][0:128, 0:n], func=ACTF.Copy, bias=0.0, scale=1.0),
                        inc="a")
                bk.did_read(ptb, eng, ev_e)
                state["ln_ev"][eng] = ev_e
                chunk_evs.append(ev_e)
            state["ln1_chunk_evs"][i] = chunk_evs[-2:]
            state["xch_cons"][sl] = ev_nm
            return
        for c in range(DC):
            bs = c % 2
            dst = XBK2
            em.wait("scalar", state["xbk2_cons"][bs])
            ev_nm = em.op("scalar", lambda e, c=c, sl=sl, bs=bs, n=nrow, d=dst: e.activation(
                out=d[0:n, bs * 128:bs * 128 + 128],
                in_=XCH[0:n, sl * 2048 + c * 128: sl * 2048 + (c + 1) * 128],
                func=ACTF.Identity, bias=STT[0:n, 29:30], scale=STT[0:n, 28:29]),
                inc="a")
            if is_ln2:
                pmb = BK_PM[bs]
                bk.write("tensor", pmb)
                em.wait("tensor", ev_nm)
                ev_tr = em.op("tensor", lambda e, bs=bs, n=nrow: e.transpose(
                    PM[bs][0:128, 0:n], XBK2[0:n, bs * 128:bs * 128 + 128],
                    IDF[0:n, 0:n]), inc="t")
                bk.wrote(pmb, ev_tr)
                state["xbk2_cons"][bs] = ev_tr
                em.wait("vector", ev_tr)
                em.wait("vector", state["f32r_cons"][bs])
                ev_f = em.op("vector", lambda e, bs=bs, n=nrow: e.tensor_copy(
                    F32R[:, bs * 128:bs * 128 + n], PM[bs][0:128, 0:n]), inc="v")
                bk.did_read(pmb, "vector", ev_f)
                em.wait("scalar", ev_f)
                ev_cb = em.op("scalar", lambda e, bs=bs, c=c, t0=t0, n=nrow: e.activation(
                    out=XNT[:, c * CW + t0: c * CW + t0 + n],
                    in_=PM[bs][0:128, 0:n], func=ACTF.Copy, bias=0.0, scale=1.0),
                    inc="a")
                bk.did_read(pmb, "scalar", ev_cb)
                state["ln_ev"]["scalar"] = ev_cb
                bk.write("tensor", BK_PS[0])
                em.wait("tensor", ev_f)
                ev_rmm = em.op("tensor", lambda e, bs=bs, c=c, i=i, n=nrow: e.matmul(
                    PSW[0:n, 8 * i: 8 * i + 8],
                    lhsT=F32R[:, bs * 128:bs * 128 + n],
                    rhs=WR[:, 8 * c:8 * c + 8],
                    start=(c == 0), stop=(c == DC - 1)), inc="t")
                state["f32r_cons"][bs] = ev_rmm
                if c == DC - 1:
                    bk.wrote(BK_PS[0], ev_rmm)
        state["ln2_chunk_evs"][i] = ev_cb
        state["xch_cons"][sl] = ev_nm

    def ln_phase(tile, is_ln2):
        base = tile * TILE
        ntok = 1040 if (tile == 0 and not is_ln2) else 1024
        nch = (ntok + 127) // 128
        for i in range(nch):
            nrow = min(128, ntok - i * 128)
            ln_chunk(tile, i, nrow, base + i * 128, is_ln2)
        if is_ln2:
            router_post(tile)

    def router_post(tile):
        RL = T1R
        bk.read("vector", BK_PS[0])
        em.wait("vector", state["t1_cons"][0])
        em.wait("vector", state["t1_cons"][1])
        ev_rl = em.op("vector", lambda e: e.tensor_copy(
            RL[:, 0:128], PSW[0:128, 0:128]), inc="v")
        bk.did_read(BK_PS[0], "vector", ev_rl)
        ev_g = None
        for i in range(8):
            em.op("vector", lambda e, i=i: e.reduce_max(
                RPS[:, 32:33], RL[:, 8 * i:8 * i + 8], axis=AX.X, negate=True))
            ev1 = em.op("vector", lambda e, i=i: e.tensor_scalar(
                out=RPS[:, 0:8], in0=RL[:, 8 * i:8 * i + 8], scalar1=RPS[:, 32:33],
                scalar2=None, op0=ALU.add), inc="v")
            em.wait("scalar", ev1)
            ev2 = em.op("scalar", lambda e: e.activation(
                out=RPS[:, 0:8], in_=RPS[:, 0:8], func=ACTF.Exp, bias=ZAP[:, 0:1],
                scale=1.0), inc="a")
            em.wait("vector", ev2)
            em.op("vector", lambda e: e.reduce_sum(
                RPS[:, 33:34], RPS[:, 0:8], axis=AX.X))
            em.op("vector", lambda e: e.reciprocal(RPS[:, 34:35], RPS[:, 33:34]))
            em.op("vector", lambda e: e.tensor_scalar(
                out=RPS[:, 0:8], in0=RPS[:, 0:8], scalar1=RPS[:, 34:35],
                scalar2=None, op0=ALU.mult))
            em.op("vector", lambda e: e.max(RPS[:, 8:16], RPS[:, 0:8]))
            em.op("vector", lambda e: e.tensor_scalar(
                out=RPS[:, 16:24], in0=RPS[:, 0:8], scalar1=RPS[:, 9:10],
                scalar2=None, op0=ALU.is_ge))
            em.op("vector", lambda e: e.tensor_tensor(
                out=RPS[:, 35:36], in0=RPS[:, 8:9], in1=RPS[:, 9:10], op=ALU.add))
            em.op("vector", lambda e: e.reciprocal(RPS[:, 36:37], RPS[:, 35:36]))
            ev_g = em.op("vector", lambda e, i=i: e.scalar_tensor_tensor(
                out=GT[:, 8 * i:8 * i + 8], in0=RPS[:, 16:24], scalar=RPS[:, 36:37],
                in1=RPS[:, 0:8], op0=ALU.mult, op1=ALU.mult), inc="v")
        state["gates_ev"] = ev_g
        state["rl_ev"] = ev_rl

    def v_phase(tile):
        ntok = 1040 if tile == 0 else 1024
        nj = 9 if tile == 0 else 8
        ev_e = None
        for n in range(4):
            sl = n % 2
            load_wb(sl, wv_d[n])
            ev_w = state["wb"][sl]
            for j in range(nj):
                nrow = min(128, ntok - j * 128)
                pm = (n * 9 + j) % 2
                pmb = BK_PM[pm]
                bk.write("tensor", pmb)
                em.wait("tensor", ev_w)
                em.waits("tensor", state["ln1_chunk_evs"][j])
                ev_mm = None
                for c in range(DC):
                    ev_mm = em.op("tensor", lambda e, c=c, j=j, sl=sl, pm=pm, nr=nrow: e.matmul(
                        PM[pm][0:nr, 0:512],
                        lhsT=XNT[:, c * CW + j * 128: c * CW + j * 128 + nr],
                        rhs=WB[:, sl * DC * 512 + c * 512: sl * DC * 512 + (c + 1) * 512],
                        start=(c == 0), stop=(c == DC - 1)),
                        inc=("t" if c == DC - 1 else None))
                bk.wrote(pmb, ev_mm)
                state["wb_cons"][sl] = ev_mm
                eng = "vector" if j % 2 == 0 else "scalar"
                em.wait(eng, ev_mm)
                if eng == "vector":
                    ev_e = em.op("vector", lambda e, j=j, n=n, pm=pm, nr=nrow: e.tensor_copy(
                        VB[0:nr, j * 2048 + n * 512: j * 2048 + (n + 1) * 512],
                        PM[pm][0:nr, 0:512]), inc="v")
                else:
                    ev_e = em.op("scalar", lambda e, j=j, n=n, pm=pm, nr=nrow: e.activation(
                        out=VB[0:nr, j * 2048 + n * 512: j * 2048 + (n + 1) * 512],
                        in_=PM[pm][0:nr, 0:512], func=ACTF.Copy, bias=0.0,
                        scale=1.0), inc="a")
                bk.did_read(pmb, eng, ev_e)
                state["v_done"][eng] = ev_e

    def units_for_tile(tile):
        if tile == 0:
            us = [("s0", 0, 8, 0, 16, 0)]
            us += [("main", 8 + 128 * j, 128, 128 * j, 144, j) for j in range(8)]
        else:
            us = [("main", 8 + 128 * j, 128, 128 * j, 144, j) for j in range(7)]
            us.append(("last", 904, 120, 896, 128, 7))
        return us

    def qk_group(tile, pair, kind, off, n, last):
        sl = pair % 2
        ring = kind * 2 + sl
        blk = 2 * pair + kind
        wsl = blk % 2
        if off == 0:
            load_wa(wsl, wqk_d[blk])
        ev_w = state["wa"][wsl]
        state["qk_pm"] = (state.get("qk_pm", 0) + 1) % 2
        pm = state["qk_pm"]
        pmb = BK_PM[pm]
        bk.write("tensor", pmb)
        em.wait("tensor", ev_w)
        ev_mm = None
        for c in range(DC):
            ev_mm = em.op("tensor", lambda e, c=c, off=off, n=n, wsl=wsl, pm=pm: e.matmul(
                PM[pm][0:128, 0:n],
                lhsT=WA[:, wsl * 2048 + c * 128: wsl * 2048 + (c + 1) * 128],
                rhs=XNT[:, c * CW + off: c * CW + off + n],
                start=(c == 0), stop=(c == DC - 1)),
                inc=("t" if c == DC - 1 else None))
        bk.wrote(pmb, ev_mm)
        state["wa_cons"][wsl] = ev_mm
        eng = "vector" if kind == 0 else "scalar"
        em.wait(eng, ev_mm)
        em.wait(eng, state["qk_cons"][ring])
        if eng == "vector":
            ev_e = em.op("vector", lambda e, ring=ring, off=off, n=n, pm=pm: e.tensor_copy(
                QKB[:, ring * CW + off: ring * CW + off + n], PM[pm][0:128, 0:n]),
                inc="v")
        else:
            ev_e = em.op("scalar", lambda e, ring=ring, off=off, n=n, pm=pm: e.activation(
                out=QKB[:, ring * CW + off: ring * CW + off + n],
                in_=PM[pm][0:128, 0:n], func=ACTF.Copy, bias=0.0, scale=1.0),
                inc="a")
        bk.did_read(pmb, eng, ev_e)
        if last:
            state.setdefault("qk_ready", {})[(pair, kind)] = ev_e

    def head_bundles(tile, head):
        import os as _os3
        solo_last = int(_os3.environ.get("K_SOLO_LAST", "0"))
        us = units_for_tile(tile)
        bs, i = [], 0
        while i < len(us):
            can_pair = (us[i][0] == "main" and i + 1 < len(us)
                        and us[i + 1][0] == "main")
            if solo_last and tile == 0 and us[i][5] == 6 and us[i][0] == "main":
                can_pair = False
            if can_pair:
                bs.append((head, [us[i], us[i + 1]]))
                i += 2
            else:
                bs.append((head, [us[i]]))
                i += 1
        return bs

    def qk_attn_phase(tile):
        ncols = 1040 if tile == 0 else 1024
        nsl = [(0, 512), (512, 512), (1024, 16)] if ncols == 1040 else [(0, 512), (512, 512)]
        if tile == 1:
            for c in range(DC):
                ev_sv = em.op("vector", lambda e, c=c: e.tensor_copy(
                    ATT[:, c * 1024: c * 1024 + 8], SLV[:, c * 8: c * 8 + 8]),
                    inc="v", dep=(c == 0))
            state["sliver_ev"] = ev_sv
        for kind in (0, 1):
            for gi, (off, n) in enumerate(nsl):
                qk_group(tile, 0, kind, off, n, gi == len(nsl) - 1)

        bundles = []
        for pair in range(16):
            for hh in range(2):
                bundles += head_bundles(tile, 2 * pair + hh)
        import os as _os
        per_pair = len(bundles) // 16
        _lim = int(_os.environ.get("K_DBG_NBUND", "0"))
        if _lim:
            bundles = bundles[:_lim]
        NB = len(bundles)
        qkq = []
        maxpair = (NB - 1) // per_pair if _lim else 15
        import os as _os2
        _noil = int(_os2.environ.get("K_NO_QK_INTERLEAVE", "0"))
        for pair in range(min(15, maxpair)):
            ngr = 2 * len(nsl)
            for g in range(ngr):
                pos = pair * per_pair + (0 if _noil else (g * per_pair) // ngr)
                qkq.append((pos, pair + 1, g))
        ctx = [None] * NB
        incA = {}
        incB = {}
        incC = {}

        def mkctx(bi):
            head, us = bundles[bi]
            c = dict(head=head, units=us, s=bi % 2, s4=bi % 4, par=head % 2,
                     pair=head // 2, dvc=head * 64, nu=len(us))
            c["p0"] = 64 * c["par"]
            return c

        def S0(bi, evA):
            c = ctx[bi]
            em.wait("tensor", evA)
            em.waits("tensor", [state["qk_ready"][(c["pair"], 0)],
                                state["qk_ready"][(c["pair"], 1)]])
            bk.write("tensor", "PSW")
            rsl = c["pair"] % 2
            p0 = c["p0"]
            ev = None
            for ui, (kind, q0, nq, k0, nk, vj) in enumerate(c["units"]):
                qr = QKB[p0:p0 + 64, rsl * CW + q0: rsl * CW + q0 + nq]
                kr = QKB[p0:p0 + 64, (2 + rsl) * CW + k0: (2 + rsl) * CW + k0 + nk]
                ev = em.op("tensor", lambda e, qr=qr, kr=kr, ui=ui, nq=nq, nk=nk: e.matmul(
                    PSW[0:nq, ui * 512: ui * 512 + nk], lhsT=qr, rhs=kr,
                    start=True, stop=True),
                    inc=("t" if ui == c["nu"] - 1 else None))
            c["ev_sc"] = ev
            bk.wrote("PSW", ev)
            state["qk_cons"][rsl] = ev
            state["qk_cons"][2 + rsl] = ev

        def S1(bi):
            c = ctx[bi]
            s = c["s"]
            s4 = c["s4"]
            em.wait("vector", c["ev_sc"])
            em.wait("vector", incC.get(bi - 1))
            if c["nu"] == 2:
                pv = PSW[0:128, 0:1024].rearrange("p (u x) -> p u x", u=2)[:, :, 0:144]
                em.op("vector", lambda e, pv=pv, s=s: e.scalar_tensor_tensor(
                    out=T1R[:, s4 * 288: s4 * 288 + 288], in0=pv, scalar=1.0,
                    in1=MM2[:, 0:288], op0=ALU.mult, op1=ALU.mult), dep=False)
            else:
                kind, q0, nq, k0, nk, vj = c["units"][0]
                mm = {"main": MM2, "s0": MMS, "last": MML}[kind]
                em.op("vector", lambda e, s=s, s4=s4, nq=nq, nk=nk, mm=mm: e.scalar_tensor_tensor(
                    out=T1R[0:nq, s4 * 288: s4 * 288 + nk], in0=PSW[0:nq, 0:nk],
                    scalar=1.0, in1=mm[0:nq, 0:nk], op0=ALU.mult, op1=ALU.mult),
                    dep=False)
            bk.did_read("PSW", "vector", None)

        def S1tt(bi):
            c = ctx[bi]
            s = c["s"]
            s4 = c["s4"]
            kind, q0, nq, k0, nk, vj = c["units"][0]
            am = {"s0": AMS, "last": AML}[kind]
            em.op("vector", lambda e, s=s, s4=s4, nq=nq, nk=nk, am=am: e.tensor_tensor(
                out=T1R[0:nq, s4 * 288: s4 * 288 + nk],
                in0=T1R[0:nq, s4 * 288: s4 * 288 + nk],
                in1=am[0:nq, 0:nk], op=ALU.add), dep=False)

        def S1rmax(bi):
            c = ctx[bi]
            s = c["s"]
            s4 = c["s4"]
            if c["nu"] == 2:
                tv = T1R[:, s4 * 288:(s4 + 1) * 288].rearrange("p (u x) -> p u x", u=2)
                em.op("vector", lambda e, tv=tv, s=s: e.reduce_max(
                    SST[:, s4 * 8: s4 * 8 + 2], tv, axis=AX.X, negate=True), dep=False)
            else:
                kind, q0, nq, k0, nk, vj = c["units"][0]
                em.op("vector", lambda e, s=s, s4=s4, nq=nq, nk=nk: e.reduce_max(
                    SST[0:nq, s4 * 8: s4 * 8 + 1], T1R[0:nq, s4 * 288: s4 * 288 + nk],
                    axis=AX.X, negate=True), dep=False)

        def S2sum(bi):
            c = ctx[bi]
            s = c["s"]
            s4 = c["s4"]
            em.wait("vector", incC.get(bi + 1))
            if c["nu"] == 2:
                tv = T1R[:, s4 * 288:(s4 + 1) * 288].rearrange("p (u x) -> p u x", u=2)
                em.op("vector", lambda e, tv=tv, s=s: e.reduce_sum(
                    SST[:, s4 * 8 + 2: s4 * 8 + 4], tv, axis=AX.X), dep=False)
            else:
                kind, q0, nq, k0, nk, vj = c["units"][0]
                em.op("vector", lambda e, s=s, s4=s4, nq=nq, nk=nk: e.reduce_sum(
                    SST[0:nq, s4 * 8 + 2: s4 * 8 + 3], T1R[0:nq, s4 * 288: s4 * 288 + nk],
                    axis=AX.X), dep=False)

        def S2corr(bi):
            c = ctx[bi]
            s = c["s"]
            s4 = c["s4"]
            for ui, (kind, q0, nq, k0, nk, vj) in enumerate(c["units"]):
                em.op("vector", lambda e, s=s, s4=s4, ui=ui: e.scalar_tensor_tensor(
                    out=SST[:, s4 * 8 + 4 + ui: s4 * 8 + 5 + ui],
                    in0=T1R[:, s4 * 288 + ui * 144 + 143: s4 * 288 + ui * 144 + 144],
                    scalar=-128.0, in1=SST[:, s4 * 8 + 2 + ui: s4 * 8 + 3 + ui],
                    op0=ALU.mult, op1=ALU.add), dep=False)

        def S2recip(bi):
            c = ctx[bi]
            s = c["s"]
            s4 = c["s4"]
            main = c["units"][0][0] == "main"
            base = s4 * 8 + (4 if main else 2)
            nq = 128 if c["nu"] == 2 else c["units"][0][2]
            em.op("vector", lambda e, s=s, s4=s4, base=base, nu=c["nu"], nq=nq: e.reciprocal(
                SST[0:nq, s4 * 8 + 6: s4 * 8 + 6 + nu], SST[0:nq, base: base + nu]),
                dep=False)

        def S2pnorm(bi, need_drain):
            c = ctx[bi]
            s = c["s"]
            s4 = c["s4"]
            em.wait("vector", (ctx[bi - 2].get("ev_tr") if bi >= 2 else None))
            if c["units"][0][0] == "s0":
                em.wait("vector", state.get("s0_tr_ev"))
            for ui, (kind, q0, nq, k0, nk, vj) in enumerate(c["units"]):
                if kind == "main":
                    em.op("vector", lambda e, s4=s4, ui=ui, nq=nq: e.scalar_tensor_tensor(
                        out=PR[0:nq, s4 * 320 + ui * 160: s4 * 320 + ui * 160 + 144],
                        in0=T1R[0:nq, s4 * 288 + ui * 144: s4 * 288 + ui * 144 + 144],
                        scalar=SST[0:nq, s4 * 8 + 6 + ui: s4 * 8 + 7 + ui],
                        in1=B01[0:nq, 0:144], op0=ALU.mult, op1=ALU.mult),
                        dep=(need_drain and ui == 0))
                elif kind == "s0":
                    em.op("vector", lambda e, s=s, s4=s4: e.tensor_scalar(
                        out=PS0R[0:8, 0:16], in0=T1R[0:8, s4 * 288: s4 * 288 + 16],
                        scalar1=SST[0:8, s4 * 8 + 6: s4 * 8 + 7], scalar2=None,
                        op0=ALU.mult), dep=need_drain)
                else:
                    em.op("vector", lambda e, s=s, s4=s4, nq=nq, nk=nk: e.tensor_scalar(
                        out=PR[0:nq, s4 * 320: s4 * 320 + nk],
                        in0=T1R[0:nq, s4 * 288: s4 * 288 + nk],
                        scalar1=SST[0:nq, s4 * 8 + 6: s4 * 8 + 7], scalar2=None,
                        op0=ALU.mult), dep=need_drain)

        def Sexp(bi, ev):
            c = ctx[bi]
            s = c["s"]
            s4 = c["s4"]
            em.wait("scalar", ev)
            for ui, (kind, q0, nq, k0, nk, vj) in enumerate(c["units"]):
                em.op("scalar", lambda e, s=s, s4=s4, ui=ui, nq=nq, nk=nk: e.activation(
                    out=T1R[0:nq, s4 * 288 + ui * 144: s4 * 288 + ui * 144 + nk],
                    in_=T1R[0:nq, s4 * 288 + ui * 144: s4 * 288 + ui * 144 + nk],
                    func=ACTF.Exp, bias=SST[0:nq, s4 * 8 + ui: s4 * 8 + ui + 1],
                    scale=1.0))

        def S3tr(bi, ev):
            c = ctx[bi]
            s = c["s"]
            s4 = c["s4"]
            ptb = BK_PT[s]
            em.wait("tensor", ev)
            bk.write("tensor", ptb)
            evt = None
            for ui, (kind, q0, nq, k0, nk, vj) in enumerate(c["units"]):
                if kind == "s0":
                    evt = em.op("tensor", lambda e, s=s, s4=s4: e.transpose(
                        PT[s][0:32, 0:32], PS0R[0:32, 0:32], IDB[0:32, 0:32]),
                        inc="t")
                    state["s0_tr_ev"] = evt
                elif kind == "last":
                    evt = em.op("tensor", lambda e, s=s, s4=s4: e.transpose(
                        PT[s][0:128, 0:128], PR[:, s4 * 320: s4 * 320 + 128],
                        IDB[:, :]), inc="t")
                else:
                    em.op("tensor", lambda e, s=s, s4=s4, ui=ui: e.transpose(
                        PT[s][0:128, ui * 256: ui * 256 + 128],
                        PR[:, s4 * 320 + ui * 160: s4 * 320 + ui * 160 + 128],
                        IDB[:, :]))
                    evt = em.op("tensor", lambda e, s=s, s4=s4, ui=ui: e.transpose(
                        PT[s][0:32, ui * 256 + 128: ui * 256 + 256],
                        PR[:, s4 * 320 + ui * 160 + 128: s4 * 320 + ui * 160 + 160],
                        IDB[:, :]), inc="t")
            c["ev_tr"] = evt
            bk.wrote(ptb, evt)

        def S3ev(bi):
            c = ctx[bi]
            s = c["s"]
            s4 = c["s4"]
            em.wait("vector", c["ev_tr"])
            em.wait("vector", (ctx[bi - 2]["ev_pv"] if bi >= 2 else None))
            kind = c["units"][0][0]
            if kind == "s0":
                em.op("vector", lambda e, s=s, s4=s4: e.tensor_copy(
                    PTSB[0:32, s * 512: s * 512 + 32], PT[s][0:32, 0:32]), dep=False)
            elif kind == "last":
                em.op("vector", lambda e, s=s, s4=s4: e.tensor_copy(
                    PTSB[0:128, s * 512: s * 512 + 128], PT[s][0:128, 0:128]),
                    dep=False)
            else:
                w = 256 * c["nu"]
                em.op("vector", lambda e, s=s, s4=s4, w=w: e.tensor_copy(
                    PTSB[0:128, s * 512: s * 512 + w], PT[s][0:128, 0:w]), dep=False)
            bk.did_read(BK_PT[s], "vector", None)

        def S3pv(bi, evB, evC):
            c = ctx[bi]
            s = c["s"]
            s4 = c["s4"]
            pvb = BK_PV[s]
            em.waits("tensor", [evB, evC,
                                state["v_done"]["vector"], state["v_done"]["scalar"]])
            bk.write("tensor", pvb)
            p0 = c["p0"]
            dvc = c["dvc"]
            tp = (0, p0)
            ev = None
            for ui, (kind, q0, nq, k0, nk, vj) in enumerate(c["units"]):
                if kind == "s0":
                    ev = em.op("tensor", lambda e, s=s, s4=s4, p0=p0, dvc=dvc, tp=tp: e.matmul(
                        PV[s][p0:p0 + 64, 0:8],
                        lhsT=VB[0:32, dvc: dvc + 64],
                        rhs=PTSB[0:32, s * 512: s * 512 + 8],
                        start=True, stop=True, tile_position=tp), inc="t")
                elif kind == "last":
                    ev = em.op("tensor", lambda e, s=s, s4=s4, p0=p0, dvc=dvc, vj=vj, nq=nq, tp=tp: e.matmul(
                        PV[s][p0:p0 + 64, 0:nq],
                        lhsT=VB[:, vj * 2048 + dvc: vj * 2048 + dvc + 64],
                        rhs=PTSB[0:128, s * 512: s * 512 + nq],
                        start=True, stop=True, tile_position=tp), inc="t")
                else:
                    em.op("tensor", lambda e, s=s, s4=s4, ui=ui, p0=p0, dvc=dvc, vj=vj, tp=tp: e.matmul(
                        PV[s][p0:p0 + 64, ui * 128: ui * 128 + 128],
                        lhsT=VB[:, vj * 2048 + dvc: vj * 2048 + dvc + 64],
                        rhs=PTSB[0:128, s * 512 + ui * 256: s * 512 + ui * 256 + 128],
                        start=True, stop=False, tile_position=tp))
                    ev = em.op("tensor", lambda e, s=s, s4=s4, ui=ui, p0=p0, dvc=dvc, vj=vj, tp=tp: e.matmul(
                        PV[s][p0:p0 + 64, ui * 128: ui * 128 + 128],
                        lhsT=VB[0:32, (vj + 1) * 2048 + dvc: (vj + 1) * 2048 + dvc + 64],
                        rhs=PTSB[0:32, s * 512 + ui * 256 + 128: s * 512 + ui * 256 + 256],
                        start=False, stop=True, tile_position=tp),
                        inc=("t" if ui == c["nu"] - 1 else None))
            c["ev_pv"] = ev
            bk.wrote(pvb, ev)

        def S4(bi):
            c = ctx[bi]
            s = c["s"]
            s4 = c["s4"]
            em.wait("scalar", c["ev_pv"])
            p0 = c["p0"]
            dv0 = p0
            ch = c["head"] // 2
            us = c["units"]
            if c["nu"] == 2:
                q0 = us[0][1]
                if tile == 0 and us[1][5] == 7:
                    em.op("scalar", lambda e, s=s, s4=s4, p0=p0, dv0=dv0, ch=ch, q0=q0: e.activation(
                        out=ATT[dv0:dv0 + 64, ch * 1024 + q0: ch * 1024 + 1024],
                        in_=PV[s][p0:p0 + 64, 0:248], func=ACTF.Copy, bias=0.0,
                        scale=1.0))
                    em.op("scalar", lambda e, s=s, s4=s4, p0=p0, dv0=dv0, ch=ch: e.activation(
                        out=SLV[dv0:dv0 + 64, ch * 8: ch * 8 + 8],
                        in_=PV[s][p0:p0 + 64, 248:256], func=ACTF.Copy, bias=0.0,
                        scale=1.0))
                else:
                    em.op("scalar", lambda e, s=s, s4=s4, p0=p0, dv0=dv0, ch=ch, q0=q0: e.activation(
                        out=ATT[dv0:dv0 + 64, ch * 1024 + q0: ch * 1024 + q0 + 256],
                        in_=PV[s][p0:p0 + 64, 0:256], func=ACTF.Copy, bias=0.0,
                        scale=1.0))
            else:
                kind, q0, nq, k0, nk, vj = us[0]
                if tile == 0 and kind == "main" and vj == 7:
                    em.op("scalar", lambda e, s=s, s4=s4, p0=p0, dv0=dv0, ch=ch, q0=q0: e.activation(
                        out=ATT[dv0:dv0 + 64, ch * 1024 + q0: ch * 1024 + 1024],
                        in_=PV[s][p0:p0 + 64, 0:120], func=ACTF.Copy, bias=0.0,
                        scale=1.0))
                    em.op("scalar", lambda e, s=s, s4=s4, p0=p0, dv0=dv0, ch=ch: e.activation(
                        out=SLV[dv0:dv0 + 64, ch * 8: ch * 8 + 8],
                        in_=PV[s][p0:p0 + 64, 120:128], func=ACTF.Copy, bias=0.0,
                        scale=1.0))
                else:
                    em.op("scalar", lambda e, s=s, s4=s4, p0=p0, dv0=dv0, ch=ch, q0=q0, nq=nq: e.activation(
                        out=ATT[dv0:dv0 + 64, ch * 1024 + q0: ch * 1024 + q0 + nq],
                        in_=PV[s][p0:p0 + 64, 0:nq], func=ACTF.Copy, bias=0.0,
                        scale=1.0))
            bk.did_read(BK_PV[s], "scalar", None)

        qk_i = 0
        for i in range(NB + 6):
            if 1 <= i <= NB:
                ctx_prev = ctx[i - 1]
            if 1 <= i <= NB and i < 2:
                pass
            if 1 <= i <= NB:
                S1(i - 1)
            if 2 <= i <= NB + 1:
                S2sum(i - 2)
            if 1 <= i <= NB and ctx[i - 1]["units"][0][0] in ("s0", "last"):
                if i < 2:
                    em.op("vector", lambda e: e.drain())
                S1tt(i - 1)
                if i < 3:
                    em.op("vector", lambda e: e.drain())
                S1rmax(i - 1)
            elif 1 <= i <= NB:
                if i < 2:
                    em.op("vector", lambda e: e.drain())
                S1rmax(i - 1)
            if 3 <= i <= NB + 2 and ctx[i - 3]["units"][0][0] == "main":
                S2corr(i - 3)
            evA = em.fence("vector", "v") if i <= NB + 2 else None
            incA[i] = evA
            ops_after_recip = 0
            if 3 <= i <= NB + 2:
                if evA is None:
                    em.op("vector", lambda e: e.drain())
                S2recip(i - 3)
            if 4 <= i <= NB + 3:
                S3ev(i - 4)
                ops_after_recip = 1
            if 3 <= i <= NB + 2:
                S2pnorm(i - 3, need_drain=(ops_after_recip == 0))
            evB = em.fence("vector", "v") if i >= 3 else None
            incB[i] = evB
            if 1 <= i <= NB:
                Sexp(i - 1, evA)
            if 5 <= i <= NB + 4:
                S4(i - 5)
            evC = em.fence("scalar", "a") if i >= 1 else None
            incC[i] = evC
            if i < NB:
                ctx[i] = mkctx(i)
                S0(i, evA)
            if 3 <= i <= NB + 2:
                S3tr(i - 3, evB)
            if 4 <= i <= NB + 3:
                S3pv(i - 4, evB, evC)
            while qk_i < len(qkq) and qkq[qk_i][0] <= i:
                _, pnext, gidx = qkq[qk_i]
                kind = gidx // len(nsl)
                off, n = nsl[gidx % len(nsl)]
                qk_group(tile, pnext, kind, off, n, gidx % len(nsl) == len(nsl) - 1)
                qk_i += 1
        state["attn_last_ev"] = incC[NB + 5]

    def proj_phase(tile):
        base = tile * TILE
        em.wait("tensor", state.get("attn_last_ev"))
        if tile == 1:
            em.wait("tensor", state.get("sliver_ev"))
        for n in range(4):
            sl = n % 2
            load_wb(sl, wp_d[n])
            ev_w = state["wb"][sl]
            for t in range(8):
                pm = (n * 8 + t) % 2
                pmb = BK_PM[pm]
                bk.write("tensor", pmb)
                em.wait("tensor", ev_w)
                for c in range(DC):
                    em.op("tensor", lambda e, c=c, t=t, sl=sl, pm=pm: e.matmul(
                        PM[pm][0:128, 0:512],
                        lhsT=ATT[:, c * 1024 + t * 128: c * 1024 + (t + 1) * 128],
                        rhs=WB[:, sl * DC * 512 + c * 512: sl * DC * 512 + (c + 1) * 512],
                        start=(c == 0), stop=False))
                ev_mm = em.op("tensor", lambda e, n=n, pm=pm: e.matmul(
                    PM[pm][0:128, 0:512], lhsT=ONE[0:1, 0:128],
                    rhs=PBB[0:1, n * 512:(n + 1) * 512],
                    start=False, stop=True), inc="t")
                bk.wrote(pmb, ev_mm)
                state["wb_cons"][sl] = ev_mm
                xs = t % 2
                em.wait("sync", state["xsl_cons"][xs])
                ev_x = em.op("sync", lambda e, t=t, n=n, xs=xs, base=base: e.dma_start(
                    out=XSL[:, xs * 512:(xs + 1) * 512],
                    in_=x_d[base + t * 128: base + (t + 1) * 128, n * 512:(n + 1) * 512]),
                    inc="xd", dma=True)
                em.waits("vector", [ev_mm, ev_x])
                em.wait("vector", state["xps_cons"][xs])
                ev_xp = em.op("vector", lambda e, xs=xs, pm=pm: e.tensor_tensor(
                    out=XPS[:, xs * 512:(xs + 1) * 512],
                    in0=PM[pm][0:128, 0:512], in1=XSL[:, xs * 512:(xs + 1) * 512],
                    op=ALU.add), inc="v")
                bk.did_read(pmb, "vector", ev_xp)
                state["xsl_cons"][xs] = ev_xp
                em.wait("scalar", ev_xp)
                ev_o = em.op("scalar", lambda e, t=t, n=n, xs=xs, base=base: e.dma_start(
                    out=y_d[base + t * 128: base + (t + 1) * 128, n * 512:(n + 1) * 512],
                    in_=XPS[:, xs * 512:(xs + 1) * 512]), inc="od", dma=True)
                state["xps_cons"][xs] = ev_o
                if n == 3:
                    state["y_written"][tile * 8 + t] = ev_o

    def ffn_mlp(tile, which):
        src, scw = (XNT, CW) if which == 0 else (ATT, 1024)
        dst = ATT if which == 0 else VB
        w_dd = w1_d if which == 0 else w3_d

        ev_e = None
        for m in range(16):
            wsl = m % 2
            load_wa(wsl, w_dd[m])
            ev_w = state["wa"][wsl]
            for n in range(2):
                pm = (m * 2 + n) % 2
                pmb = BK_PM[pm]
                bk.write("tensor", pmb)
                em.wait("tensor", ev_w)
                if which == 0:
                    em.wait("tensor", state["ln2_chunk_evs"][4 * n + 3])
                ev_mm = None
                for c in range(DC):
                    ev_mm = em.op("tensor", lambda e, c=c, n=n, wsl=wsl, pm=pm, src=src, scw=scw: e.matmul(
                        PM[pm][0:128, 0:512],
                        lhsT=WA[:, wsl * 2048 + c * 128: wsl * 2048 + (c + 1) * 128],
                        rhs=src[:, c * scw + n * 512: c * scw + n * 512 + 512],
                        start=(c == 0), stop=(c == DC - 1)),
                        inc=("t" if c == DC - 1 else None))
                bk.wrote(pmb, ev_mm)
                state["wa_cons"][wsl] = ev_mm
                em.wait("scalar", ev_mm)
                ev_e = em.op("scalar", lambda e, m=m, n=n, pm=pm, dst=dst: e.activation(
                    out=dst[:, m * 1024 + n * 512: m * 1024 + (n + 1) * 512],
                    in_=PM[pm][0:128, 0:512], func=ACTF.Silu, bias=ZAP[0:128, 0:1], scale=1.0),
                    inc="a")
                bk.did_read(pmb, "scalar", ev_e)
        state["mlp_ev"] = ev_e

    def ffn_out(tile):
        sc_ps = [None, None, PV[0], PV[1]]
        sc_bk = [BK_PS[0], BK_PS[1], BK_PV[0], BK_PV[1]]
        ev_h3_gate = state.get("h1_mm_last")
        ev_acc = None
        for n in range(4):
            sl = n % 2
            load_wb(sl, w2_d[n])
            ev_w = state["wb"][sl]
            bk.write("tensor", sc_bk[n])
            ev_s = None
            for t in range(8):
                pm = (n * 8 + t) % 2
                pmb = BK_PM[pm]
                bk.write("tensor", pmb)
                em.wait("tensor", ev_w)
                ev_mm = None
                for c in range(DC):
                    ev_mm = em.op("tensor", lambda e, c=c, t=t, sl=sl, pm=pm: e.matmul(
                        PM[pm][0:128, 0:512],
                        lhsT=VB[:, c * 1024 + t * 128: c * 1024 + (t + 1) * 128],
                        rhs=WB[:, sl * DC * 512 + c * 512: sl * DC * 512 + (c + 1) * 512],
                        start=(c == 0), stop=(c == DC - 1)),
                        inc=("t" if c == DC - 1 else None))
                bk.wrote(pmb, ev_mm)
                state["wb_cons"][sl] = ev_mm
                em.wait("scalar", ev_mm)
                em.wait("scalar", ev_h3_gate)
                ev_h3 = em.op("scalar", lambda e, t=t, n=n, pm=pm: e.activation(
                    out=XNT[:, t * 2048 + n * 512: t * 2048 + (n + 1) * 512],
                    in_=PM[pm][0:128, 0:512], func=ACTF.Copy, bias=0.0, scale=1.0),
                    inc="a")
                bk.did_read(pmb, "scalar", ev_h3)
                em.waits("tensor", [ev_h3, state.get("gates_ev")])
                ev_s = em.op("tensor", lambda e, t=t, n=n: e.matmul(
                    (PSW[0:8, 512 * n:512 * (n + 1)] if n < 2 else sc_ps[n][0:8, 0:512]),
                    lhsT=GT[:, t * 8:(t + 1) * 8],
                    rhs=XNT[:, t * 2048 + n * 512: t * 2048 + (n + 1) * 512],
                    start=(t == 0), stop=(t == 7)), inc="t")
            bk.wrote(sc_bk[n], ev_s)
            em.wait("vector", ev_s)
            src_ps = (lambda n=n: PSW[0:8, 512 * n:512 * (n + 1)] if n < 2 else sc_ps[n][0:8, 0:512])()
            if tile == 0:
                ev_acc = em.op("vector", lambda e, n=n, sp=src_ps: e.tensor_copy(
                    FIN[0:8, n * 512:(n + 1) * 512], sp), inc="v")
            else:
                ev_acc = em.op("vector", lambda e, n=n, sp=src_ps: e.tensor_tensor(
                    out=FIN[0:8, n * 512:(n + 1) * 512],
                    in0=FIN[0:8, n * 512:(n + 1) * 512],
                    in1=sp, op=ALU.add), inc="v")
            bk.did_read(sc_bk[n], "vector", ev_acc)
        state["acc_ev"] = ev_acc

    def finale():
        em.wait("gpsimd", state["acc_ev"])
        ev1 = em.op("gpsimd", lambda e: e.dma_start(
            out=cc_i[:, :], in_=FIN[0:8, 0:D]), inc="gd", dma=True)
        em.wait("gpsimd", ev1)
        ev2 = em.op("gpsimd", lambda e: e.collective_compute(
            "AllReduce", ALU.add, replica_groups=[list(range(8))],
            ins=[cc_i[:, :]], outs=[cc_o[:, :]]), inc="g")
        em.wait("gpsimd", ev2)
        em.waits("gpsimd", [state["xsl_cons"][0], state["xsl_cons"][1],
                            state["xps_cons"][0], state["xps_cons"][1]])
        ev3a = em.op("gpsimd", lambda e: e.dma_start(
            out=XSL[0:8, 0:1024], in_=cc_o[:, 0:1024]), inc="gd", dma=True)
        ev3b = em.op("gpsimd", lambda e: e.dma_start(
            out=XPS[0:8, 0:1024], in_=cc_o[:, 1024:2048]), inc="gd", dma=True)
        em.wait("sync", ev1)
        em.wait("sync", state["y_written"][0])
        ev4 = em.op("sync", lambda e: e.dma_start(
            out=FIN[0:8, 0:D], in_=y_d[0:8, :]), inc="xd", dma=True)
        em.waits("vector", [ev3a, ev3b, ev4])
        em.op("vector", lambda e: e.tensor_scalar(
            out=XSL[0:8, 0:1024], in0=XSL[0:8, 0:1024], scalar1=CM[0:8, 0:1],
            scalar2=None, op0=ALU.mult))
        em.op("vector", lambda e: e.tensor_scalar(
            out=XPS[0:8, 0:1024], in0=XPS[0:8, 0:1024], scalar1=CM[0:8, 0:1],
            scalar2=None, op0=ALU.mult))
        em.op("vector", lambda e: e.tensor_tensor(
            out=XSL[0:8, 0:1024], in0=XSL[0:8, 0:1024], in1=FIN[0:8, 0:1024],
            op=ALU.add))
        ev5 = em.op("vector", lambda e: e.tensor_tensor(
            out=XPS[0:8, 0:1024], in0=XPS[0:8, 0:1024], in1=FIN[0:8, 1024:2048],
            op=ALU.add), inc="v")
        em.wait("scalar", ev5)
        em.op("scalar", lambda e: e.dma_start(
            out=y_d[0:8, 0:1024], in_=XSL[0:8, 0:1024]), inc="od", dma=True)
        em.op("scalar", lambda e: e.dma_start(
            out=y_d[0:8, 1024:2048], in_=XPS[0:8, 0:1024]), inc="od", dma=True)
        em.op("scalar", lambda e, v=em.counts["od"]: e.wait_ge(em.sems["od"], v))

    def emit_all(nsteps):
        cnt = [0]
        def step(f):
            if nsteps is None or cnt[0] < nsteps:
                f()
            cnt[0] += 1
        for tile in (0, 1):
            step(lambda tile=tile: ln_phase(tile, is_ln2=False))
            step(lambda tile=tile: v_phase(tile))
            step(lambda tile=tile: qk_attn_phase(tile))
            step(lambda tile=tile: proj_phase(tile))
            step(lambda tile=tile: ln_phase(tile, is_ln2=True))
            def _mlp0(tile=tile):
                ffn_mlp(tile, 0)
                state["h1_mm_last"] = state["wa_cons"][1]
            step(_mlp0)
            def _mlp1(tile=tile):
                ffn_mlp(tile, 1)
                state["ffn_h2_done"] = state["wa_cons"][1]
            step(_mlp1)
            step(lambda tile=tile: ffn_out(tile))
        step(finale)
    emit_all(steps)

    TENS = {"XNT": XNT, "VB": VB, "ATT": ATT, "QKB": QKB, "GT": GT, "T1R": T1R,
            "SLV": SLV, "FIN": FIN, "STT": STT, "RPS": RPS, "XCH": XCH,
            "SST": SST, "PR": PR, "PTSB": PTSB}
    if dbg:
        for s in ("t", "v", "a", "wd", "xd", "od", "gd", "g"):
            em.wait("scalar", (s, em.counts[s]))
        for name in dbg:
            th = TENS[name]
            shp = [th.shape[0], th.shape[1]]
            dt = th.ap().dtype
            dd = nc.dram_tensor("dbg_" + name, shp, dt, kind="ExternalOutput").ap()
            em.op("scalar", lambda e, dd=dd, th=th: e.dma_start(
                out=dd[:, :], in_=th[:, :]), inc="od", dma=True)
        em.op("scalar", lambda e, v=em.counts["od"]: e.wait_ge(em.sems["od"], v))

    with nc.Block() as block:
        @block.sync
        def _(e):
            for f in em.ops["sync"]:
                f(e)

        @block.tensor
        def _(e):
            for f in em.ops["tensor"]:
                f(e)

        @block.vector
        def _(e):
            for f in em.ops["vector"]:
                f(e)

        @block.scalar
        def _(e):
            for f in em.ops["scalar"]:
                f(e)

        @block.gpsimd
        def _(e):
            for f in em.ops["gpsimd"]:
                f(e)

    es.close()
    return nc



def _host_consts():
    mm_main = np.zeros((128, 144), np.float32)
    b01 = np.zeros((128, 144), np.float32)
    for r in range(128):
        for c in range(r, r + 16):
            d = abs(c - r - 8)
            mm_main[r, c] = SCALE * (WIN - d / 2.0) / WIN
            b01[r, c] = 1.0
    mm2 = np.concatenate([mm_main, mm_main], axis=1)
    mm_s0 = np.zeros((8, 16), np.float32)
    am_s0 = np.full((8, 16), NEG, np.float32)
    for r in range(8):
        for c in range(0, r + 8):
            d = abs(c - r)
            mm_s0[r, c] = SCALE * (WIN - d / 2.0) / WIN
            am_s0[r, c] = 0.0
    mm_last = np.zeros((120, 128), np.float32)
    am_last = np.full((120, 128), NEG, np.float32)
    for r in range(120):
        for c in range(r, min(r + 16, 128)):
            d = abs(c - r - 8)
            mm_last[r, c] = SCALE * (WIN - d / 2.0) / WIN
            am_last[r, c] = 0.0
    return mm2, b01, mm_s0, am_s0, mm_last, am_last


def _prep_weights(qkv_w, proj_w, proj_b, w1, w3, w2, wr):
    f = np.float32
    qkv_w = np.asarray(qkv_w, f)
    qk = np.ascontiguousarray(qkv_w[0:4096])
    T = qk.T
    Bq = T.reshape(16, 128, 32, 128).transpose(2, 1, 0, 3)
    order = []
    for p in range(16):
        order += [p, 16 + p]
    wqk = np.ascontiguousarray(Bq[order]).reshape(32, 128, 16 * 128)

    def nslice(w):
        T = np.asarray(w, f).T
        return np.ascontiguousarray(
            T.reshape(16, 128, 4, 512).transpose(2, 1, 0, 3)).reshape(4, 128, 16 * 512)

    wv = nslice(qkv_w[4096:6144])
    wp = nslice(proj_w)
    w2n = nslice(w2)

    def mblk(w):
        T = np.asarray(w, f).T
        return np.ascontiguousarray(
            T.reshape(16, 128, 16, 128).transpose(2, 1, 0, 3)).reshape(16, 128, 16 * 128)

    w1m = mblk(w1)
    w3m = mblk(w3)
    wrh = np.ascontiguousarray(
        np.asarray(wr, f).T.reshape(16, 128, 8).transpose(1, 0, 2)).reshape(128, 128)
    pb = np.ascontiguousarray(np.asarray(proj_b, f).reshape(1, D))
    return wqk, wv, wp, w1m, w3m, w2n, wrh, pb


def kernel(L0_attn_x, L0_attn_qkv_w, L0_attn_proj_w, L0_attn_proj_b,
           L0_ffn_w1, L0_ffn_w3, L0_ffn_w2, L0_ffn_wr):
    x = np.ascontiguousarray(np.asarray(L0_attn_x, np.float32))
    wqk, wv, wp, w1m, w3m, w2n, wrh, pb = _prep_weights(
        L0_attn_qkv_w, L0_attn_proj_w, L0_attn_proj_b, L0_ffn_w1, L0_ffn_w3,
        L0_ffn_w2, L0_ffn_wr)
    mm2, b01, mm_s0, am_s0, mm_last, am_last = _host_consts()
    ident = np.eye(128, dtype=np.float32)
    nc = build_nc()
    in_maps = []
    for c in range(8):
        cm = np.full((8, 1), 1.0 if c == 0 else 0.0, np.float32)
        in_maps.append({
            "x": np.ascontiguousarray(x[c]),
            "wqk": wqk, "wv": wv, "wp": wp, "w1": w1m, "w3": w3m, "w2": w2n,
            "wr": wrh, "pb": pb,
            "mm2": mm2, "b01": b01, "mm_s0": mm_s0,
            "am_s0": am_s0, "mm_last": mm_last, "am_last": am_last,
            "ident": ident, "coremask": cm,
        })
    res = run_bass_kernel_spmd(nc, in_maps, core_ids=list(range(8)))
    return np.stack([np.asarray(res.results[c]["y"]) for c in range(8)], axis=0)


# version 06
# speedup vs baseline: 1.5750x; 1.0490x over previous
import numpy as np
import concourse.bass as bass
import concourse.mybir as mybir
from concourse.bass_utils import run_bass_kernel_spmd
from contextlib import ExitStack

F32 = mybir.dt.float32
BF16 = mybir.dt.bfloat16
ALU = mybir.AluOpType
ACTF = mybir.ActivationFunctionType
AX = mybir.AxisListType

D = 2048
DC = 16
NT = 2048
TILE = 1024
H = 32
HD = 64
CW = 1040
NEG = -1.0e30
EPS = 1e-5
SCALE = HD ** -0.5
WIN = 8

ENGINES = ("sync", "tensor", "vector", "scalar", "gpsimd")


class Em:
    def __init__(self, nc, es):
        self.nc = nc
        self.ops = {e: [] for e in ENGINES}
        self.sems = {}
        self.counts = {}
        self.last_wait = {e: {} for e in ENGINES}
        self._vec_dirty = False
        for name in ("t", "v", "a", "g", "wd", "xd", "od", "gd"):
            self.sems[name] = es.enter_context(nc.semaphore("sem_" + name))
            self.counts[name] = 0

    def fence(self, eng, inc):
        self.counts[inc] += 1
        val = self.counts[inc]
        h = self.sems[inc]
        self.ops[eng].append(lambda e, h=h: e.drain().then_inc(h, 1))
        if eng == "vector":
            self._vec_dirty = False
        return (inc, val)

    def wait(self, eng, ev):
        if ev is None:
            return
        s, val = ev
        if val <= 0:
            return
        if self.last_wait[eng].get(s, 0) >= val:
            return
        self.last_wait[eng][s] = val
        h = self.sems[s]
        self.ops[eng].append(lambda e, h=h, val=val: e.wait_ge(h, val))

    def waits(self, eng, evs):
        for ev in evs:
            self.wait(eng, ev)

    def op(self, eng, fn, inc=None, dma=False, dep=True):
        if eng == "vector":
            if dep and self._vec_dirty:
                self.ops[eng].append(lambda e: e.drain())
            self._vec_dirty = True
        if inc is None:
            self.ops[eng].append(fn)
            return None
        n = 16 if dma else 1
        self.counts[inc] += n
        val = self.counts[inc]
        h = self.sems[inc]
        if not dma and eng in ("vector", "scalar"):
            self.ops[eng].append(fn)
            self.ops[eng].append(lambda e, h=h, n=n: e.drain().then_inc(h, n))
            if eng == "vector":
                self._vec_dirty = False
        else:
            self.ops[eng].append(lambda e, fn=fn, h=h, n=n: fn(e).then_inc(h, n))
        return (inc, val)


class Banks:
    def __init__(self, em):
        self.em = em
        self.w = {}
        self.r = {}

    def write(self, eng, bank):
        for ev in self.r.get(bank, {}).values():
            self.em.wait(eng, ev)
        self.em.wait(eng, self.w.get(bank))

    def wrote(self, bank, ev):
        self.w[bank] = ev
        self.r[bank] = {}

    def read(self, eng, bank):
        self.em.wait(eng, self.w.get(bank))

    def did_read(self, bank, eng, ev):
        self.r.setdefault(bank, {})[eng] = ev


def build_nc(steps=None, dbg=()):
    nc = bass.Bass("TRN2", target_bir_lowering=False)
    es = ExitStack()

    x_d = nc.dram_tensor("x", [NT, D], F32, kind="ExternalInput").ap()
    wqk_d = nc.dram_tensor("wqk", [32, 128, DC * 128], F32, kind="ExternalInput").ap()
    wv_d = nc.dram_tensor("wv", [4, 128, DC * 512], F32, kind="ExternalInput").ap()
    wp_d = nc.dram_tensor("wp", [4, 128, DC * 512], F32, kind="ExternalInput").ap()
    w1_d = nc.dram_tensor("w1", [16, 128, DC * 128], F32, kind="ExternalInput").ap()
    w3_d = nc.dram_tensor("w3", [16, 128, DC * 128], F32, kind="ExternalInput").ap()
    w2_d = nc.dram_tensor("w2", [4, 128, DC * 512], F32, kind="ExternalInput").ap()
    wr_d = nc.dram_tensor("wr", [128, DC * 8], F32, kind="ExternalInput").ap()
    pb_d = nc.dram_tensor("pb", [1, D], F32, kind="ExternalInput").ap()
    mm2_d = nc.dram_tensor("mm2", [128, 288], F32, kind="ExternalInput").ap()
    b01_d = nc.dram_tensor("b01", [128, 144], F32, kind="ExternalInput").ap()
    mms_d = nc.dram_tensor("mm_s0", [8, 16], F32, kind="ExternalInput").ap()
    ams_d = nc.dram_tensor("am_s0", [8, 16], F32, kind="ExternalInput").ap()
    mml_d = nc.dram_tensor("mm_last", [120, 128], F32, kind="ExternalInput").ap()
    aml_d = nc.dram_tensor("am_last", [120, 128], F32, kind="ExternalInput").ap()
    id_d = nc.dram_tensor("ident", [128, 128], F32, kind="ExternalInput").ap()
    cm_d = nc.dram_tensor("coremask", [8, 1], F32, kind="ExternalInput").ap()
    y_d = nc.dram_tensor("y", [NT, D], F32, kind="ExternalOutput").ap()
    cc_i = nc.dram_tensor("cc_i", [8, D], F32)
    cc_o = nc.dram_tensor("cc_o", [8, D], F32)
    dbg_d = {}

    sb = lambda name, shp, dt: es.enter_context(nc.sbuf_tensor(name, shp, dt))
    XNT = sb("XNT", [128, DC * CW], BF16)
    VB = sb("VB", [128, 9 * 2048], BF16)
    ATT = sb("ATT", [128, DC * 1024], BF16)
    SLV = sb("SLV", [128, DC * 8], BF16)
    QKB = sb("QKB", [128, 4 * CW], BF16)
    WA = sb("WA", [128, 2 * 2048], BF16)
    WB = sb("WB", [128, 2 * DC * 512], BF16)
    XCH = sb("XCH", [128, 2 * 2048], F32)
    XSL = sb("XSL", [128, 2 * 512], F32)
    XPS = sb("XPS", [128, 2 * 512], F32)
    T1R = sb("T1R", [128, 4 * 288], F32)
    PR = sb("PR", [128, 4 * 320], BF16)
    PS0R = sb("PS0R", [32, 32], BF16)
    PTSB = sb("PTSB", [128, 2 * 512], BF16)
    STT = sb("STT", [128, 32], F32)
    SST = sb("SST", [128, 32], F32)
    RPS = sb("RPS", [128, 40], F32)
    GT = sb("GT", [128, 8 * 8], BF16)
    MM2 = sb("MM2", [128, 288], BF16)
    B01 = sb("B01", [128, 144], BF16)
    MMS = sb("MMS", [8, 16], F32)
    AMS = sb("AMS", [8, 16], F32)
    MML = sb("MML", [120, 128], BF16)
    AML = sb("AML", [120, 128], BF16)
    IDB = sb("IDB", [128, 128], BF16)
    IDF = sb("IDF", [128, 128], F32)
    ONE = sb("ONE", [1, 128], BF16)
    PBB = sb("PBB", [1, D], BF16)
    WR = sb("WR", [128, DC * 8], F32)
    CM = sb("CM", [8, 1], F32)
    ZAP = sb("ZAP", [128, 1], F32)
    EPSAP = sb("EPSAP", [128, 1], F32)
    SQSL = sb("SQSL", [128, 512], F32)
    FIN = sb("FIN", [8, D], F32)

    ps = lambda name, shp, dt: es.enter_context(nc.psum_tensor(name, shp, dt))
    PM = [ps("PM0", [128, 512], F32), ps("PM1", [128, 512], F32)]
    PSW = ps("PSW", [128, 1024], F32)
    PV = [ps("PV0", [128, 512], F32), ps("PV1", [128, 512], F32)]
    PT = [ps("PT0", [128, 1024], BF16), ps("PT1", [128, 1024], BF16)]
    BK_PM, BK_PS, BK_PV, BK_PT = (0, 1), ("PSW", "PSW"), (4, 5), (6, 7)
    PS = [PSW, PSW]

    em = Em(nc, es)
    bk = Banks(em)

    ld_evs = []
    for dst, src in ((MMS, mms_d), (AMS, ams_d),
                     (IDF, id_d), (WR, wr_d),
                     (CM, cm_d)):
        ld_evs.append(em.op("sync", lambda e, d=dst, s=src: e.dma_start(
            out=d[:, :], in_=s[:, :]), inc="xd", dma=True))
    ld_evs.append(em.op("gpsimd", lambda e: e.dma_start(
        out=IDB[:, :], in_=id_d[:, :]), inc="wd", dma=True))
    ld_evs.append(em.op("gpsimd", lambda e: e.dma_start(
        out=PBB[:, :], in_=pb_d[:, :]), inc="wd", dma=True))
    ld_evs.append(em.op("gpsimd", lambda e: e.dma_start(
        out=MM2[:, :], in_=mm2_d[:, :]), inc="wd", dma=True))
    ld_evs.append(em.op("gpsimd", lambda e: e.dma_start(
        out=MML[:, :], in_=mml_d[:, :]), inc="wd", dma=True))
    ld_evs.append(em.op("gpsimd", lambda e: e.dma_start(
        out=AML[:, :], in_=aml_d[:, :]), inc="wd", dma=True))
    ld_evs.append(em.op("gpsimd", lambda e: e.dma_start(
        out=B01[:, :], in_=b01_d[:, :]), inc="wd", dma=True))
    em.waits("vector", ld_evs)
    em.op("vector", lambda e: e.memset(ZAP[:, :], 0.0))
    em.op("vector", lambda e: e.memset(EPSAP[:, :], EPS))
    for s_ in range(4):
        for u_ in range(2):
            em.op("vector", lambda e, o=s_ * 320 + u_ * 160: e.memset(
                PR[:, o + 144: o + 160], 0.0))
    em.op("vector", lambda e: e.memset(PS0R[:, :], 0.0))
    em.op("vector", lambda e: e.memset(VB[0:32, 8 * 2048:9 * 2048], 0.0))
    ev_memset = em.op("vector", lambda e: e.memset(ONE[:, :], 1.0), inc="v")
    em.waits("tensor", ld_evs + [ev_memset])
    em.waits("scalar", ld_evs + [ev_memset])

    state = {
        "wa": [None, None], "wb": [None, None],
        "wa_cons": [None, None], "wb_cons": [None, None],
        "xch_cons": [None, None],
        "qk_cons": [None, None, None, None],
        "xsl_cons": [None, None], "xps_cons": [None, None],
        "t1_cons": [None, None], "pr_cons": [None, None],
        "ptm_cons": [None, None], "pth_cons": [None, None],
        "xbk_cons": [None, None], "xbk2_cons": [None, None],
        "f32r_cons": [None, None],
        "y_written": {},
        "ln_ev": {"vector": None, "scalar": None},
        "ln1_chunk_evs": {},
        "ln2_chunk_evs": {},
        "v_done": {"vector": None, "scalar": None},
    }

    def load_wa(slot, src_ap):
        em.wait("gpsimd", state["wa_cons"][slot])
        ev = em.op("gpsimd", lambda e, s=src_ap, sl=slot: e.dma_start(
            out=WA[:, sl * 2048:(sl + 1) * 2048], in_=s,
            max_dma_last_dim=8192), inc="wd", dma=True)
        state["wa"][slot] = ev
        return ev

    def load_wb(slot, src_ap):
        em.wait("gpsimd", state["wb_cons"][slot])
        ev = em.op("gpsimd", lambda e, s=src_ap, sl=slot: e.dma_start(
            out=WB[:, sl * DC * 512:(sl + 1) * DC * 512], in_=s,
            max_dma_last_dim=8192), inc="wd", dma=True)
        state["wb"][slot] = ev
        return ev

    def ln_chunk(tile, i, nrow, rows_src, is_ln2):
        sl = i % 2
        em.wait("sync", state["xch_cons"][sl])
        if is_ln2:
            em.wait("sync", state["y_written"][rows_src // 128])
            src = y_d[rows_src:rows_src + nrow, :]
        else:
            src = x_d[rows_src:rows_src + nrow, :]
        ev_x = em.op("sync", lambda e, s=src, sl=sl, n=nrow: e.dma_start(
            out=XCH[0:n, sl * 2048: sl * 2048 + 2048], in_=s), inc="xd", dma=True)
        em.wait("vector", ev_x)
        em.op("vector", lambda e, sl=sl, n=nrow: e.reduce_sum(
            STT[0:n, 23:24], XCH[0:n, sl * 2048: sl * 2048 + 2048], axis=AX.X))
        em.op("vector", lambda e, n=nrow: e.tensor_scalar(
            out=STT[0:n, 24:25], in0=STT[0:n, 23:24], scalar1=1.0 / D,
            scalar2=None, op0=ALU.mult))
        for q in range(4):
            em.op("vector", lambda e, q=q, sl=sl, n=nrow: e.scalar_tensor_tensor(
                out=SQSL[0:n, 0:512],
                in0=XCH[0:n, sl * 2048 + q * 512: sl * 2048 + (q + 1) * 512],
                scalar=1.0,
                in1=XCH[0:n, sl * 2048 + q * 512: sl * 2048 + (q + 1) * 512],
                op0=ALU.mult, op1=ALU.mult))
            em.op("vector", lambda e, q=q, n=nrow: e.reduce_sum(
                STT[0:n, 16 + q:17 + q], SQSL[0:n, 0:512], axis=AX.X))
        em.op("vector", lambda e, n=nrow: e.tensor_tensor(
            out=STT[0:n, 20:21], in0=STT[0:n, 16:17], in1=STT[0:n, 17:18], op=ALU.add))
        em.op("vector", lambda e, n=nrow: e.tensor_tensor(
            out=STT[0:n, 21:22], in0=STT[0:n, 18:19], in1=STT[0:n, 19:20], op=ALU.add))
        em.op("vector", lambda e, n=nrow: e.tensor_tensor(
            out=STT[0:n, 22:23], in0=STT[0:n, 20:21], in1=STT[0:n, 21:22], op=ALU.add))
        em.op("vector", lambda e, n=nrow: e.tensor_scalar(
            out=STT[0:n, 26:27], in0=STT[0:n, 22:23], scalar1=1.0 / D,
            scalar2=None, op0=ALU.mult))
        em.op("vector", lambda e, n=nrow: e.tensor_tensor(
            out=STT[0:n, 18:19], in0=STT[0:n, 24:25], in1=STT[0:n, 24:25], op=ALU.mult))
        ev_mv = em.op("vector", lambda e, n=nrow: e.tensor_tensor(
            out=STT[0:n, 25:26], in0=STT[0:n, 26:27], in1=STT[0:n, 18:19],
            op=ALU.subtract), inc="v")
        em.wait("scalar", ev_mv)
        ev_sd = em.op("scalar", lambda e, n=nrow: e.activation(
            out=STT[0:n, 27:28], in_=STT[0:n, 25:26], func=ACTF.Sqrt,
            bias=EPSAP[0:n, 0:1], scale=1.0), inc="a")
        em.wait("vector", ev_sd)
        em.op("vector", lambda e, n=nrow: e.reciprocal(
            STT[0:n, 28:29], STT[0:n, 27:28]))
        ev_nb = em.op("vector", lambda e, n=nrow: e.tensor_scalar(
            out=STT[0:n, 29:30], in0=STT[0:n, 24:25], scalar1=STT[0:n, 28:29],
            scalar2=-1.0, op0=ALU.mult, op1=ALU.mult), inc="v")
        em.wait("scalar", ev_nb)

        t0 = i * 128
        ev_nm = None
        chunk_evs = []
        if not is_ln2:
            XN = T1R[:, 0:1024].bitcast(BF16)
            em.wait("scalar", state.get("xn_tr_last"))
            em.wait("scalar", state.get("gates_ev"))
            em.wait("scalar", state.get("ln2q_mm"))
            ev_nm = em.op("scalar", lambda e, XN=XN, n=nrow, sl=sl: e.activation(
                out=XN[0:n, 0:2048],
                in_=XCH[0:n, sl * 2048: sl * 2048 + 2048],
                func=ACTF.Identity, bias=STT[0:n, 29:30], scale=STT[0:n, 28:29]),
                inc="a")
            for c in range(DC):
                bs = c % 2
                ptb = BK_PT[bs]
                bk.write("tensor", ptb)
                em.wait("tensor", ev_nm)
                ev_tr = em.op("tensor", lambda e, bs=bs, c=c, XN=XN, n=nrow: e.transpose(
                    PT[bs][0:128, 0:n],
                    XN[0:n, c * 128: (c + 1) * 128],
                    IDB[0:n, 0:n]), inc="t")
                bk.wrote(ptb, ev_tr)
                state["xn_tr_last"] = ev_tr
                eng = "vector" if c % 2 == 0 else "scalar"
                em.wait(eng, ev_tr)
                if eng == "vector":
                    ev_e = em.op("vector", lambda e, bs=bs, c=c, t0=t0, n=nrow: e.tensor_copy(
                        XNT[:, c * CW + t0: c * CW + t0 + n], PT[bs][0:128, 0:n]),
                        inc="v")
                else:
                    ev_e = em.op("scalar", lambda e, bs=bs, c=c, t0=t0, n=nrow: e.activation(
                        out=XNT[:, c * CW + t0: c * CW + t0 + n],
                        in_=PT[bs][0:128, 0:n], func=ACTF.Copy, bias=0.0, scale=1.0),
                        inc="a")
                bk.did_read(ptb, eng, ev_e)
                state["ln_ev"][eng] = ev_e
                chunk_evs.append(ev_e)
            state["ln1_chunk_evs"][i] = chunk_evs[-2:]
            state["xch_cons"][sl] = ev_nm
            return
        XB4 = T1R[:, 0:512]
        F4 = T1R[:, 512:1024]
        QP = [PM[0], PM[1], PV[0], PV[1]]
        QB = [BK_PM[0], BK_PM[1], BK_PV[0], BK_PV[1]]
        ev_cb = None
        for q in range(4):
            em.wait("scalar", state.get("ln2q_tr"))
            ev_nm = None
            for b in range(4):
                c = 4 * q + b
                ev_nm = em.op("scalar", lambda e, c=c, b=b, sl=sl, n=nrow: e.activation(
                    out=XB4[0:n, b * 128: b * 128 + 128],
                    in_=XCH[0:n, sl * 2048 + c * 128: sl * 2048 + (c + 1) * 128],
                    func=ACTF.Identity, bias=STT[0:n, 29:30], scale=STT[0:n, 28:29]),
                    inc=("a" if b == 3 else None))
            ev_tr = None
            for b in range(4):
                bk.write("tensor", QB[b])
                em.wait("tensor", ev_nm)
                ev_tr = em.op("tensor", lambda e, b=b, n=nrow: e.transpose(
                    QP[b][0:128, 0:n], XB4[0:n, b * 128: b * 128 + 128],
                    IDF[0:n, 0:n]), inc=("t" if b == 3 else None))
                if b == 3:
                    for bb in range(4):
                        bk.wrote(QB[bb], ev_tr)
            state["ln2q_tr"] = ev_tr
            em.wait("vector", ev_tr)
            em.wait("vector", state.get("ln2q_mm"))
            ev_f = None
            for b in range(4):
                ev_f = em.op("vector", lambda e, b=b, n=nrow: e.tensor_copy(
                    F4[:, b * 128: b * 128 + n], QP[b][0:128, 0:n]),
                    inc=("v" if b == 3 else None), dep=False)
            for b in range(4):
                bk.did_read(QB[b], "vector", ev_f)
            em.wait("scalar", ev_f)
            for b in range(4):
                c = 4 * q + b
                ev_cb = em.op("scalar", lambda e, b=b, c=c, t0=t0, n=nrow: e.activation(
                    out=XNT[:, c * CW + t0: c * CW + t0 + n],
                    in_=QP[b][0:128, 0:n], func=ACTF.Copy, bias=0.0,
                    scale=1.0), inc=("a" if b == 3 else None))
                bk.did_read(QB[b], "scalar", ev_cb)
            state["ln_ev"]["scalar"] = ev_cb
            bk.write("tensor", BK_PS[0])
            em.wait("tensor", ev_f)
            ev_rmm = None
            for b in range(4):
                c = 4 * q + b
                ev_rmm = em.op("tensor", lambda e, b=b, c=c, i=i, n=nrow: e.matmul(
                    PSW[0:n, 8 * i: 8 * i + 8],
                    lhsT=F4[:, b * 128: b * 128 + n],
                    rhs=WR[:, 8 * c:8 * c + 8],
                    start=(c == 0), stop=(c == DC - 1)),
                    inc=("t" if b == 3 else None))
            state["ln2q_mm"] = ev_rmm
            if q == 3:
                bk.wrote(BK_PS[0], ev_rmm)
        state["ln2_chunk_evs"][i] = ev_cb
        state["xch_cons"][sl] = ev_nm

    def ln_phase(tile, is_ln2):
        base = tile * TILE
        ntok = 1040 if (tile == 0 and not is_ln2) else 1024
        nch = (ntok + 127) // 128
        for i in range(nch):
            nrow = min(128, ntok - i * 128)
            ln_chunk(tile, i, nrow, base + i * 128, is_ln2)
        if is_ln2:
            router_post(tile)

    def router_post(tile):
        RL = T1R
        bk.read("vector", BK_PS[0])
        em.wait("vector", state["t1_cons"][0])
        em.wait("vector", state["t1_cons"][1])
        ev_rl = em.op("vector", lambda e: e.tensor_copy(
            RL[:, 0:128], PSW[0:128, 0:128]), inc="v")
        bk.did_read(BK_PS[0], "vector", ev_rl)
        ev_g = None
        for i in range(8):
            em.op("vector", lambda e, i=i: e.reduce_max(
                RPS[:, 32:33], RL[:, 8 * i:8 * i + 8], axis=AX.X, negate=True))
            ev1 = em.op("vector", lambda e, i=i: e.tensor_scalar(
                out=RPS[:, 0:8], in0=RL[:, 8 * i:8 * i + 8], scalar1=RPS[:, 32:33],
                scalar2=None, op0=ALU.add), inc="v")
            em.wait("scalar", ev1)
            ev2 = em.op("scalar", lambda e: e.activation(
                out=RPS[:, 0:8], in_=RPS[:, 0:8], func=ACTF.Exp, bias=ZAP[:, 0:1],
                scale=1.0), inc="a")
            em.wait("vector", ev2)
            em.op("vector", lambda e: e.reduce_sum(
                RPS[:, 33:34], RPS[:, 0:8], axis=AX.X))
            em.op("vector", lambda e: e.reciprocal(RPS[:, 34:35], RPS[:, 33:34]))
            em.op("vector", lambda e: e.tensor_scalar(
                out=RPS[:, 0:8], in0=RPS[:, 0:8], scalar1=RPS[:, 34:35],
                scalar2=None, op0=ALU.mult))
            em.op("vector", lambda e: e.max(RPS[:, 8:16], RPS[:, 0:8]))
            em.op("vector", lambda e: e.tensor_scalar(
                out=RPS[:, 16:24], in0=RPS[:, 0:8], scalar1=RPS[:, 9:10],
                scalar2=None, op0=ALU.is_ge))
            em.op("vector", lambda e: e.tensor_tensor(
                out=RPS[:, 35:36], in0=RPS[:, 8:9], in1=RPS[:, 9:10], op=ALU.add))
            em.op("vector", lambda e: e.reciprocal(RPS[:, 36:37], RPS[:, 35:36]))
            ev_g = em.op("vector", lambda e, i=i: e.scalar_tensor_tensor(
                out=GT[:, 8 * i:8 * i + 8], in0=RPS[:, 16:24], scalar=RPS[:, 36:37],
                in1=RPS[:, 0:8], op0=ALU.mult, op1=ALU.mult), inc="v")
        state["gates_ev"] = ev_g
        state["rl_ev"] = ev_rl

    def v_phase(tile):
        ntok = 1040 if tile == 0 else 1024
        nj = 9 if tile == 0 else 8
        ev_e = None
        for n in range(4):
            sl = n % 2
            load_wb(sl, wv_d[n])
            ev_w = state["wb"][sl]
            for j in range(nj):
                nrow = min(128, ntok - j * 128)
                pm = (n * 9 + j) % 2
                pmb = BK_PM[pm]
                bk.write("tensor", pmb)
                em.wait("tensor", ev_w)
                em.waits("tensor", state["ln1_chunk_evs"][j])
                ev_mm = None
                for c in range(DC):
                    ev_mm = em.op("tensor", lambda e, c=c, j=j, sl=sl, pm=pm, nr=nrow: e.matmul(
                        PM[pm][0:nr, 0:512],
                        lhsT=XNT[:, c * CW + j * 128: c * CW + j * 128 + nr],
                        rhs=WB[:, sl * DC * 512 + c * 512: sl * DC * 512 + (c + 1) * 512],
                        start=(c == 0), stop=(c == DC - 1)),
                        inc=("t" if c == DC - 1 else None))
                bk.wrote(pmb, ev_mm)
                state["wb_cons"][sl] = ev_mm
                eng = "vector" if j % 2 == 0 else "scalar"
                em.wait(eng, ev_mm)
                if eng == "vector":
                    ev_e = em.op("vector", lambda e, j=j, n=n, pm=pm, nr=nrow: e.tensor_copy(
                        VB[0:nr, j * 2048 + n * 512: j * 2048 + (n + 1) * 512],
                        PM[pm][0:nr, 0:512]), inc="v")
                else:
                    ev_e = em.op("scalar", lambda e, j=j, n=n, pm=pm, nr=nrow: e.activation(
                        out=VB[0:nr, j * 2048 + n * 512: j * 2048 + (n + 1) * 512],
                        in_=PM[pm][0:nr, 0:512], func=ACTF.Copy, bias=0.0,
                        scale=1.0), inc="a")
                bk.did_read(pmb, eng, ev_e)
                state["v_done"][eng] = ev_e

    def units_for_tile(tile):
        if tile == 0:
            us = [("s0", 0, 8, 0, 16, 0)]
            us += [("main", 8 + 128 * j, 128, 128 * j, 144, j) for j in range(8)]
        else:
            us = [("main", 8 + 128 * j, 128, 128 * j, 144, j) for j in range(7)]
            us.append(("last", 904, 120, 896, 128, 7))
        return us

    def qk_group(tile, pair, kind, off, n, last):
        sl = pair % 2
        ring = kind * 2 + sl
        blk = 2 * pair + kind
        wsl = blk % 2
        if off == 0:
            load_wa(wsl, wqk_d[blk])
        ev_w = state["wa"][wsl]
        state["qk_pm"] = (state.get("qk_pm", 0) + 1) % 2
        pm = state["qk_pm"]
        pmb = BK_PM[pm]
        bk.write("tensor", pmb)
        em.wait("tensor", ev_w)
        ev_mm = None
        for c in range(DC):
            ev_mm = em.op("tensor", lambda e, c=c, off=off, n=n, wsl=wsl, pm=pm: e.matmul(
                PM[pm][0:128, 0:n],
                lhsT=WA[:, wsl * 2048 + c * 128: wsl * 2048 + (c + 1) * 128],
                rhs=XNT[:, c * CW + off: c * CW + off + n],
                start=(c == 0), stop=(c == DC - 1)),
                inc=("t" if c == DC - 1 else None))
        bk.wrote(pmb, ev_mm)
        state["wa_cons"][wsl] = ev_mm
        eng = "vector" if kind == 0 else "scalar"
        em.wait(eng, ev_mm)
        em.wait(eng, state["qk_cons"][ring])
        if eng == "vector":
            ev_e = em.op("vector", lambda e, ring=ring, off=off, n=n, pm=pm: e.tensor_copy(
                QKB[:, ring * CW + off: ring * CW + off + n], PM[pm][0:128, 0:n]),
                inc="v")
        else:
            ev_e = em.op("scalar", lambda e, ring=ring, off=off, n=n, pm=pm: e.activation(
                out=QKB[:, ring * CW + off: ring * CW + off + n],
                in_=PM[pm][0:128, 0:n], func=ACTF.Copy, bias=0.0, scale=1.0),
                inc="a")
        bk.did_read(pmb, eng, ev_e)
        if last:
            state.setdefault("qk_ready", {})[(pair, kind)] = ev_e

    def head_bundles(tile, head):
        import os as _os3
        solo_last = int(_os3.environ.get("K_SOLO_LAST", "0"))
        us = units_for_tile(tile)
        bs, i = [], 0
        while i < len(us):
            can_pair = (us[i][0] == "main" and i + 1 < len(us)
                        and us[i + 1][0] == "main")
            if solo_last and tile == 0 and us[i][5] == 6 and us[i][0] == "main":
                can_pair = False
            if can_pair:
                bs.append((head, [us[i], us[i + 1]]))
                i += 2
            else:
                bs.append((head, [us[i]]))
                i += 1
        return bs

    def qk_attn_phase(tile):
        ncols = 1040 if tile == 0 else 1024
        nsl = [(0, 512), (512, 512), (1024, 16)] if ncols == 1040 else [(0, 512), (512, 512)]
        if tile == 1:
            for c in range(DC):
                ev_sv = em.op("vector", lambda e, c=c: e.tensor_copy(
                    ATT[:, c * 1024: c * 1024 + 8], SLV[:, c * 8: c * 8 + 8]),
                    inc="v", dep=(c == 0))
            state["sliver_ev"] = ev_sv
        for kind in (0, 1):
            for gi, (off, n) in enumerate(nsl):
                qk_group(tile, 0, kind, off, n, gi == len(nsl) - 1)

        bundles = []
        for pair in range(16):
            for hh in range(2):
                bundles += head_bundles(tile, 2 * pair + hh)
        import os as _os
        per_pair = len(bundles) // 16
        _lim = int(_os.environ.get("K_DBG_NBUND", "0"))
        if _lim:
            bundles = bundles[:_lim]
        NB = len(bundles)
        qkq = []
        maxpair = (NB - 1) // per_pair if _lim else 15
        import os as _os2
        _noil = int(_os2.environ.get("K_NO_QK_INTERLEAVE", "0"))
        for pair in range(min(15, maxpair)):
            ngr = 2 * len(nsl)
            for g in range(ngr):
                pos = pair * per_pair + (0 if _noil else (g * per_pair) // ngr)
                qkq.append((pos, pair + 1, g))
        ctx = [None] * NB
        incA = {}
        incB = {}
        incC = {}

        def mkctx(bi):
            head, us = bundles[bi]
            c = dict(head=head, units=us, s=bi % 2, s4=bi % 4, par=head % 2,
                     pair=head // 2, dvc=head * 64, nu=len(us))
            c["p0"] = 64 * c["par"]
            return c

        def S0(bi, evA):
            c = ctx[bi]
            em.wait("tensor", evA)
            em.waits("tensor", [state["qk_ready"][(c["pair"], 0)],
                                state["qk_ready"][(c["pair"], 1)]])
            bk.write("tensor", "PSW")
            rsl = c["pair"] % 2
            p0 = c["p0"]
            ev = None
            for ui, (kind, q0, nq, k0, nk, vj) in enumerate(c["units"]):
                qr = QKB[p0:p0 + 64, rsl * CW + q0: rsl * CW + q0 + nq]
                kr = QKB[p0:p0 + 64, (2 + rsl) * CW + k0: (2 + rsl) * CW + k0 + nk]
                ev = em.op("tensor", lambda e, qr=qr, kr=kr, ui=ui, nq=nq, nk=nk: e.matmul(
                    PSW[0:nq, ui * 512: ui * 512 + nk], lhsT=qr, rhs=kr,
                    start=True, stop=True),
                    inc=("t" if ui == c["nu"] - 1 else None))
            c["ev_sc"] = ev
            bk.wrote("PSW", ev)
            state["qk_cons"][rsl] = ev
            state["qk_cons"][2 + rsl] = ev

        def S1(bi):
            c = ctx[bi]
            s = c["s"]
            s4 = c["s4"]
            em.wait("vector", c["ev_sc"])
            em.wait("vector", incC.get(bi - 1))
            if c["nu"] == 2:
                pv = PSW[0:128, 0:1024].rearrange("p (u x) -> p u x", u=2)[:, :, 0:144]
                em.op("vector", lambda e, pv=pv, s=s: e.scalar_tensor_tensor(
                    out=T1R[:, s4 * 288: s4 * 288 + 288], in0=pv, scalar=1.0,
                    in1=MM2[:, 0:288], op0=ALU.mult, op1=ALU.mult), dep=False)
            else:
                kind, q0, nq, k0, nk, vj = c["units"][0]
                mm = {"main": MM2, "s0": MMS, "last": MML}[kind]
                em.op("vector", lambda e, s=s, s4=s4, nq=nq, nk=nk, mm=mm: e.scalar_tensor_tensor(
                    out=T1R[0:nq, s4 * 288: s4 * 288 + nk], in0=PSW[0:nq, 0:nk],
                    scalar=1.0, in1=mm[0:nq, 0:nk], op0=ALU.mult, op1=ALU.mult),
                    dep=False)
            bk.did_read("PSW", "vector", None)

        def S1tt(bi):
            c = ctx[bi]
            s = c["s"]
            s4 = c["s4"]
            kind, q0, nq, k0, nk, vj = c["units"][0]
            am = {"s0": AMS, "last": AML}[kind]
            em.op("vector", lambda e, s=s, s4=s4, nq=nq, nk=nk, am=am: e.tensor_tensor(
                out=T1R[0:nq, s4 * 288: s4 * 288 + nk],
                in0=T1R[0:nq, s4 * 288: s4 * 288 + nk],
                in1=am[0:nq, 0:nk], op=ALU.add), dep=False)

        def S1rmax(bi):
            c = ctx[bi]
            s = c["s"]
            s4 = c["s4"]
            if c["nu"] == 2:
                tv = T1R[:, s4 * 288:(s4 + 1) * 288].rearrange("p (u x) -> p u x", u=2)
                em.op("vector", lambda e, tv=tv, s=s: e.reduce_max(
                    SST[:, s4 * 8: s4 * 8 + 2], tv, axis=AX.X, negate=True), dep=False)
            else:
                kind, q0, nq, k0, nk, vj = c["units"][0]
                em.op("vector", lambda e, s=s, s4=s4, nq=nq, nk=nk: e.reduce_max(
                    SST[0:nq, s4 * 8: s4 * 8 + 1], T1R[0:nq, s4 * 288: s4 * 288 + nk],
                    axis=AX.X, negate=True), dep=False)

        def S2sum(bi):
            c = ctx[bi]
            s = c["s"]
            s4 = c["s4"]
            em.wait("vector", incC.get(bi + 1))
            if c["nu"] == 2:
                tv = T1R[:, s4 * 288:(s4 + 1) * 288].rearrange("p (u x) -> p u x", u=2)
                em.op("vector", lambda e, tv=tv, s=s: e.reduce_sum(
                    SST[:, s4 * 8 + 2: s4 * 8 + 4], tv, axis=AX.X), dep=False)
            else:
                kind, q0, nq, k0, nk, vj = c["units"][0]
                em.op("vector", lambda e, s=s, s4=s4, nq=nq, nk=nk: e.reduce_sum(
                    SST[0:nq, s4 * 8 + 2: s4 * 8 + 3], T1R[0:nq, s4 * 288: s4 * 288 + nk],
                    axis=AX.X), dep=False)

        def S2corr(bi):
            c = ctx[bi]
            s = c["s"]
            s4 = c["s4"]
            for ui, (kind, q0, nq, k0, nk, vj) in enumerate(c["units"]):
                em.op("vector", lambda e, s=s, s4=s4, ui=ui: e.scalar_tensor_tensor(
                    out=SST[:, s4 * 8 + 4 + ui: s4 * 8 + 5 + ui],
                    in0=T1R[:, s4 * 288 + ui * 144 + 143: s4 * 288 + ui * 144 + 144],
                    scalar=-128.0, in1=SST[:, s4 * 8 + 2 + ui: s4 * 8 + 3 + ui],
                    op0=ALU.mult, op1=ALU.add), dep=False)

        def S2recip(bi):
            c = ctx[bi]
            s = c["s"]
            s4 = c["s4"]
            main = c["units"][0][0] == "main"
            base = s4 * 8 + (4 if main else 2)
            nq = 128 if c["nu"] == 2 else c["units"][0][2]
            em.op("vector", lambda e, s=s, s4=s4, base=base, nu=c["nu"], nq=nq: e.reciprocal(
                SST[0:nq, s4 * 8 + 6: s4 * 8 + 6 + nu], SST[0:nq, base: base + nu]),
                dep=False)

        def S2pnorm(bi, need_drain):
            c = ctx[bi]
            s = c["s"]
            s4 = c["s4"]
            em.wait("vector", (ctx[bi - 2].get("ev_tr") if bi >= 2 else None))
            if c["units"][0][0] == "s0":
                em.wait("vector", state.get("s0_tr_ev"))
            for ui, (kind, q0, nq, k0, nk, vj) in enumerate(c["units"]):
                if kind == "main":
                    em.op("vector", lambda e, s4=s4, ui=ui, nq=nq: e.scalar_tensor_tensor(
                        out=PR[0:nq, s4 * 320 + ui * 160: s4 * 320 + ui * 160 + 144],
                        in0=T1R[0:nq, s4 * 288 + ui * 144: s4 * 288 + ui * 144 + 144],
                        scalar=SST[0:nq, s4 * 8 + 6 + ui: s4 * 8 + 7 + ui],
                        in1=B01[0:nq, 0:144], op0=ALU.mult, op1=ALU.mult),
                        dep=(need_drain and ui == 0))
                elif kind == "s0":
                    em.op("vector", lambda e, s=s, s4=s4: e.tensor_scalar(
                        out=PS0R[0:8, 0:16], in0=T1R[0:8, s4 * 288: s4 * 288 + 16],
                        scalar1=SST[0:8, s4 * 8 + 6: s4 * 8 + 7], scalar2=None,
                        op0=ALU.mult), dep=need_drain)
                else:
                    em.op("vector", lambda e, s=s, s4=s4, nq=nq, nk=nk: e.tensor_scalar(
                        out=PR[0:nq, s4 * 320: s4 * 320 + nk],
                        in0=T1R[0:nq, s4 * 288: s4 * 288 + nk],
                        scalar1=SST[0:nq, s4 * 8 + 6: s4 * 8 + 7], scalar2=None,
                        op0=ALU.mult), dep=need_drain)

        def Sexp(bi, ev):
            c = ctx[bi]
            s = c["s"]
            s4 = c["s4"]
            em.wait("scalar", ev)
            for ui, (kind, q0, nq, k0, nk, vj) in enumerate(c["units"]):
                em.op("scalar", lambda e, s=s, s4=s4, ui=ui, nq=nq, nk=nk: e.activation(
                    out=T1R[0:nq, s4 * 288 + ui * 144: s4 * 288 + ui * 144 + nk],
                    in_=T1R[0:nq, s4 * 288 + ui * 144: s4 * 288 + ui * 144 + nk],
                    func=ACTF.Exp, bias=SST[0:nq, s4 * 8 + ui: s4 * 8 + ui + 1],
                    scale=1.0))

        def S3tr(bi, ev):
            c = ctx[bi]
            s = c["s"]
            s4 = c["s4"]
            ptb = BK_PT[s]
            em.wait("tensor", ev)
            bk.write("tensor", ptb)
            evt = None
            for ui, (kind, q0, nq, k0, nk, vj) in enumerate(c["units"]):
                if kind == "s0":
                    evt = em.op("tensor", lambda e, s=s, s4=s4: e.transpose(
                        PT[s][0:32, 0:32], PS0R[0:32, 0:32], IDB[0:32, 0:32]),
                        inc="t")
                    state["s0_tr_ev"] = evt
                elif kind == "last":
                    evt = em.op("tensor", lambda e, s=s, s4=s4: e.transpose(
                        PT[s][0:128, 0:128], PR[:, s4 * 320: s4 * 320 + 128],
                        IDB[:, :]), inc="t")
                else:
                    em.op("tensor", lambda e, s=s, s4=s4, ui=ui: e.transpose(
                        PT[s][0:128, ui * 256: ui * 256 + 128],
                        PR[:, s4 * 320 + ui * 160: s4 * 320 + ui * 160 + 128],
                        IDB[:, :]))
                    evt = em.op("tensor", lambda e, s=s, s4=s4, ui=ui: e.transpose(
                        PT[s][0:32, ui * 256 + 128: ui * 256 + 256],
                        PR[:, s4 * 320 + ui * 160 + 128: s4 * 320 + ui * 160 + 160],
                        IDB[:, :]), inc="t")
            c["ev_tr"] = evt
            bk.wrote(ptb, evt)

        def S3ev(bi):
            c = ctx[bi]
            s = c["s"]
            s4 = c["s4"]
            em.wait("vector", c["ev_tr"])
            em.wait("vector", (ctx[bi - 2]["ev_pv"] if bi >= 2 else None))
            kind = c["units"][0][0]
            if kind == "s0":
                em.op("vector", lambda e, s=s, s4=s4: e.tensor_copy(
                    PTSB[0:32, s * 512: s * 512 + 32], PT[s][0:32, 0:32]), dep=False)
            elif kind == "last":
                em.op("vector", lambda e, s=s, s4=s4: e.tensor_copy(
                    PTSB[0:128, s * 512: s * 512 + 128], PT[s][0:128, 0:128]),
                    dep=False)
            else:
                w = 256 * c["nu"]
                em.op("vector", lambda e, s=s, s4=s4, w=w: e.tensor_copy(
                    PTSB[0:128, s * 512: s * 512 + w], PT[s][0:128, 0:w]), dep=False)
            bk.did_read(BK_PT[s], "vector", None)

        def S3pv(bi, evB, evC):
            c = ctx[bi]
            s = c["s"]
            s4 = c["s4"]
            pvb = BK_PV[s]
            em.waits("tensor", [evB, evC,
                                state["v_done"]["vector"], state["v_done"]["scalar"]])
            bk.write("tensor", pvb)
            p0 = c["p0"]
            dvc = c["dvc"]
            tp = (0, p0)
            ev = None
            for ui, (kind, q0, nq, k0, nk, vj) in enumerate(c["units"]):
                if kind == "s0":
                    ev = em.op("tensor", lambda e, s=s, s4=s4, p0=p0, dvc=dvc, tp=tp: e.matmul(
                        PV[s][p0:p0 + 64, 0:8],
                        lhsT=VB[0:32, dvc: dvc + 64],
                        rhs=PTSB[0:32, s * 512: s * 512 + 8],
                        start=True, stop=True, tile_position=tp), inc="t")
                elif kind == "last":
                    ev = em.op("tensor", lambda e, s=s, s4=s4, p0=p0, dvc=dvc, vj=vj, nq=nq, tp=tp: e.matmul(
                        PV[s][p0:p0 + 64, 0:nq],
                        lhsT=VB[:, vj * 2048 + dvc: vj * 2048 + dvc + 64],
                        rhs=PTSB[0:128, s * 512: s * 512 + nq],
                        start=True, stop=True, tile_position=tp), inc="t")
                else:
                    em.op("tensor", lambda e, s=s, s4=s4, ui=ui, p0=p0, dvc=dvc, vj=vj, tp=tp: e.matmul(
                        PV[s][p0:p0 + 64, ui * 128: ui * 128 + 128],
                        lhsT=VB[:, vj * 2048 + dvc: vj * 2048 + dvc + 64],
                        rhs=PTSB[0:128, s * 512 + ui * 256: s * 512 + ui * 256 + 128],
                        start=True, stop=False, tile_position=tp))
                    ev = em.op("tensor", lambda e, s=s, s4=s4, ui=ui, p0=p0, dvc=dvc, vj=vj, tp=tp: e.matmul(
                        PV[s][p0:p0 + 64, ui * 128: ui * 128 + 128],
                        lhsT=VB[0:32, (vj + 1) * 2048 + dvc: (vj + 1) * 2048 + dvc + 64],
                        rhs=PTSB[0:32, s * 512 + ui * 256 + 128: s * 512 + ui * 256 + 256],
                        start=False, stop=True, tile_position=tp),
                        inc=("t" if ui == c["nu"] - 1 else None))
            c["ev_pv"] = ev
            bk.wrote(pvb, ev)

        def S4(bi):
            c = ctx[bi]
            s = c["s"]
            s4 = c["s4"]
            em.wait("scalar", c["ev_pv"])
            p0 = c["p0"]
            dv0 = p0
            ch = c["head"] // 2
            us = c["units"]
            if c["nu"] == 2:
                q0 = us[0][1]
                if tile == 0 and us[1][5] == 7:
                    em.op("scalar", lambda e, s=s, s4=s4, p0=p0, dv0=dv0, ch=ch, q0=q0: e.activation(
                        out=ATT[dv0:dv0 + 64, ch * 1024 + q0: ch * 1024 + 1024],
                        in_=PV[s][p0:p0 + 64, 0:248], func=ACTF.Copy, bias=0.0,
                        scale=1.0))
                    em.op("scalar", lambda e, s=s, s4=s4, p0=p0, dv0=dv0, ch=ch: e.activation(
                        out=SLV[dv0:dv0 + 64, ch * 8: ch * 8 + 8],
                        in_=PV[s][p0:p0 + 64, 248:256], func=ACTF.Copy, bias=0.0,
                        scale=1.0))
                else:
                    em.op("scalar", lambda e, s=s, s4=s4, p0=p0, dv0=dv0, ch=ch, q0=q0: e.activation(
                        out=ATT[dv0:dv0 + 64, ch * 1024 + q0: ch * 1024 + q0 + 256],
                        in_=PV[s][p0:p0 + 64, 0:256], func=ACTF.Copy, bias=0.0,
                        scale=1.0))
            else:
                kind, q0, nq, k0, nk, vj = us[0]
                if tile == 0 and kind == "main" and vj == 7:
                    em.op("scalar", lambda e, s=s, s4=s4, p0=p0, dv0=dv0, ch=ch, q0=q0: e.activation(
                        out=ATT[dv0:dv0 + 64, ch * 1024 + q0: ch * 1024 + 1024],
                        in_=PV[s][p0:p0 + 64, 0:120], func=ACTF.Copy, bias=0.0,
                        scale=1.0))
                    em.op("scalar", lambda e, s=s, s4=s4, p0=p0, dv0=dv0, ch=ch: e.activation(
                        out=SLV[dv0:dv0 + 64, ch * 8: ch * 8 + 8],
                        in_=PV[s][p0:p0 + 64, 120:128], func=ACTF.Copy, bias=0.0,
                        scale=1.0))
                else:
                    em.op("scalar", lambda e, s=s, s4=s4, p0=p0, dv0=dv0, ch=ch, q0=q0, nq=nq: e.activation(
                        out=ATT[dv0:dv0 + 64, ch * 1024 + q0: ch * 1024 + q0 + nq],
                        in_=PV[s][p0:p0 + 64, 0:nq], func=ACTF.Copy, bias=0.0,
                        scale=1.0))
            bk.did_read(BK_PV[s], "scalar", None)

        qk_i = 0
        for i in range(NB + 6):
            if 1 <= i <= NB:
                ctx_prev = ctx[i - 1]
            if 1 <= i <= NB and i < 2:
                pass
            if 1 <= i <= NB:
                S1(i - 1)
            if 2 <= i <= NB + 1:
                S2sum(i - 2)
            if 1 <= i <= NB and ctx[i - 1]["units"][0][0] in ("s0", "last"):
                if i < 2:
                    em.op("vector", lambda e: e.drain())
                S1tt(i - 1)
                if i < 3:
                    em.op("vector", lambda e: e.drain())
                S1rmax(i - 1)
            elif 1 <= i <= NB:
                if i < 2:
                    em.op("vector", lambda e: e.drain())
                S1rmax(i - 1)
            if 3 <= i <= NB + 2 and ctx[i - 3]["units"][0][0] == "main":
                S2corr(i - 3)
            evA = em.fence("vector", "v") if i <= NB + 2 else None
            incA[i] = evA
            ops_after_recip = 0
            if 3 <= i <= NB + 2:
                if evA is None:
                    em.op("vector", lambda e: e.drain())
                S2recip(i - 3)
            if 4 <= i <= NB + 3:
                S3ev(i - 4)
                ops_after_recip = 1
            if 3 <= i <= NB + 2:
                S2pnorm(i - 3, need_drain=(ops_after_recip == 0))
            evB = em.fence("vector", "v") if i >= 3 else None
            incB[i] = evB
            if 1 <= i <= NB:
                Sexp(i - 1, evA)
            if 5 <= i <= NB + 4:
                S4(i - 5)
            evC = em.fence("scalar", "a") if i >= 1 else None
            incC[i] = evC
            if i < NB:
                ctx[i] = mkctx(i)
                S0(i, evA)
            if 3 <= i <= NB + 2:
                S3tr(i - 3, evB)
            if 4 <= i <= NB + 3:
                S3pv(i - 4, evB, evC)
            while qk_i < len(qkq) and qkq[qk_i][0] <= i:
                _, pnext, gidx = qkq[qk_i]
                kind = gidx // len(nsl)
                off, n = nsl[gidx % len(nsl)]
                qk_group(tile, pnext, kind, off, n, gidx % len(nsl) == len(nsl) - 1)
                qk_i += 1
        state["attn_last_ev"] = incC[NB + 5]

    def proj_phase(tile):
        base = tile * TILE
        em.wait("tensor", state.get("attn_last_ev"))
        if tile == 1:
            em.wait("tensor", state.get("sliver_ev"))
        for n in range(4):
            sl = n % 2
            load_wb(sl, wp_d[n])
            ev_w = state["wb"][sl]
            for t in range(8):
                pm = (n * 8 + t) % 2
                pmb = BK_PM[pm]
                bk.write("tensor", pmb)
                em.wait("tensor", ev_w)
                for c in range(DC):
                    em.op("tensor", lambda e, c=c, t=t, sl=sl, pm=pm: e.matmul(
                        PM[pm][0:128, 0:512],
                        lhsT=ATT[:, c * 1024 + t * 128: c * 1024 + (t + 1) * 128],
                        rhs=WB[:, sl * DC * 512 + c * 512: sl * DC * 512 + (c + 1) * 512],
                        start=(c == 0), stop=False))
                ev_mm = em.op("tensor", lambda e, n=n, pm=pm: e.matmul(
                    PM[pm][0:128, 0:512], lhsT=ONE[0:1, 0:128],
                    rhs=PBB[0:1, n * 512:(n + 1) * 512],
                    start=False, stop=True), inc="t")
                bk.wrote(pmb, ev_mm)
                state["wb_cons"][sl] = ev_mm
                xs = t % 2
                em.wait("sync", state["xsl_cons"][xs])
                ev_x = em.op("sync", lambda e, t=t, n=n, xs=xs, base=base: e.dma_start(
                    out=XSL[:, xs * 512:(xs + 1) * 512],
                    in_=x_d[base + t * 128: base + (t + 1) * 128, n * 512:(n + 1) * 512]),
                    inc="xd", dma=True)
                em.waits("vector", [ev_mm, ev_x])
                em.wait("vector", state["xps_cons"][xs])
                ev_xp = em.op("vector", lambda e, xs=xs, pm=pm: e.tensor_tensor(
                    out=XPS[:, xs * 512:(xs + 1) * 512],
                    in0=PM[pm][0:128, 0:512], in1=XSL[:, xs * 512:(xs + 1) * 512],
                    op=ALU.add), inc="v")
                bk.did_read(pmb, "vector", ev_xp)
                state["xsl_cons"][xs] = ev_xp
                em.wait("scalar", ev_xp)
                ev_o = em.op("scalar", lambda e, t=t, n=n, xs=xs, base=base: e.dma_start(
                    out=y_d[base + t * 128: base + (t + 1) * 128, n * 512:(n + 1) * 512],
                    in_=XPS[:, xs * 512:(xs + 1) * 512]), inc="od", dma=True)
                state["xps_cons"][xs] = ev_o
                if n == 3:
                    state["y_written"][tile * 8 + t] = ev_o

    def ffn_mlp(tile, which):
        src, scw = (XNT, CW) if which == 0 else (ATT, 1024)
        dst = ATT if which == 0 else VB
        w_dd = w1_d if which == 0 else w3_d

        ev_e = None
        for m in range(16):
            wsl = m % 2
            load_wa(wsl, w_dd[m])
            ev_w = state["wa"][wsl]
            for n in range(2):
                pm = (m * 2 + n) % 2
                pmb = BK_PM[pm]
                bk.write("tensor", pmb)
                em.wait("tensor", ev_w)
                if which == 0:
                    em.wait("tensor", state["ln2_chunk_evs"][4 * n + 3])
                ev_mm = None
                for c in range(DC):
                    ev_mm = em.op("tensor", lambda e, c=c, n=n, wsl=wsl, pm=pm, src=src, scw=scw: e.matmul(
                        PM[pm][0:128, 0:512],
                        lhsT=WA[:, wsl * 2048 + c * 128: wsl * 2048 + (c + 1) * 128],
                        rhs=src[:, c * scw + n * 512: c * scw + n * 512 + 512],
                        start=(c == 0), stop=(c == DC - 1)),
                        inc=("t" if c == DC - 1 else None))
                bk.wrote(pmb, ev_mm)
                state["wa_cons"][wsl] = ev_mm
                em.wait("scalar", ev_mm)
                ev_e = em.op("scalar", lambda e, m=m, n=n, pm=pm, dst=dst: e.activation(
                    out=dst[:, m * 1024 + n * 512: m * 1024 + (n + 1) * 512],
                    in_=PM[pm][0:128, 0:512], func=ACTF.Silu, bias=ZAP[0:128, 0:1], scale=1.0),
                    inc="a")
                bk.did_read(pmb, "scalar", ev_e)
        state["mlp_ev"] = ev_e

    def ffn_out(tile):
        sc_ps = [None, None, PV[0], PV[1]]
        sc_bk = [BK_PS[0], BK_PS[1], BK_PV[0], BK_PV[1]]
        ev_h3_gate = state.get("h1_mm_last")
        ev_acc = None
        for n in range(4):
            sl = n % 2
            load_wb(sl, w2_d[n])
            ev_w = state["wb"][sl]
            bk.write("tensor", sc_bk[n])
            ev_s = None
            for t in range(8):
                pm = (n * 8 + t) % 2
                pmb = BK_PM[pm]
                bk.write("tensor", pmb)
                em.wait("tensor", ev_w)
                ev_mm = None
                for c in range(DC):
                    ev_mm = em.op("tensor", lambda e, c=c, t=t, sl=sl, pm=pm: e.matmul(
                        PM[pm][0:128, 0:512],
                        lhsT=VB[:, c * 1024 + t * 128: c * 1024 + (t + 1) * 128],
                        rhs=WB[:, sl * DC * 512 + c * 512: sl * DC * 512 + (c + 1) * 512],
                        start=(c == 0), stop=(c == DC - 1)),
                        inc=("t" if c == DC - 1 else None))
                bk.wrote(pmb, ev_mm)
                state["wb_cons"][sl] = ev_mm
                em.wait("scalar", ev_mm)
                em.wait("scalar", ev_h3_gate)
                ev_h3 = em.op("scalar", lambda e, t=t, n=n, pm=pm: e.activation(
                    out=XNT[:, t * 2048 + n * 512: t * 2048 + (n + 1) * 512],
                    in_=PM[pm][0:128, 0:512], func=ACTF.Copy, bias=0.0, scale=1.0),
                    inc="a")
                bk.did_read(pmb, "scalar", ev_h3)
                em.waits("tensor", [ev_h3, state.get("gates_ev")])
                ev_s = em.op("tensor", lambda e, t=t, n=n: e.matmul(
                    (PSW[0:8, 512 * n:512 * (n + 1)] if n < 2 else sc_ps[n][0:8, 0:512]),
                    lhsT=GT[:, t * 8:(t + 1) * 8],
                    rhs=XNT[:, t * 2048 + n * 512: t * 2048 + (n + 1) * 512],
                    start=(t == 0), stop=(t == 7)), inc="t")
            bk.wrote(sc_bk[n], ev_s)
            em.wait("vector", ev_s)
            src_ps = (lambda n=n: PSW[0:8, 512 * n:512 * (n + 1)] if n < 2 else sc_ps[n][0:8, 0:512])()
            if tile == 0:
                ev_acc = em.op("vector", lambda e, n=n, sp=src_ps: e.tensor_copy(
                    FIN[0:8, n * 512:(n + 1) * 512], sp), inc="v")
            else:
                ev_acc = em.op("vector", lambda e, n=n, sp=src_ps: e.tensor_tensor(
                    out=FIN[0:8, n * 512:(n + 1) * 512],
                    in0=FIN[0:8, n * 512:(n + 1) * 512],
                    in1=sp, op=ALU.add), inc="v")
            bk.did_read(sc_bk[n], "vector", ev_acc)
        state["acc_ev"] = ev_acc

    def finale():
        em.wait("gpsimd", state["acc_ev"])
        ev1 = em.op("gpsimd", lambda e: e.dma_start(
            out=cc_i[:, :], in_=FIN[0:8, 0:D]), inc="gd", dma=True)
        em.wait("gpsimd", ev1)
        ev2 = em.op("gpsimd", lambda e: e.collective_compute(
            "AllReduce", ALU.add, replica_groups=[list(range(8))],
            ins=[cc_i[:, :]], outs=[cc_o[:, :]]), inc="g")
        em.wait("gpsimd", ev2)
        em.waits("gpsimd", [state["xsl_cons"][0], state["xsl_cons"][1],
                            state["xps_cons"][0], state["xps_cons"][1]])
        ev3a = em.op("gpsimd", lambda e: e.dma_start(
            out=XSL[0:8, 0:1024], in_=cc_o[:, 0:1024]), inc="gd", dma=True)
        ev3b = em.op("gpsimd", lambda e: e.dma_start(
            out=XPS[0:8, 0:1024], in_=cc_o[:, 1024:2048]), inc="gd", dma=True)
        em.wait("sync", ev1)
        em.wait("sync", state["y_written"][0])
        ev4 = em.op("sync", lambda e: e.dma_start(
            out=FIN[0:8, 0:D], in_=y_d[0:8, :]), inc="xd", dma=True)
        em.waits("vector", [ev3a, ev3b, ev4])
        em.op("vector", lambda e: e.tensor_scalar(
            out=XSL[0:8, 0:1024], in0=XSL[0:8, 0:1024], scalar1=CM[0:8, 0:1],
            scalar2=None, op0=ALU.mult))
        em.op("vector", lambda e: e.tensor_scalar(
            out=XPS[0:8, 0:1024], in0=XPS[0:8, 0:1024], scalar1=CM[0:8, 0:1],
            scalar2=None, op0=ALU.mult))
        em.op("vector", lambda e: e.tensor_tensor(
            out=XSL[0:8, 0:1024], in0=XSL[0:8, 0:1024], in1=FIN[0:8, 0:1024],
            op=ALU.add))
        ev5 = em.op("vector", lambda e: e.tensor_tensor(
            out=XPS[0:8, 0:1024], in0=XPS[0:8, 0:1024], in1=FIN[0:8, 1024:2048],
            op=ALU.add), inc="v")
        em.wait("scalar", ev5)
        em.op("scalar", lambda e: e.dma_start(
            out=y_d[0:8, 0:1024], in_=XSL[0:8, 0:1024]), inc="od", dma=True)
        em.op("scalar", lambda e: e.dma_start(
            out=y_d[0:8, 1024:2048], in_=XPS[0:8, 0:1024]), inc="od", dma=True)
        em.op("scalar", lambda e, v=em.counts["od"]: e.wait_ge(em.sems["od"], v))

    def emit_all(nsteps):
        cnt = [0]
        def step(f):
            if nsteps is None or cnt[0] < nsteps:
                f()
            cnt[0] += 1
        for tile in (0, 1):
            step(lambda tile=tile: ln_phase(tile, is_ln2=False))
            step(lambda tile=tile: v_phase(tile))
            step(lambda tile=tile: qk_attn_phase(tile))
            step(lambda tile=tile: proj_phase(tile))
            step(lambda tile=tile: ln_phase(tile, is_ln2=True))
            def _mlp0(tile=tile):
                ffn_mlp(tile, 0)
                state["h1_mm_last"] = state["wa_cons"][1]
            step(_mlp0)
            def _mlp1(tile=tile):
                ffn_mlp(tile, 1)
                state["ffn_h2_done"] = state["wa_cons"][1]
            step(_mlp1)
            step(lambda tile=tile: ffn_out(tile))
        step(finale)
    emit_all(steps)

    TENS = {"XNT": XNT, "VB": VB, "ATT": ATT, "QKB": QKB, "GT": GT, "T1R": T1R,
            "SLV": SLV, "FIN": FIN, "STT": STT, "RPS": RPS, "XCH": XCH,
            "SST": SST, "PR": PR, "PTSB": PTSB}
    if dbg:
        for s in ("t", "v", "a", "wd", "xd", "od", "gd", "g"):
            em.wait("scalar", (s, em.counts[s]))
        for name in dbg:
            th = TENS[name]
            shp = [th.shape[0], th.shape[1]]
            dt = th.ap().dtype
            dd = nc.dram_tensor("dbg_" + name, shp, dt, kind="ExternalOutput").ap()
            em.op("scalar", lambda e, dd=dd, th=th: e.dma_start(
                out=dd[:, :], in_=th[:, :]), inc="od", dma=True)
        em.op("scalar", lambda e, v=em.counts["od"]: e.wait_ge(em.sems["od"], v))

    with nc.Block() as block:
        @block.sync
        def _(e):
            for f in em.ops["sync"]:
                f(e)

        @block.tensor
        def _(e):
            for f in em.ops["tensor"]:
                f(e)

        @block.vector
        def _(e):
            for f in em.ops["vector"]:
                f(e)

        @block.scalar
        def _(e):
            for f in em.ops["scalar"]:
                f(e)

        @block.gpsimd
        def _(e):
            for f in em.ops["gpsimd"]:
                f(e)

    es.close()
    return nc



def _host_consts():
    mm_main = np.zeros((128, 144), np.float32)
    b01 = np.zeros((128, 144), np.float32)
    for r in range(128):
        for c in range(r, r + 16):
            d = abs(c - r - 8)
            mm_main[r, c] = SCALE * (WIN - d / 2.0) / WIN
            b01[r, c] = 1.0
    mm2 = np.concatenate([mm_main, mm_main], axis=1)
    mm_s0 = np.zeros((8, 16), np.float32)
    am_s0 = np.full((8, 16), NEG, np.float32)
    for r in range(8):
        for c in range(0, r + 8):
            d = abs(c - r)
            mm_s0[r, c] = SCALE * (WIN - d / 2.0) / WIN
            am_s0[r, c] = 0.0
    mm_last = np.zeros((120, 128), np.float32)
    am_last = np.full((120, 128), NEG, np.float32)
    for r in range(120):
        for c in range(r, min(r + 16, 128)):
            d = abs(c - r - 8)
            mm_last[r, c] = SCALE * (WIN - d / 2.0) / WIN
            am_last[r, c] = 0.0
    return mm2, b01, mm_s0, am_s0, mm_last, am_last


def _prep_weights(qkv_w, proj_w, proj_b, w1, w3, w2, wr):
    f = np.float32
    qkv_w = np.asarray(qkv_w, f)
    qk = np.ascontiguousarray(qkv_w[0:4096])
    T = qk.T
    Bq = T.reshape(16, 128, 32, 128).transpose(2, 1, 0, 3)
    order = []
    for p in range(16):
        order += [p, 16 + p]
    wqk = np.ascontiguousarray(Bq[order]).reshape(32, 128, 16 * 128)

    def nslice(w):
        T = np.asarray(w, f).T
        return np.ascontiguousarray(
            T.reshape(16, 128, 4, 512).transpose(2, 1, 0, 3)).reshape(4, 128, 16 * 512)

    wv = nslice(qkv_w[4096:6144])
    wp = nslice(proj_w)
    w2n = nslice(w2)

    def mblk(w):
        T = np.asarray(w, f).T
        return np.ascontiguousarray(
            T.reshape(16, 128, 16, 128).transpose(2, 1, 0, 3)).reshape(16, 128, 16 * 128)

    w1m = mblk(w1)
    w3m = mblk(w3)
    wrh = np.ascontiguousarray(
        np.asarray(wr, f).T.reshape(16, 128, 8).transpose(1, 0, 2)).reshape(128, 128)
    pb = np.ascontiguousarray(np.asarray(proj_b, f).reshape(1, D))
    return wqk, wv, wp, w1m, w3m, w2n, wrh, pb


def kernel(L0_attn_x, L0_attn_qkv_w, L0_attn_proj_w, L0_attn_proj_b,
           L0_ffn_w1, L0_ffn_w3, L0_ffn_w2, L0_ffn_wr):
    x = np.ascontiguousarray(np.asarray(L0_attn_x, np.float32))
    wqk, wv, wp, w1m, w3m, w2n, wrh, pb = _prep_weights(
        L0_attn_qkv_w, L0_attn_proj_w, L0_attn_proj_b, L0_ffn_w1, L0_ffn_w3,
        L0_ffn_w2, L0_ffn_wr)
    mm2, b01, mm_s0, am_s0, mm_last, am_last = _host_consts()
    ident = np.eye(128, dtype=np.float32)
    nc = build_nc()
    in_maps = []
    for c in range(8):
        cm = np.full((8, 1), 1.0 if c == 0 else 0.0, np.float32)
        in_maps.append({
            "x": np.ascontiguousarray(x[c]),
            "wqk": wqk, "wv": wv, "wp": wp, "w1": w1m, "w3": w3m, "w2": w2n,
            "wr": wrh, "pb": pb,
            "mm2": mm2, "b01": b01, "mm_s0": mm_s0,
            "am_s0": am_s0, "mm_last": mm_last, "am_last": am_last,
            "ident": ident, "coremask": cm,
        })
    res = run_bass_kernel_spmd(nc, in_maps, core_ids=list(range(8)))
    return np.stack([np.asarray(res.results[c]["y"]) for c in range(8)], axis=0)


# version 07
# speedup vs baseline: 1.6245x; 1.0314x over previous
import numpy as np
import concourse.bass as bass
import concourse.mybir as mybir
from concourse.bass_utils import run_bass_kernel_spmd
from contextlib import ExitStack

F32 = mybir.dt.float32
BF16 = mybir.dt.bfloat16
ALU = mybir.AluOpType
ACTF = mybir.ActivationFunctionType
AX = mybir.AxisListType

D = 2048
DC = 16
NT = 2048
TILE = 1024
H = 32
HD = 64
CW = 1040
NEG = -1.0e30
EPS = 1e-5
SCALE = HD ** -0.5
WIN = 8

ENGINES = ("sync", "tensor", "vector", "scalar", "gpsimd")


class Em:
    def __init__(self, nc, es):
        self.nc = nc
        self.ops = {e: [] for e in ENGINES}
        self.sems = {}
        self.counts = {}
        self.last_wait = {e: {} for e in ENGINES}
        self._vec_dirty = False
        for name in ("t", "v", "a", "g", "wd", "xd", "od", "gd"):
            self.sems[name] = es.enter_context(nc.semaphore("sem_" + name))
            self.counts[name] = 0

    def fence(self, eng, inc):
        self.counts[inc] += 1
        val = self.counts[inc]
        h = self.sems[inc]
        self.ops[eng].append(lambda e, h=h: e.drain().then_inc(h, 1))
        if eng == "vector":
            self._vec_dirty = False
        return (inc, val)

    def wait(self, eng, ev):
        if ev is None:
            return
        s, val = ev
        if val <= 0:
            return
        if self.last_wait[eng].get(s, 0) >= val:
            return
        self.last_wait[eng][s] = val
        h = self.sems[s]
        self.ops[eng].append(lambda e, h=h, val=val: e.wait_ge(h, val))

    def waits(self, eng, evs):
        for ev in evs:
            self.wait(eng, ev)

    def op(self, eng, fn, inc=None, dma=False, dep=True):
        if eng == "vector":
            if dep and self._vec_dirty:
                self.ops[eng].append(lambda e: e.drain())
            self._vec_dirty = True
        if inc is None:
            self.ops[eng].append(fn)
            return None
        n = 16 if dma else 1
        self.counts[inc] += n
        val = self.counts[inc]
        h = self.sems[inc]
        if not dma and eng in ("vector", "scalar"):
            self.ops[eng].append(fn)
            self.ops[eng].append(lambda e, h=h, n=n: e.drain().then_inc(h, n))
            if eng == "vector":
                self._vec_dirty = False
        else:
            self.ops[eng].append(lambda e, fn=fn, h=h, n=n: fn(e).then_inc(h, n))
        return (inc, val)


class Banks:
    def __init__(self, em):
        self.em = em
        self.w = {}
        self.r = {}

    def write(self, eng, bank):
        for ev in self.r.get(bank, {}).values():
            self.em.wait(eng, ev)
        self.em.wait(eng, self.w.get(bank))

    def wrote(self, bank, ev):
        self.w[bank] = ev
        self.r[bank] = {}

    def read(self, eng, bank):
        self.em.wait(eng, self.w.get(bank))

    def did_read(self, bank, eng, ev):
        self.r.setdefault(bank, {})[eng] = ev


def build_nc(steps=None, dbg=()):
    nc = bass.Bass("TRN2", target_bir_lowering=False)
    es = ExitStack()

    x_d = nc.dram_tensor("x", [NT, D], F32, kind="ExternalInput").ap()
    wqk_d = nc.dram_tensor("wqk", [32, 128, DC * 128], F32, kind="ExternalInput").ap()
    wv_d = nc.dram_tensor("wv", [4, 128, DC * 512], F32, kind="ExternalInput").ap()
    wp_d = nc.dram_tensor("wp", [4, 128, DC * 512], F32, kind="ExternalInput").ap()
    w1_d = nc.dram_tensor("w1", [16, 128, DC * 128], F32, kind="ExternalInput").ap()
    w3_d = nc.dram_tensor("w3", [16, 128, DC * 128], F32, kind="ExternalInput").ap()
    w2_d = nc.dram_tensor("w2", [4, 128, DC * 512], F32, kind="ExternalInput").ap()
    wr_d = nc.dram_tensor("wr", [128, DC * 8], F32, kind="ExternalInput").ap()
    pb_d = nc.dram_tensor("pb", [1, D], F32, kind="ExternalInput").ap()
    mm2_d = nc.dram_tensor("mm2", [128, 288], F32, kind="ExternalInput").ap()
    b01_d = nc.dram_tensor("b01", [128, 144], F32, kind="ExternalInput").ap()
    mms_d = nc.dram_tensor("mm_s0", [8, 16], F32, kind="ExternalInput").ap()
    ams_d = nc.dram_tensor("am_s0", [8, 16], F32, kind="ExternalInput").ap()
    mml_d = nc.dram_tensor("mm_last", [120, 128], F32, kind="ExternalInput").ap()
    aml_d = nc.dram_tensor("am_last", [120, 128], F32, kind="ExternalInput").ap()
    id_d = nc.dram_tensor("ident", [128, 128], F32, kind="ExternalInput").ap()
    cm_d = nc.dram_tensor("coremask", [8, 1], F32, kind="ExternalInput").ap()
    y_d = nc.dram_tensor("y", [NT, D], F32, kind="ExternalOutput").ap()
    cc_i = nc.dram_tensor("cc_i", [8, D], F32)
    cc_o = nc.dram_tensor("cc_o", [8, D], F32)
    dbg_d = {}

    sb = lambda name, shp, dt: es.enter_context(nc.sbuf_tensor(name, shp, dt))
    XNT = sb("XNT", [128, DC * CW], BF16)
    VB = sb("VB", [128, 9 * 2048], BF16)
    ATT = sb("ATT", [128, DC * 1024], BF16)
    SLV = sb("SLV", [128, DC * 8], BF16)
    QKB = sb("QKB", [128, 4 * CW], BF16)
    WA = sb("WA", [128, 2 * 2048], BF16)
    WB = sb("WB", [128, 2 * DC * 512], BF16)
    XCH = sb("XCH", [128, 2 * 2048], F32)
    XSL = sb("XSL", [128, 2 * 512], F32)
    XPS = sb("XPS", [128, 2 * 512], F32)
    T1R = sb("T1R", [128, 4 * 288], F32)
    PR = sb("PR", [128, 4 * 320], BF16)
    PS0R = sb("PS0R", [32, 32], BF16)
    PTSB = sb("PTSB", [128, 2 * 512], BF16)
    STT = sb("STT", [128, 32], F32)
    SST = sb("SST", [128, 32], F32)
    RPS = sb("RPS", [128, 40], F32)
    GT = sb("GT", [128, 8 * 8], BF16)
    MM2 = sb("MM2", [128, 288], BF16)
    B01 = sb("B01", [128, 144], BF16)
    MMS = sb("MMS", [8, 16], F32)
    AMS = sb("AMS", [8, 16], F32)
    MML = sb("MML", [120, 128], BF16)
    AML = sb("AML", [120, 128], BF16)
    IDB = sb("IDB", [128, 128], BF16)
    IDF = sb("IDF", [128, 128], F32)
    ONE = sb("ONE", [1, 128], BF16)
    PBB = sb("PBB", [1, D], BF16)
    WR = sb("WR", [128, DC * 8], F32)
    CM = sb("CM", [8, 1], F32)
    ZAP = sb("ZAP", [128, 1], F32)
    EPSAP = sb("EPSAP", [128, 1], F32)
    SQSL = sb("SQSL", [128, 2048], F32)
    FIN = sb("FIN", [8, D], F32)

    ps = lambda name, shp, dt: es.enter_context(nc.psum_tensor(name, shp, dt))
    PM = [ps("PM0", [128, 512], F32), ps("PM1", [128, 512], F32)]
    PSW = ps("PSW", [128, 1024], F32)
    PV = [ps("PV0", [128, 512], F32), ps("PV1", [128, 512], F32)]
    PT = [ps("PT0", [128, 1024], BF16), ps("PT1", [128, 1024], BF16)]
    BK_PM, BK_PS, BK_PV, BK_PT = (0, 1), ("PSW", "PSW"), (4, 5), (6, 7)
    PS = [PSW, PSW]

    em = Em(nc, es)
    bk = Banks(em)

    ld_evs = []
    for dst, src in ((MMS, mms_d), (AMS, ams_d),
                     (IDF, id_d), (WR, wr_d),
                     (CM, cm_d)):
        ld_evs.append(em.op("sync", lambda e, d=dst, s=src: e.dma_start(
            out=d[:, :], in_=s[:, :]), inc="xd", dma=True))
    ld_evs.append(em.op("gpsimd", lambda e: e.dma_start(
        out=IDB[:, :], in_=id_d[:, :]), inc="wd", dma=True))
    ld_evs.append(em.op("gpsimd", lambda e: e.dma_start(
        out=PBB[:, :], in_=pb_d[:, :]), inc="wd", dma=True))
    ld_evs.append(em.op("gpsimd", lambda e: e.dma_start(
        out=MM2[:, :], in_=mm2_d[:, :]), inc="wd", dma=True))
    ld_evs.append(em.op("gpsimd", lambda e: e.dma_start(
        out=MML[:, :], in_=mml_d[:, :]), inc="wd", dma=True))
    ld_evs.append(em.op("gpsimd", lambda e: e.dma_start(
        out=AML[:, :], in_=aml_d[:, :]), inc="wd", dma=True))
    ld_evs.append(em.op("gpsimd", lambda e: e.dma_start(
        out=B01[:, :], in_=b01_d[:, :]), inc="wd", dma=True))
    em.waits("vector", ld_evs)
    em.op("vector", lambda e: e.memset(ZAP[:, :], 0.0))
    em.op("vector", lambda e: e.memset(EPSAP[:, :], EPS))
    for s_ in range(4):
        for u_ in range(2):
            em.op("vector", lambda e, o=s_ * 320 + u_ * 160: e.memset(
                PR[:, o + 144: o + 160], 0.0))
    em.op("vector", lambda e: e.memset(PS0R[:, :], 0.0))
    em.op("vector", lambda e: e.memset(VB[0:32, 8 * 2048:9 * 2048], 0.0))
    ev_memset = em.op("vector", lambda e: e.memset(ONE[:, :], 1.0), inc="v")
    em.waits("tensor", ld_evs + [ev_memset])
    em.waits("scalar", ld_evs + [ev_memset])

    state = {
        "wa": [None, None], "wb": [None, None],
        "wa_cons": [None, None], "wb_cons": [None, None],
        "xch_cons": [None, None],
        "qk_cons": [None, None, None, None],
        "xsl_cons": [None, None], "xps_cons": [None, None],
        "t1_cons": [None, None], "pr_cons": [None, None],
        "ptm_cons": [None, None], "pth_cons": [None, None],
        "xbk_cons": [None, None], "xbk2_cons": [None, None],
        "f32r_cons": [None, None],
        "y_written": {},
        "ln_ev": {"vector": None, "scalar": None},
        "ln1_chunk_evs": {},
        "ln2_chunk_evs": {},
        "v_done": {"vector": None, "scalar": None},
    }

    def load_wa(slot, src_ap):
        em.wait("gpsimd", state["wa_cons"][slot])
        ev = em.op("gpsimd", lambda e, s=src_ap, sl=slot: e.dma_start(
            out=WA[:, sl * 2048:(sl + 1) * 2048], in_=s,
            max_dma_last_dim=8192), inc="wd", dma=True)
        state["wa"][slot] = ev
        return ev

    def load_wb(slot, src_ap):
        em.wait("gpsimd", state["wb_cons"][slot])
        ev = em.op("gpsimd", lambda e, s=src_ap, sl=slot: e.dma_start(
            out=WB[:, sl * DC * 512:(sl + 1) * DC * 512], in_=s,
            max_dma_last_dim=8192), inc="wd", dma=True)
        state["wb"][slot] = ev
        return ev

    def ln_chunk(tile, i, nrow, rows_src, is_ln2):
        sl = i % 2
        em.wait("sync", state["xch_cons"][sl])
        if is_ln2:
            em.wait("sync", state["y_written"][rows_src // 128])
            src = y_d[rows_src:rows_src + nrow, :]
        else:
            src = x_d[rows_src:rows_src + nrow, :]
        ev_x = em.op("sync", lambda e, s=src, sl=sl, n=nrow: e.dma_start(
            out=XCH[0:n, sl * 2048: sl * 2048 + 2048], in_=s), inc="xd", dma=True)
        em.wait("scalar", ev_x)
        em.wait("scalar", state.get("sq_cons"))
        ev_sq = None
        for q in range(2):
            ev_sq = em.op("scalar", lambda e, q=q, sl=sl, n=nrow: e.activation(
                out=SQSL[0:n, q * 1024:(q + 1) * 1024],
                in_=XCH[0:n, sl * 2048 + q * 1024: sl * 2048 + (q + 1) * 1024],
                func=ACTF.Square, bias=ZAP[0:n, 0:1], scale=1.0),
                inc=("a" if q == 1 else None))
        em.wait("vector", ev_x)
        em.op("vector", lambda e, sl=sl, n=nrow: e.reduce_sum(
            STT[0:n, 23:24], XCH[0:n, sl * 2048: sl * 2048 + 2048], axis=AX.X),
            dep=False)
        em.wait("vector", ev_sq)
        em.op("vector", lambda e, n=nrow: e.reduce_sum(
            STT[0:n, 16:17], SQSL[0:n, 0:1024], axis=AX.X), dep=False)
        em.op("vector", lambda e, n=nrow: e.tensor_scalar(
            out=STT[0:n, 24:25], in0=STT[0:n, 23:24], scalar1=1.0 / D,
            scalar2=None, op0=ALU.mult), dep=False)
        ev_sq2 = em.op("vector", lambda e, n=nrow: e.reduce_sum(
            STT[0:n, 17:18], SQSL[0:n, 1024:2048], axis=AX.X), dep=False, inc="v")
        state["sq_cons"] = ev_sq2
        em.op("vector", lambda e, n=nrow: e.tensor_tensor(
            out=STT[0:n, 18:19], in0=STT[0:n, 24:25], in1=STT[0:n, 24:25],
            op=ALU.mult), dep=False)
        em.op("vector", lambda e, n=nrow: e.tensor_tensor(
            out=STT[0:n, 22:23], in0=STT[0:n, 16:17], in1=STT[0:n, 17:18],
            op=ALU.add), dep=False)
        em.op("vector", lambda e, n=nrow: e.tensor_scalar(
            out=STT[0:n, 26:27], in0=STT[0:n, 22:23], scalar1=1.0 / D,
            scalar2=None, op0=ALU.mult))
        ev_mv = em.op("vector", lambda e, n=nrow: e.tensor_tensor(
            out=STT[0:n, 25:26], in0=STT[0:n, 26:27], in1=STT[0:n, 18:19],
            op=ALU.subtract), inc="v")
        em.wait("scalar", ev_mv)
        ev_sd = em.op("scalar", lambda e, n=nrow: e.activation(
            out=STT[0:n, 27:28], in_=STT[0:n, 25:26], func=ACTF.Sqrt,
            bias=EPSAP[0:n, 0:1], scale=1.0), inc="a")
        em.wait("vector", ev_sd)
        em.op("vector", lambda e, n=nrow: e.reciprocal(
            STT[0:n, 28:29], STT[0:n, 27:28]))
        ev_nb = em.op("vector", lambda e, n=nrow: e.tensor_scalar(
            out=STT[0:n, 29:30], in0=STT[0:n, 24:25], scalar1=STT[0:n, 28:29],
            scalar2=-1.0, op0=ALU.mult, op1=ALU.mult), inc="v")
        em.wait("scalar", ev_nb)

        t0 = i * 128
        ev_nm = None
        chunk_evs = []
        if not is_ln2:
            XN = T1R[:, 0:1024].bitcast(BF16)
            em.wait("scalar", state.get("xn_tr_last"))
            em.wait("scalar", state.get("gates_ev"))
            em.wait("scalar", state.get("ln2q_mm"))
            ev_nm = em.op("scalar", lambda e, XN=XN, n=nrow, sl=sl: e.activation(
                out=XN[0:n, 0:2048],
                in_=XCH[0:n, sl * 2048: sl * 2048 + 2048],
                func=ACTF.Identity, bias=STT[0:n, 29:30], scale=STT[0:n, 28:29]),
                inc="a")
            for c in range(DC):
                bs = c % 2
                ptb = BK_PT[bs]
                bk.write("tensor", ptb)
                em.wait("tensor", ev_nm)
                ev_tr = em.op("tensor", lambda e, bs=bs, c=c, XN=XN, n=nrow: e.transpose(
                    PT[bs][0:128, 0:n],
                    XN[0:n, c * 128: (c + 1) * 128],
                    IDB[0:n, 0:n]), inc="t")
                bk.wrote(ptb, ev_tr)
                state["xn_tr_last"] = ev_tr
                eng = "vector" if c % 2 == 0 else "scalar"
                em.wait(eng, ev_tr)
                if eng == "vector":
                    ev_e = em.op("vector", lambda e, bs=bs, c=c, t0=t0, n=nrow: e.tensor_copy(
                        XNT[:, c * CW + t0: c * CW + t0 + n], PT[bs][0:128, 0:n]),
                        inc="v")
                else:
                    ev_e = em.op("scalar", lambda e, bs=bs, c=c, t0=t0, n=nrow: e.activation(
                        out=XNT[:, c * CW + t0: c * CW + t0 + n],
                        in_=PT[bs][0:128, 0:n], func=ACTF.Copy, bias=0.0, scale=1.0),
                        inc="a")
                bk.did_read(ptb, eng, ev_e)
                state["ln_ev"][eng] = ev_e
                chunk_evs.append(ev_e)
            state["ln1_chunk_evs"][i] = chunk_evs[-2:]
            state["xch_cons"][sl] = ev_nm
            return
        XB4 = T1R[:, 0:512]
        F4 = T1R[:, 512:1024]
        QP = [PM[0], PM[1], PV[0], PV[1]]
        QB = [BK_PM[0], BK_PM[1], BK_PV[0], BK_PV[1]]
        ev_cb = None
        for q in range(4):
            em.wait("scalar", state.get("ln2q_tr"))
            ev_nm = None
            for b in range(4):
                c = 4 * q + b
                ev_nm = em.op("scalar", lambda e, c=c, b=b, sl=sl, n=nrow: e.activation(
                    out=XB4[0:n, b * 128: b * 128 + 128],
                    in_=XCH[0:n, sl * 2048 + c * 128: sl * 2048 + (c + 1) * 128],
                    func=ACTF.Identity, bias=STT[0:n, 29:30], scale=STT[0:n, 28:29]),
                    inc=("a" if b == 3 else None))
            ev_tr = None
            for b in range(4):
                bk.write("tensor", QB[b])
                em.wait("tensor", ev_nm)
                ev_tr = em.op("tensor", lambda e, b=b, n=nrow: e.transpose(
                    QP[b][0:128, 0:n], XB4[0:n, b * 128: b * 128 + 128],
                    IDF[0:n, 0:n]), inc=("t" if b == 3 else None))
                if b == 3:
                    for bb in range(4):
                        bk.wrote(QB[bb], ev_tr)
            state["ln2q_tr"] = ev_tr
            em.wait("vector", ev_tr)
            em.wait("vector", state.get("ln2q_mm"))
            ev_f = None
            for b in range(4):
                ev_f = em.op("vector", lambda e, b=b, n=nrow: e.tensor_copy(
                    F4[:, b * 128: b * 128 + n], QP[b][0:128, 0:n]),
                    inc=("v" if b == 3 else None), dep=False)
            for b in range(4):
                bk.did_read(QB[b], "vector", ev_f)
            em.wait("scalar", ev_f)
            for b in range(4):
                c = 4 * q + b
                ev_cb = em.op("scalar", lambda e, b=b, c=c, t0=t0, n=nrow: e.activation(
                    out=XNT[:, c * CW + t0: c * CW + t0 + n],
                    in_=QP[b][0:128, 0:n], func=ACTF.Copy, bias=0.0,
                    scale=1.0), inc=("a" if b == 3 else None))
                bk.did_read(QB[b], "scalar", ev_cb)
            state["ln_ev"]["scalar"] = ev_cb
            bk.write("tensor", BK_PS[0])
            em.wait("tensor", ev_f)
            ev_rmm = None
            for b in range(4):
                c = 4 * q + b
                ev_rmm = em.op("tensor", lambda e, b=b, c=c, i=i, n=nrow: e.matmul(
                    PSW[0:n, 8 * i: 8 * i + 8],
                    lhsT=F4[:, b * 128: b * 128 + n],
                    rhs=WR[:, 8 * c:8 * c + 8],
                    start=(c == 0), stop=(c == DC - 1)),
                    inc=("t" if b == 3 else None))
            state["ln2q_mm"] = ev_rmm
            if q == 3:
                bk.wrote(BK_PS[0], ev_rmm)
        state["ln2_chunk_evs"][i] = ev_cb
        state["xch_cons"][sl] = ev_nm

    def ln_phase(tile, is_ln2):
        base = tile * TILE
        ntok = 1040 if (tile == 0 and not is_ln2) else 1024
        nch = (ntok + 127) // 128
        for i in range(nch):
            nrow = min(128, ntok - i * 128)
            ln_chunk(tile, i, nrow, base + i * 128, is_ln2)
        if is_ln2:
            router_post(tile)

    def router_post(tile):
        RL = T1R
        bk.read("vector", BK_PS[0])
        em.wait("vector", state["t1_cons"][0])
        em.wait("vector", state["t1_cons"][1])
        ev_rl = em.op("vector", lambda e: e.tensor_copy(
            RL[:, 0:128], PSW[0:128, 0:128]), inc="v")
        bk.did_read(BK_PS[0], "vector", ev_rl)
        ev_g = None
        for i in range(8):
            em.op("vector", lambda e, i=i: e.reduce_max(
                RPS[:, 32:33], RL[:, 8 * i:8 * i + 8], axis=AX.X, negate=True))
            ev1 = em.op("vector", lambda e, i=i: e.tensor_scalar(
                out=RPS[:, 0:8], in0=RL[:, 8 * i:8 * i + 8], scalar1=RPS[:, 32:33],
                scalar2=None, op0=ALU.add), inc="v")
            em.wait("scalar", ev1)
            ev2 = em.op("scalar", lambda e: e.activation(
                out=RPS[:, 0:8], in_=RPS[:, 0:8], func=ACTF.Exp, bias=ZAP[:, 0:1],
                scale=1.0), inc="a")
            em.wait("vector", ev2)
            em.op("vector", lambda e: e.reduce_sum(
                RPS[:, 33:34], RPS[:, 0:8], axis=AX.X))
            em.op("vector", lambda e: e.reciprocal(RPS[:, 34:35], RPS[:, 33:34]))
            em.op("vector", lambda e: e.tensor_scalar(
                out=RPS[:, 0:8], in0=RPS[:, 0:8], scalar1=RPS[:, 34:35],
                scalar2=None, op0=ALU.mult))
            em.op("vector", lambda e: e.max(RPS[:, 8:16], RPS[:, 0:8]))
            em.op("vector", lambda e: e.tensor_scalar(
                out=RPS[:, 16:24], in0=RPS[:, 0:8], scalar1=RPS[:, 9:10],
                scalar2=None, op0=ALU.is_ge))
            em.op("vector", lambda e: e.tensor_tensor(
                out=RPS[:, 35:36], in0=RPS[:, 8:9], in1=RPS[:, 9:10], op=ALU.add))
            em.op("vector", lambda e: e.reciprocal(RPS[:, 36:37], RPS[:, 35:36]))
            ev_g = em.op("vector", lambda e, i=i: e.scalar_tensor_tensor(
                out=GT[:, 8 * i:8 * i + 8], in0=RPS[:, 16:24], scalar=RPS[:, 36:37],
                in1=RPS[:, 0:8], op0=ALU.mult, op1=ALU.mult), inc="v")
        state["gates_ev"] = ev_g
        state["rl_ev"] = ev_rl

    def v_phase(tile):
        ntok = 1040 if tile == 0 else 1024
        nj = 9 if tile == 0 else 8
        ev_e = None
        for n in range(4):
            sl = n % 2
            load_wb(sl, wv_d[n])
            ev_w = state["wb"][sl]
            for j in range(nj):
                nrow = min(128, ntok - j * 128)
                pm = (n * 9 + j) % 2
                pmb = BK_PM[pm]
                bk.write("tensor", pmb)
                em.wait("tensor", ev_w)
                em.waits("tensor", state["ln1_chunk_evs"][j])
                ev_mm = None
                for c in range(DC):
                    ev_mm = em.op("tensor", lambda e, c=c, j=j, sl=sl, pm=pm, nr=nrow: e.matmul(
                        PM[pm][0:nr, 0:512],
                        lhsT=XNT[:, c * CW + j * 128: c * CW + j * 128 + nr],
                        rhs=WB[:, sl * DC * 512 + c * 512: sl * DC * 512 + (c + 1) * 512],
                        start=(c == 0), stop=(c == DC - 1)),
                        inc=("t" if c == DC - 1 else None))
                bk.wrote(pmb, ev_mm)
                state["wb_cons"][sl] = ev_mm
                eng = "vector" if j % 2 == 0 else "scalar"
                em.wait(eng, ev_mm)
                if eng == "vector":
                    ev_e = em.op("vector", lambda e, j=j, n=n, pm=pm, nr=nrow: e.tensor_copy(
                        VB[0:nr, j * 2048 + n * 512: j * 2048 + (n + 1) * 512],
                        PM[pm][0:nr, 0:512]), inc="v")
                else:
                    ev_e = em.op("scalar", lambda e, j=j, n=n, pm=pm, nr=nrow: e.activation(
                        out=VB[0:nr, j * 2048 + n * 512: j * 2048 + (n + 1) * 512],
                        in_=PM[pm][0:nr, 0:512], func=ACTF.Copy, bias=0.0,
                        scale=1.0), inc="a")
                bk.did_read(pmb, eng, ev_e)
                state["v_done"][eng] = ev_e

    def units_for_tile(tile):
        if tile == 0:
            us = [("s0", 0, 8, 0, 16, 0)]
            us += [("main", 8 + 128 * j, 128, 128 * j, 144, j) for j in range(8)]
        else:
            us = [("main", 8 + 128 * j, 128, 128 * j, 144, j) for j in range(7)]
            us.append(("last", 904, 120, 896, 128, 7))
        return us

    def qk_group(tile, pair, kind, off, n, last):
        sl = pair % 2
        ring = kind * 2 + sl
        blk = 2 * pair + kind
        wsl = blk % 2
        if off == 0:
            load_wa(wsl, wqk_d[blk])
        ev_w = state["wa"][wsl]
        state["qk_pm"] = (state.get("qk_pm", 0) + 1) % 2
        pm = state["qk_pm"]
        pmb = BK_PM[pm]
        bk.write("tensor", pmb)
        em.wait("tensor", ev_w)
        ev_mm = None
        for c in range(DC):
            ev_mm = em.op("tensor", lambda e, c=c, off=off, n=n, wsl=wsl, pm=pm: e.matmul(
                PM[pm][0:128, 0:n],
                lhsT=WA[:, wsl * 2048 + c * 128: wsl * 2048 + (c + 1) * 128],
                rhs=XNT[:, c * CW + off: c * CW + off + n],
                start=(c == 0), stop=(c == DC - 1)),
                inc=("t" if c == DC - 1 else None))
        bk.wrote(pmb, ev_mm)
        state["wa_cons"][wsl] = ev_mm
        eng = "vector" if kind == 0 else "scalar"
        em.wait(eng, ev_mm)
        em.wait(eng, state["qk_cons"][ring])
        if eng == "vector":
            ev_e = em.op("vector", lambda e, ring=ring, off=off, n=n, pm=pm: e.tensor_copy(
                QKB[:, ring * CW + off: ring * CW + off + n], PM[pm][0:128, 0:n]),
                inc="v")
        else:
            ev_e = em.op("scalar", lambda e, ring=ring, off=off, n=n, pm=pm: e.activation(
                out=QKB[:, ring * CW + off: ring * CW + off + n],
                in_=PM[pm][0:128, 0:n], func=ACTF.Copy, bias=0.0, scale=1.0),
                inc="a")
        bk.did_read(pmb, eng, ev_e)
        if last:
            state.setdefault("qk_ready", {})[(pair, kind)] = ev_e

    def head_bundles(tile, head):
        import os as _os3
        solo_last = int(_os3.environ.get("K_SOLO_LAST", "0"))
        us = units_for_tile(tile)
        bs, i = [], 0
        while i < len(us):
            can_pair = (us[i][0] == "main" and i + 1 < len(us)
                        and us[i + 1][0] == "main")
            if solo_last and tile == 0 and us[i][5] == 6 and us[i][0] == "main":
                can_pair = False
            if can_pair:
                bs.append((head, [us[i], us[i + 1]]))
                i += 2
            else:
                bs.append((head, [us[i]]))
                i += 1
        return bs

    def qk_attn_phase(tile):
        ncols = 1040 if tile == 0 else 1024
        nsl = [(0, 512), (512, 512), (1024, 16)] if ncols == 1040 else [(0, 512), (512, 512)]
        if tile == 1:
            for c in range(DC):
                ev_sv = em.op("vector", lambda e, c=c: e.tensor_copy(
                    ATT[:, c * 1024: c * 1024 + 8], SLV[:, c * 8: c * 8 + 8]),
                    inc="v", dep=(c == 0))
            state["sliver_ev"] = ev_sv
        for kind in (0, 1):
            for gi, (off, n) in enumerate(nsl):
                qk_group(tile, 0, kind, off, n, gi == len(nsl) - 1)

        bundles = []
        for pair in range(16):
            for hh in range(2):
                bundles += head_bundles(tile, 2 * pair + hh)
        import os as _os
        per_pair = len(bundles) // 16
        _lim = int(_os.environ.get("K_DBG_NBUND", "0"))
        if _lim:
            bundles = bundles[:_lim]
        NB = len(bundles)
        qkq = []
        maxpair = (NB - 1) // per_pair if _lim else 15
        import os as _os2
        _noil = int(_os2.environ.get("K_NO_QK_INTERLEAVE", "0"))
        for pair in range(min(15, maxpair)):
            ngr = 2 * len(nsl)
            for g in range(ngr):
                pos = pair * per_pair + (0 if _noil else (g * per_pair) // ngr)
                qkq.append((pos, pair + 1, g))
        ctx = [None] * NB
        incA = {}
        incB = {}
        incC = {}

        def mkctx(bi):
            head, us = bundles[bi]
            c = dict(head=head, units=us, s=bi % 2, s4=bi % 4, par=head % 2,
                     pair=head // 2, dvc=head * 64, nu=len(us))
            c["p0"] = 64 * c["par"]
            return c

        def S0(bi, evA):
            c = ctx[bi]
            em.wait("tensor", evA)
            em.waits("tensor", [state["qk_ready"][(c["pair"], 0)],
                                state["qk_ready"][(c["pair"], 1)]])
            bk.write("tensor", "PSW")
            rsl = c["pair"] % 2
            p0 = c["p0"]
            ev = None
            for ui, (kind, q0, nq, k0, nk, vj) in enumerate(c["units"]):
                qr = QKB[p0:p0 + 64, rsl * CW + q0: rsl * CW + q0 + nq]
                kr = QKB[p0:p0 + 64, (2 + rsl) * CW + k0: (2 + rsl) * CW + k0 + nk]
                ev = em.op("tensor", lambda e, qr=qr, kr=kr, ui=ui, nq=nq, nk=nk: e.matmul(
                    PSW[0:nq, ui * 512: ui * 512 + nk], lhsT=qr, rhs=kr,
                    start=True, stop=True),
                    inc=("t" if ui == c["nu"] - 1 else None))
            c["ev_sc"] = ev
            bk.wrote("PSW", ev)
            state["qk_cons"][rsl] = ev
            state["qk_cons"][2 + rsl] = ev

        def S1(bi):
            c = ctx[bi]
            s = c["s"]
            s4 = c["s4"]
            em.wait("vector", c["ev_sc"])
            em.wait("vector", incC.get(bi - 1))
            if c["nu"] == 2:
                pv = PSW[0:128, 0:1024].rearrange("p (u x) -> p u x", u=2)[:, :, 0:144]
                em.op("vector", lambda e, pv=pv, s=s: e.scalar_tensor_tensor(
                    out=T1R[:, s4 * 288: s4 * 288 + 288], in0=pv, scalar=1.0,
                    in1=MM2[:, 0:288], op0=ALU.mult, op1=ALU.mult), dep=False)
            else:
                kind, q0, nq, k0, nk, vj = c["units"][0]
                mm = {"main": MM2, "s0": MMS, "last": MML}[kind]
                em.op("vector", lambda e, s=s, s4=s4, nq=nq, nk=nk, mm=mm: e.scalar_tensor_tensor(
                    out=T1R[0:nq, s4 * 288: s4 * 288 + nk], in0=PSW[0:nq, 0:nk],
                    scalar=1.0, in1=mm[0:nq, 0:nk], op0=ALU.mult, op1=ALU.mult),
                    dep=False)
            bk.did_read("PSW", "vector", None)

        def S1tt(bi):
            c = ctx[bi]
            s = c["s"]
            s4 = c["s4"]
            kind, q0, nq, k0, nk, vj = c["units"][0]
            am = {"s0": AMS, "last": AML}[kind]
            em.op("vector", lambda e, s=s, s4=s4, nq=nq, nk=nk, am=am: e.tensor_tensor(
                out=T1R[0:nq, s4 * 288: s4 * 288 + nk],
                in0=T1R[0:nq, s4 * 288: s4 * 288 + nk],
                in1=am[0:nq, 0:nk], op=ALU.add), dep=False)

        def S1rmax(bi):
            c = ctx[bi]
            s = c["s"]
            s4 = c["s4"]
            if c["nu"] == 2:
                tv = T1R[:, s4 * 288:(s4 + 1) * 288].rearrange("p (u x) -> p u x", u=2)
                em.op("vector", lambda e, tv=tv, s=s: e.reduce_max(
                    SST[:, s4 * 8: s4 * 8 + 2], tv, axis=AX.X, negate=True), dep=False)
            else:
                kind, q0, nq, k0, nk, vj = c["units"][0]
                em.op("vector", lambda e, s=s, s4=s4, nq=nq, nk=nk: e.reduce_max(
                    SST[0:nq, s4 * 8: s4 * 8 + 1], T1R[0:nq, s4 * 288: s4 * 288 + nk],
                    axis=AX.X, negate=True), dep=False)

        def S2sum(bi):
            c = ctx[bi]
            s = c["s"]
            s4 = c["s4"]
            em.wait("vector", incC.get(bi + 1))
            if c["nu"] == 2:
                tv = T1R[:, s4 * 288:(s4 + 1) * 288].rearrange("p (u x) -> p u x", u=2)
                em.op("vector", lambda e, tv=tv, s=s: e.reduce_sum(
                    SST[:, s4 * 8 + 2: s4 * 8 + 4], tv, axis=AX.X), dep=False)
            else:
                kind, q0, nq, k0, nk, vj = c["units"][0]
                em.op("vector", lambda e, s=s, s4=s4, nq=nq, nk=nk: e.reduce_sum(
                    SST[0:nq, s4 * 8 + 2: s4 * 8 + 3], T1R[0:nq, s4 * 288: s4 * 288 + nk],
                    axis=AX.X), dep=False)

        def S2corr(bi):
            c = ctx[bi]
            s = c["s"]
            s4 = c["s4"]
            for ui, (kind, q0, nq, k0, nk, vj) in enumerate(c["units"]):
                em.op("vector", lambda e, s=s, s4=s4, ui=ui: e.scalar_tensor_tensor(
                    out=SST[:, s4 * 8 + 4 + ui: s4 * 8 + 5 + ui],
                    in0=T1R[:, s4 * 288 + ui * 144 + 143: s4 * 288 + ui * 144 + 144],
                    scalar=-128.0, in1=SST[:, s4 * 8 + 2 + ui: s4 * 8 + 3 + ui],
                    op0=ALU.mult, op1=ALU.add), dep=False)

        def S2recip(bi):
            c = ctx[bi]
            s = c["s"]
            s4 = c["s4"]
            main = c["units"][0][0] == "main"
            base = s4 * 8 + (4 if main else 2)
            nq = 128 if c["nu"] == 2 else c["units"][0][2]
            em.op("vector", lambda e, s=s, s4=s4, base=base, nu=c["nu"], nq=nq: e.reciprocal(
                SST[0:nq, s4 * 8 + 6: s4 * 8 + 6 + nu], SST[0:nq, base: base + nu]),
                dep=False)

        def S2pnorm(bi, need_drain):
            c = ctx[bi]
            s = c["s"]
            s4 = c["s4"]
            em.wait("vector", (ctx[bi - 2].get("ev_tr") if bi >= 2 else None))
            if c["units"][0][0] == "s0":
                em.wait("vector", state.get("s0_tr_ev"))
            for ui, (kind, q0, nq, k0, nk, vj) in enumerate(c["units"]):
                if kind == "main":
                    em.op("vector", lambda e, s4=s4, ui=ui, nq=nq: e.scalar_tensor_tensor(
                        out=PR[0:nq, s4 * 320 + ui * 160: s4 * 320 + ui * 160 + 144],
                        in0=T1R[0:nq, s4 * 288 + ui * 144: s4 * 288 + ui * 144 + 144],
                        scalar=SST[0:nq, s4 * 8 + 6 + ui: s4 * 8 + 7 + ui],
                        in1=B01[0:nq, 0:144], op0=ALU.mult, op1=ALU.mult),
                        dep=(need_drain and ui == 0))
                elif kind == "s0":
                    em.op("vector", lambda e, s=s, s4=s4: e.tensor_scalar(
                        out=PS0R[0:8, 0:16], in0=T1R[0:8, s4 * 288: s4 * 288 + 16],
                        scalar1=SST[0:8, s4 * 8 + 6: s4 * 8 + 7], scalar2=None,
                        op0=ALU.mult), dep=need_drain)
                else:
                    em.op("vector", lambda e, s=s, s4=s4, nq=nq, nk=nk: e.tensor_scalar(
                        out=PR[0:nq, s4 * 320: s4 * 320 + nk],
                        in0=T1R[0:nq, s4 * 288: s4 * 288 + nk],
                        scalar1=SST[0:nq, s4 * 8 + 6: s4 * 8 + 7], scalar2=None,
                        op0=ALU.mult), dep=need_drain)

        def Sexp(bi, ev):
            c = ctx[bi]
            s = c["s"]
            s4 = c["s4"]
            em.wait("scalar", ev)
            for ui, (kind, q0, nq, k0, nk, vj) in enumerate(c["units"]):
                em.op("scalar", lambda e, s=s, s4=s4, ui=ui, nq=nq, nk=nk: e.activation(
                    out=T1R[0:nq, s4 * 288 + ui * 144: s4 * 288 + ui * 144 + nk],
                    in_=T1R[0:nq, s4 * 288 + ui * 144: s4 * 288 + ui * 144 + nk],
                    func=ACTF.Exp, bias=SST[0:nq, s4 * 8 + ui: s4 * 8 + ui + 1],
                    scale=1.0))

        def S3tr(bi, ev):
            c = ctx[bi]
            s = c["s"]
            s4 = c["s4"]
            ptb = BK_PT[s]
            em.wait("tensor", ev)
            bk.write("tensor", ptb)
            evt = None
            for ui, (kind, q0, nq, k0, nk, vj) in enumerate(c["units"]):
                if kind == "s0":
                    evt = em.op("tensor", lambda e, s=s, s4=s4: e.transpose(
                        PT[s][0:32, 0:32], PS0R[0:32, 0:32], IDB[0:32, 0:32]),
                        inc="t")
                    state["s0_tr_ev"] = evt
                elif kind == "last":
                    evt = em.op("tensor", lambda e, s=s, s4=s4: e.transpose(
                        PT[s][0:128, 0:128], PR[:, s4 * 320: s4 * 320 + 128],
                        IDB[:, :]), inc="t")
                else:
                    em.op("tensor", lambda e, s=s, s4=s4, ui=ui: e.transpose(
                        PT[s][0:128, ui * 256: ui * 256 + 128],
                        PR[:, s4 * 320 + ui * 160: s4 * 320 + ui * 160 + 128],
                        IDB[:, :]))
                    evt = em.op("tensor", lambda e, s=s, s4=s4, ui=ui: e.transpose(
                        PT[s][0:32, ui * 256 + 128: ui * 256 + 256],
                        PR[:, s4 * 320 + ui * 160 + 128: s4 * 320 + ui * 160 + 160],
                        IDB[:, :]), inc="t")
            c["ev_tr"] = evt
            bk.wrote(ptb, evt)

        def S3ev(bi):
            c = ctx[bi]
            s = c["s"]
            s4 = c["s4"]
            em.wait("vector", c["ev_tr"])
            em.wait("vector", (ctx[bi - 2]["ev_pv"] if bi >= 2 else None))
            kind = c["units"][0][0]
            if kind == "s0":
                em.op("vector", lambda e, s=s, s4=s4: e.tensor_copy(
                    PTSB[0:32, s * 512: s * 512 + 32], PT[s][0:32, 0:32]), dep=False)
            elif kind == "last":
                em.op("vector", lambda e, s=s, s4=s4: e.tensor_copy(
                    PTSB[0:128, s * 512: s * 512 + 128], PT[s][0:128, 0:128]),
                    dep=False)
            else:
                w = 256 * c["nu"]
                em.op("vector", lambda e, s=s, s4=s4, w=w: e.tensor_copy(
                    PTSB[0:128, s * 512: s * 512 + w], PT[s][0:128, 0:w]), dep=False)
            bk.did_read(BK_PT[s], "vector", None)

        def S3pv(bi, evB, evC):
            c = ctx[bi]
            s = c["s"]
            s4 = c["s4"]
            pvb = BK_PV[s]
            em.waits("tensor", [evB, evC,
                                state["v_done"]["vector"], state["v_done"]["scalar"]])
            bk.write("tensor", pvb)
            p0 = c["p0"]
            dvc = c["dvc"]
            tp = (0, p0)
            ev = None
            for ui, (kind, q0, nq, k0, nk, vj) in enumerate(c["units"]):
                if kind == "s0":
                    ev = em.op("tensor", lambda e, s=s, s4=s4, p0=p0, dvc=dvc, tp=tp: e.matmul(
                        PV[s][p0:p0 + 64, 0:8],
                        lhsT=VB[0:32, dvc: dvc + 64],
                        rhs=PTSB[0:32, s * 512: s * 512 + 8],
                        start=True, stop=True, tile_position=tp), inc="t")
                elif kind == "last":
                    ev = em.op("tensor", lambda e, s=s, s4=s4, p0=p0, dvc=dvc, vj=vj, nq=nq, tp=tp: e.matmul(
                        PV[s][p0:p0 + 64, 0:nq],
                        lhsT=VB[:, vj * 2048 + dvc: vj * 2048 + dvc + 64],
                        rhs=PTSB[0:128, s * 512: s * 512 + nq],
                        start=True, stop=True, tile_position=tp), inc="t")
                else:
                    em.op("tensor", lambda e, s=s, s4=s4, ui=ui, p0=p0, dvc=dvc, vj=vj, tp=tp: e.matmul(
                        PV[s][p0:p0 + 64, ui * 128: ui * 128 + 128],
                        lhsT=VB[:, vj * 2048 + dvc: vj * 2048 + dvc + 64],
                        rhs=PTSB[0:128, s * 512 + ui * 256: s * 512 + ui * 256 + 128],
                        start=True, stop=False, tile_position=tp))
                    ev = em.op("tensor", lambda e, s=s, s4=s4, ui=ui, p0=p0, dvc=dvc, vj=vj, tp=tp: e.matmul(
                        PV[s][p0:p0 + 64, ui * 128: ui * 128 + 128],
                        lhsT=VB[0:32, (vj + 1) * 2048 + dvc: (vj + 1) * 2048 + dvc + 64],
                        rhs=PTSB[0:32, s * 512 + ui * 256 + 128: s * 512 + ui * 256 + 256],
                        start=False, stop=True, tile_position=tp),
                        inc=("t" if ui == c["nu"] - 1 else None))
            c["ev_pv"] = ev
            bk.wrote(pvb, ev)

        def S4(bi):
            c = ctx[bi]
            s = c["s"]
            s4 = c["s4"]
            em.wait("scalar", c["ev_pv"])
            p0 = c["p0"]
            dv0 = p0
            ch = c["head"] // 2
            us = c["units"]
            if c["nu"] == 2:
                q0 = us[0][1]
                if tile == 0 and us[1][5] == 7:
                    em.op("scalar", lambda e, s=s, s4=s4, p0=p0, dv0=dv0, ch=ch, q0=q0: e.activation(
                        out=ATT[dv0:dv0 + 64, ch * 1024 + q0: ch * 1024 + 1024],
                        in_=PV[s][p0:p0 + 64, 0:248], func=ACTF.Copy, bias=0.0,
                        scale=1.0))
                    em.op("scalar", lambda e, s=s, s4=s4, p0=p0, dv0=dv0, ch=ch: e.activation(
                        out=SLV[dv0:dv0 + 64, ch * 8: ch * 8 + 8],
                        in_=PV[s][p0:p0 + 64, 248:256], func=ACTF.Copy, bias=0.0,
                        scale=1.0))
                else:
                    em.op("scalar", lambda e, s=s, s4=s4, p0=p0, dv0=dv0, ch=ch, q0=q0: e.activation(
                        out=ATT[dv0:dv0 + 64, ch * 1024 + q0: ch * 1024 + q0 + 256],
                        in_=PV[s][p0:p0 + 64, 0:256], func=ACTF.Copy, bias=0.0,
                        scale=1.0))
            else:
                kind, q0, nq, k0, nk, vj = us[0]
                if tile == 0 and kind == "main" and vj == 7:
                    em.op("scalar", lambda e, s=s, s4=s4, p0=p0, dv0=dv0, ch=ch, q0=q0: e.activation(
                        out=ATT[dv0:dv0 + 64, ch * 1024 + q0: ch * 1024 + 1024],
                        in_=PV[s][p0:p0 + 64, 0:120], func=ACTF.Copy, bias=0.0,
                        scale=1.0))
                    em.op("scalar", lambda e, s=s, s4=s4, p0=p0, dv0=dv0, ch=ch: e.activation(
                        out=SLV[dv0:dv0 + 64, ch * 8: ch * 8 + 8],
                        in_=PV[s][p0:p0 + 64, 120:128], func=ACTF.Copy, bias=0.0,
                        scale=1.0))
                else:
                    em.op("scalar", lambda e, s=s, s4=s4, p0=p0, dv0=dv0, ch=ch, q0=q0, nq=nq: e.activation(
                        out=ATT[dv0:dv0 + 64, ch * 1024 + q0: ch * 1024 + q0 + nq],
                        in_=PV[s][p0:p0 + 64, 0:nq], func=ACTF.Copy, bias=0.0,
                        scale=1.0))
            bk.did_read(BK_PV[s], "scalar", None)

        qk_i = 0
        for i in range(NB + 6):
            if 1 <= i <= NB:
                ctx_prev = ctx[i - 1]
            if 1 <= i <= NB and i < 2:
                pass
            if 1 <= i <= NB:
                S1(i - 1)
            if 2 <= i <= NB + 1:
                S2sum(i - 2)
            if 1 <= i <= NB and ctx[i - 1]["units"][0][0] in ("s0", "last"):
                if i < 2:
                    em.op("vector", lambda e: e.drain())
                S1tt(i - 1)
                if i < 3:
                    em.op("vector", lambda e: e.drain())
                S1rmax(i - 1)
            elif 1 <= i <= NB:
                if i < 2:
                    em.op("vector", lambda e: e.drain())
                S1rmax(i - 1)
            if 3 <= i <= NB + 2 and ctx[i - 3]["units"][0][0] == "main":
                S2corr(i - 3)
            evA = em.fence("vector", "v") if i <= NB + 2 else None
            incA[i] = evA
            ops_after_recip = 0
            if 3 <= i <= NB + 2:
                if evA is None:
                    em.op("vector", lambda e: e.drain())
                S2recip(i - 3)
            if 4 <= i <= NB + 3:
                S3ev(i - 4)
                ops_after_recip = 1
            if 3 <= i <= NB + 2:
                S2pnorm(i - 3, need_drain=(ops_after_recip == 0))
            evB = em.fence("vector", "v") if i >= 3 else None
            incB[i] = evB
            if 1 <= i <= NB:
                Sexp(i - 1, evA)
            if 5 <= i <= NB + 4:
                S4(i - 5)
            evC = em.fence("scalar", "a") if i >= 1 else None
            incC[i] = evC
            if i < NB:
                ctx[i] = mkctx(i)
                S0(i, evA)
            if 3 <= i <= NB + 2:
                S3tr(i - 3, evB)
            if 4 <= i <= NB + 3:
                S3pv(i - 4, evB, evC)
            while qk_i < len(qkq) and qkq[qk_i][0] <= i:
                _, pnext, gidx = qkq[qk_i]
                kind = gidx // len(nsl)
                off, n = nsl[gidx % len(nsl)]
                qk_group(tile, pnext, kind, off, n, gidx % len(nsl) == len(nsl) - 1)
                qk_i += 1
        state["attn_last_ev"] = incC[NB + 5]

    def proj_phase(tile):
        base = tile * TILE
        em.wait("tensor", state.get("attn_last_ev"))
        if tile == 1:
            em.wait("tensor", state.get("sliver_ev"))
        for n in range(4):
            sl = n % 2
            load_wb(sl, wp_d[n])
            ev_w = state["wb"][sl]
            for t in range(8):
                pm = (n * 8 + t) % 2
                pmb = BK_PM[pm]
                bk.write("tensor", pmb)
                em.wait("tensor", ev_w)
                for c in range(DC):
                    em.op("tensor", lambda e, c=c, t=t, sl=sl, pm=pm: e.matmul(
                        PM[pm][0:128, 0:512],
                        lhsT=ATT[:, c * 1024 + t * 128: c * 1024 + (t + 1) * 128],
                        rhs=WB[:, sl * DC * 512 + c * 512: sl * DC * 512 + (c + 1) * 512],
                        start=(c == 0), stop=False))
                ev_mm = em.op("tensor", lambda e, n=n, pm=pm: e.matmul(
                    PM[pm][0:128, 0:512], lhsT=ONE[0:1, 0:128],
                    rhs=PBB[0:1, n * 512:(n + 1) * 512],
                    start=False, stop=True), inc="t")
                bk.wrote(pmb, ev_mm)
                state["wb_cons"][sl] = ev_mm
                xs = t % 2
                em.wait("sync", state["xsl_cons"][xs])
                ev_x = em.op("sync", lambda e, t=t, n=n, xs=xs, base=base: e.dma_start(
                    out=XSL[:, xs * 512:(xs + 1) * 512],
                    in_=x_d[base + t * 128: base + (t + 1) * 128, n * 512:(n + 1) * 512]),
                    inc="xd", dma=True)
                em.waits("vector", [ev_mm, ev_x])
                em.wait("vector", state["xps_cons"][xs])
                ev_xp = em.op("vector", lambda e, xs=xs, pm=pm: e.tensor_tensor(
                    out=XPS[:, xs * 512:(xs + 1) * 512],
                    in0=PM[pm][0:128, 0:512], in1=XSL[:, xs * 512:(xs + 1) * 512],
                    op=ALU.add), inc="v")
                bk.did_read(pmb, "vector", ev_xp)
                state["xsl_cons"][xs] = ev_xp
                em.wait("scalar", ev_xp)
                ev_o = em.op("scalar", lambda e, t=t, n=n, xs=xs, base=base: e.dma_start(
                    out=y_d[base + t * 128: base + (t + 1) * 128, n * 512:(n + 1) * 512],
                    in_=XPS[:, xs * 512:(xs + 1) * 512]), inc="od", dma=True)
                state["xps_cons"][xs] = ev_o
                if n == 3:
                    state["y_written"][tile * 8 + t] = ev_o

    def ffn_mlp(tile, which):
        src, scw = (XNT, CW) if which == 0 else (ATT, 1024)
        dst = ATT if which == 0 else VB
        w_dd = w1_d if which == 0 else w3_d

        ev_e = None
        for m in range(16):
            wsl = m % 2
            load_wa(wsl, w_dd[m])
            ev_w = state["wa"][wsl]
            for n in range(2):
                pm = (m * 2 + n) % 2
                pmb = BK_PM[pm]
                bk.write("tensor", pmb)
                em.wait("tensor", ev_w)
                if which == 0:
                    em.wait("tensor", state["ln2_chunk_evs"][4 * n + 3])
                ev_mm = None
                for c in range(DC):
                    ev_mm = em.op("tensor", lambda e, c=c, n=n, wsl=wsl, pm=pm, src=src, scw=scw: e.matmul(
                        PM[pm][0:128, 0:512],
                        lhsT=WA[:, wsl * 2048 + c * 128: wsl * 2048 + (c + 1) * 128],
                        rhs=src[:, c * scw + n * 512: c * scw + n * 512 + 512],
                        start=(c == 0), stop=(c == DC - 1)),
                        inc=("t" if c == DC - 1 else None))
                bk.wrote(pmb, ev_mm)
                state["wa_cons"][wsl] = ev_mm
                em.wait("scalar", ev_mm)
                ev_e = em.op("scalar", lambda e, m=m, n=n, pm=pm, dst=dst: e.activation(
                    out=dst[:, m * 1024 + n * 512: m * 1024 + (n + 1) * 512],
                    in_=PM[pm][0:128, 0:512], func=ACTF.Silu, bias=ZAP[0:128, 0:1], scale=1.0),
                    inc="a")
                bk.did_read(pmb, "scalar", ev_e)
        state["mlp_ev"] = ev_e

    def ffn_out(tile):
        sc_ps = [None, None, PV[0], PV[1]]
        sc_bk = [BK_PS[0], BK_PS[1], BK_PV[0], BK_PV[1]]
        ev_h3_gate = state.get("h1_mm_last")
        ev_acc = None
        for n in range(4):
            sl = n % 2
            load_wb(sl, w2_d[n])
            ev_w = state["wb"][sl]
            bk.write("tensor", sc_bk[n])
            ev_s = None
            for t in range(8):
                pm = (n * 8 + t) % 2
                pmb = BK_PM[pm]
                bk.write("tensor", pmb)
                em.wait("tensor", ev_w)
                ev_mm = None
                for c in range(DC):
                    ev_mm = em.op("tensor", lambda e, c=c, t=t, sl=sl, pm=pm: e.matmul(
                        PM[pm][0:128, 0:512],
                        lhsT=VB[:, c * 1024 + t * 128: c * 1024 + (t + 1) * 128],
                        rhs=WB[:, sl * DC * 512 + c * 512: sl * DC * 512 + (c + 1) * 512],
                        start=(c == 0), stop=(c == DC - 1)),
                        inc=("t" if c == DC - 1 else None))
                bk.wrote(pmb, ev_mm)
                state["wb_cons"][sl] = ev_mm
                em.wait("scalar", ev_mm)
                em.wait("scalar", ev_h3_gate)
                ev_h3 = em.op("scalar", lambda e, t=t, n=n, pm=pm: e.activation(
                    out=XNT[:, t * 2048 + n * 512: t * 2048 + (n + 1) * 512],
                    in_=PM[pm][0:128, 0:512], func=ACTF.Copy, bias=0.0, scale=1.0),
                    inc="a")
                bk.did_read(pmb, "scalar", ev_h3)
                em.waits("tensor", [ev_h3, state.get("gates_ev")])
                ev_s = em.op("tensor", lambda e, t=t, n=n: e.matmul(
                    (PSW[0:8, 512 * n:512 * (n + 1)] if n < 2 else sc_ps[n][0:8, 0:512]),
                    lhsT=GT[:, t * 8:(t + 1) * 8],
                    rhs=XNT[:, t * 2048 + n * 512: t * 2048 + (n + 1) * 512],
                    start=(t == 0), stop=(t == 7)), inc="t")
            bk.wrote(sc_bk[n], ev_s)
            em.wait("vector", ev_s)
            src_ps = (lambda n=n: PSW[0:8, 512 * n:512 * (n + 1)] if n < 2 else sc_ps[n][0:8, 0:512])()
            if tile == 0:
                ev_acc = em.op("vector", lambda e, n=n, sp=src_ps: e.tensor_copy(
                    FIN[0:8, n * 512:(n + 1) * 512], sp), inc="v")
            else:
                ev_acc = em.op("vector", lambda e, n=n, sp=src_ps: e.tensor_tensor(
                    out=FIN[0:8, n * 512:(n + 1) * 512],
                    in0=FIN[0:8, n * 512:(n + 1) * 512],
                    in1=sp, op=ALU.add), inc="v")
            bk.did_read(sc_bk[n], "vector", ev_acc)
        state["acc_ev"] = ev_acc

    def finale():
        em.wait("gpsimd", state["acc_ev"])
        ev1 = em.op("gpsimd", lambda e: e.dma_start(
            out=cc_i[:, :], in_=FIN[0:8, 0:D]), inc="gd", dma=True)
        em.wait("gpsimd", ev1)
        ev2 = em.op("gpsimd", lambda e: e.collective_compute(
            "AllReduce", ALU.add, replica_groups=[list(range(8))],
            ins=[cc_i[:, :]], outs=[cc_o[:, :]]), inc="g")
        em.wait("gpsimd", ev2)
        em.waits("gpsimd", [state["xsl_cons"][0], state["xsl_cons"][1],
                            state["xps_cons"][0], state["xps_cons"][1]])
        ev3a = em.op("gpsimd", lambda e: e.dma_start(
            out=XSL[0:8, 0:1024], in_=cc_o[:, 0:1024]), inc="gd", dma=True)
        ev3b = em.op("gpsimd", lambda e: e.dma_start(
            out=XPS[0:8, 0:1024], in_=cc_o[:, 1024:2048]), inc="gd", dma=True)
        em.wait("sync", ev1)
        em.wait("sync", state["y_written"][0])
        ev4 = em.op("sync", lambda e: e.dma_start(
            out=FIN[0:8, 0:D], in_=y_d[0:8, :]), inc="xd", dma=True)
        em.waits("vector", [ev3a, ev3b, ev4])
        em.op("vector", lambda e: e.tensor_scalar(
            out=XSL[0:8, 0:1024], in0=XSL[0:8, 0:1024], scalar1=CM[0:8, 0:1],
            scalar2=None, op0=ALU.mult))
        em.op("vector", lambda e: e.tensor_scalar(
            out=XPS[0:8, 0:1024], in0=XPS[0:8, 0:1024], scalar1=CM[0:8, 0:1],
            scalar2=None, op0=ALU.mult))
        em.op("vector", lambda e: e.tensor_tensor(
            out=XSL[0:8, 0:1024], in0=XSL[0:8, 0:1024], in1=FIN[0:8, 0:1024],
            op=ALU.add))
        ev5 = em.op("vector", lambda e: e.tensor_tensor(
            out=XPS[0:8, 0:1024], in0=XPS[0:8, 0:1024], in1=FIN[0:8, 1024:2048],
            op=ALU.add), inc="v")
        em.wait("scalar", ev5)
        em.op("scalar", lambda e: e.dma_start(
            out=y_d[0:8, 0:1024], in_=XSL[0:8, 0:1024]), inc="od", dma=True)
        em.op("scalar", lambda e: e.dma_start(
            out=y_d[0:8, 1024:2048], in_=XPS[0:8, 0:1024]), inc="od", dma=True)
        em.op("scalar", lambda e, v=em.counts["od"]: e.wait_ge(em.sems["od"], v))

    def emit_all(nsteps):
        cnt = [0]
        def step(f):
            if nsteps is None or cnt[0] < nsteps:
                f()
            cnt[0] += 1
        for tile in (0, 1):
            step(lambda tile=tile: ln_phase(tile, is_ln2=False))
            step(lambda tile=tile: v_phase(tile))
            step(lambda tile=tile: qk_attn_phase(tile))
            step(lambda tile=tile: proj_phase(tile))
            step(lambda tile=tile: ln_phase(tile, is_ln2=True))
            def _mlp0(tile=tile):
                ffn_mlp(tile, 0)
                state["h1_mm_last"] = state["wa_cons"][1]
            step(_mlp0)
            def _mlp1(tile=tile):
                ffn_mlp(tile, 1)
                state["ffn_h2_done"] = state["wa_cons"][1]
            step(_mlp1)
            step(lambda tile=tile: ffn_out(tile))
        step(finale)
    emit_all(steps)

    TENS = {"XNT": XNT, "VB": VB, "ATT": ATT, "QKB": QKB, "GT": GT, "T1R": T1R,
            "SLV": SLV, "FIN": FIN, "STT": STT, "RPS": RPS, "XCH": XCH,
            "SST": SST, "PR": PR, "PTSB": PTSB}
    if dbg:
        for s in ("t", "v", "a", "wd", "xd", "od", "gd", "g"):
            em.wait("scalar", (s, em.counts[s]))
        for name in dbg:
            th = TENS[name]
            shp = [th.shape[0], th.shape[1]]
            dt = th.ap().dtype
            dd = nc.dram_tensor("dbg_" + name, shp, dt, kind="ExternalOutput").ap()
            em.op("scalar", lambda e, dd=dd, th=th: e.dma_start(
                out=dd[:, :], in_=th[:, :]), inc="od", dma=True)
        em.op("scalar", lambda e, v=em.counts["od"]: e.wait_ge(em.sems["od"], v))

    with nc.Block() as block:
        @block.sync
        def _(e):
            for f in em.ops["sync"]:
                f(e)

        @block.tensor
        def _(e):
            for f in em.ops["tensor"]:
                f(e)

        @block.vector
        def _(e):
            for f in em.ops["vector"]:
                f(e)

        @block.scalar
        def _(e):
            for f in em.ops["scalar"]:
                f(e)

        @block.gpsimd
        def _(e):
            for f in em.ops["gpsimd"]:
                f(e)

    es.close()
    return nc



def _host_consts():
    mm_main = np.zeros((128, 144), np.float32)
    b01 = np.zeros((128, 144), np.float32)
    for r in range(128):
        for c in range(r, r + 16):
            d = abs(c - r - 8)
            mm_main[r, c] = SCALE * (WIN - d / 2.0) / WIN
            b01[r, c] = 1.0
    mm2 = np.concatenate([mm_main, mm_main], axis=1)
    mm_s0 = np.zeros((8, 16), np.float32)
    am_s0 = np.full((8, 16), NEG, np.float32)
    for r in range(8):
        for c in range(0, r + 8):
            d = abs(c - r)
            mm_s0[r, c] = SCALE * (WIN - d / 2.0) / WIN
            am_s0[r, c] = 0.0
    mm_last = np.zeros((120, 128), np.float32)
    am_last = np.full((120, 128), NEG, np.float32)
    for r in range(120):
        for c in range(r, min(r + 16, 128)):
            d = abs(c - r - 8)
            mm_last[r, c] = SCALE * (WIN - d / 2.0) / WIN
            am_last[r, c] = 0.0
    return mm2, b01, mm_s0, am_s0, mm_last, am_last


def _prep_weights(qkv_w, proj_w, proj_b, w1, w3, w2, wr):
    f = np.float32
    qkv_w = np.asarray(qkv_w, f)
    qk = np.ascontiguousarray(qkv_w[0:4096])
    T = qk.T
    Bq = T.reshape(16, 128, 32, 128).transpose(2, 1, 0, 3)
    order = []
    for p in range(16):
        order += [p, 16 + p]
    wqk = np.ascontiguousarray(Bq[order]).reshape(32, 128, 16 * 128)

    def nslice(w):
        T = np.asarray(w, f).T
        return np.ascontiguousarray(
            T.reshape(16, 128, 4, 512).transpose(2, 1, 0, 3)).reshape(4, 128, 16 * 512)

    wv = nslice(qkv_w[4096:6144])
    wp = nslice(proj_w)
    w2n = nslice(w2)

    def mblk(w):
        T = np.asarray(w, f).T
        return np.ascontiguousarray(
            T.reshape(16, 128, 16, 128).transpose(2, 1, 0, 3)).reshape(16, 128, 16 * 128)

    w1m = mblk(w1)
    w3m = mblk(w3)
    wrh = np.ascontiguousarray(
        np.asarray(wr, f).T.reshape(16, 128, 8).transpose(1, 0, 2)).reshape(128, 128)
    pb = np.ascontiguousarray(np.asarray(proj_b, f).reshape(1, D))
    return wqk, wv, wp, w1m, w3m, w2n, wrh, pb


def kernel(L0_attn_x, L0_attn_qkv_w, L0_attn_proj_w, L0_attn_proj_b,
           L0_ffn_w1, L0_ffn_w3, L0_ffn_w2, L0_ffn_wr):
    x = np.ascontiguousarray(np.asarray(L0_attn_x, np.float32))
    wqk, wv, wp, w1m, w3m, w2n, wrh, pb = _prep_weights(
        L0_attn_qkv_w, L0_attn_proj_w, L0_attn_proj_b, L0_ffn_w1, L0_ffn_w3,
        L0_ffn_w2, L0_ffn_wr)
    mm2, b01, mm_s0, am_s0, mm_last, am_last = _host_consts()
    ident = np.eye(128, dtype=np.float32)
    nc = build_nc()
    in_maps = []
    for c in range(8):
        cm = np.full((8, 1), 1.0 if c == 0 else 0.0, np.float32)
        in_maps.append({
            "x": np.ascontiguousarray(x[c]),
            "wqk": wqk, "wv": wv, "wp": wp, "w1": w1m, "w3": w3m, "w2": w2n,
            "wr": wrh, "pb": pb,
            "mm2": mm2, "b01": b01, "mm_s0": mm_s0,
            "am_s0": am_s0, "mm_last": mm_last, "am_last": am_last,
            "ident": ident, "coremask": cm,
        })
    res = run_bass_kernel_spmd(nc, in_maps, core_ids=list(range(8)))
    return np.stack([np.asarray(res.results[c]["y"]) for c in range(8)], axis=0)
